# version 06
# baseline (speedup 1.0000x reference)
import contextlib
import numpy as np
import ml_dtypes
import concourse.bass as bass
import concourse.mybir as mybir
from concourse.bass_utils import run_bass_kernel_spmd

F32 = mybir.dt.float32
BF16 = mybir.dt.bfloat16
AF = mybir.ActivationFunctionType
ALU = mybir.AluOpType
AX = mybir.AxisListType

NCORE = 8
B, L, D = 2, 8192, 1024
E, S, CONV = 2048, 16, 4
I = 2816
T = 2048
TH = T + 3
C = 128
NCH = T // C
DT = D // 128
ET = E // 128
TT = T // 128
IT = I // 128
MAGIC = 12582912.0
IQ = I // 8
EPS = 1e-5
H1S = 2052


def _stage(inp):
    x = np.asarray(inp["L0_attn_x"])
    w_in = np.asarray(inp["L0_attn_in_proj_w"])
    conv_w = np.asarray(inp["L0_attn_conv_w"]).reshape(E, CONV)
    conv_b = np.asarray(inp["L0_attn_conv_b"]).reshape(1, E)
    xpw = np.asarray(inp["L0_attn_x_proj_w"])
    alog = np.asarray(inp["L0_attn_A_log"])
    dpar = np.asarray(inp["L0_attn_D_param"]).reshape(1, E)
    w_out = np.asarray(inp["L0_attn_out_proj_w"])
    clip_x = np.asarray(inp["L0_ffn_clip_x"]).reshape(1, 1)
    clip_dn = np.asarray(inp["L0_ffn_clip_down"]).reshape(1, 1)
    w_g = np.asarray(inp["L0_ffn_w_gate"])
    w_u = np.asarray(inp["L0_ffn_w_up"])
    w_d = np.asarray(inp["L0_ffn_w_down"])
    cc = np.asarray(inp["L0_c"])
    wmod = np.asarray(inp["L0_wmod"])

    co = np.ascontiguousarray
    wxc_tiled = co(w_in[:E].T.reshape(8, 128, 16, 128).transpose(2, 0, 1, 3))
    wz_t = co(w_in[E:].T.reshape(8, 128, E))
    xpw_t = co(xpw.T.reshape(16, 128, 33))
    wout_t = co(w_out.T.reshape(16, 128, D))
    wgT = w_g.T
    wuT = w_u.T
    wdT = w_d.T

    maps = []
    for k in range(NCORE):
        b, q = divmod(k, 4)
        lo = q * T
        xh = x[b, 0:3] if q == 0 else x[b, lo - 3:lo]
        x_tm = co(np.concatenate([xh, x[b, lo:lo + T]], axis=0))
        pm = np.zeros((1, 8), np.float32)
        om = np.zeros((8, 8), np.float32)
        for j in range(4 * b, k):
            pm[0, j] = 1.0
            for u in range(j + 1, k):
                om[u, j] = 1.0
        maps.append(dict(
            x_tm=x_tm.astype(np.float32),
            wxc_t=wxc_tiled.astype(np.float32),
            wz_t=wz_t.astype(np.float32),
            xpw_t=xpw_t.astype(np.float32),
            wout_t=wout_t.astype(np.float32),
            conv_w=co(conv_w).astype(np.float32),
            conv_b=co(conv_b).astype(np.float32),
            dparam=co(dpar).astype(np.float32),
            alog_row=co(alog[0:1, :]).astype(np.float32),
            clip_x=clip_x.astype(np.float32),
            clip_dn=clip_dn.astype(np.float32),
            wmod_sh=co(wmod[:, k * 768:(k + 1) * 768]).astype(np.float32),
            cT=co(cc.T).astype(np.float32),
            wg_nat=co(w_g[k * IQ:(k + 1) * IQ]).astype(np.float32),
            wu_nat=co(w_u[k * IQ:(k + 1) * IQ]).astype(np.float32),
            wd_nat=co(w_d[k * 128:(k + 1) * 128]).astype(np.float32),
            wgT_sh=co(wgT[k * 128:(k + 1) * 128]).astype(np.float32),
            wuT_sh=co(wuT[k * 128:(k + 1) * 128]).astype(np.float32),
            wdT_sh=co(wdT[:, k * 128:(k + 1) * 128]).astype(np.float32),
            halo=np.array([[0.0 if q == 0 else 1.0]], np.float32),
            bsel=np.array([[float(b)]], np.float32),
            pmask=pm,
            ordmask_t=om,
        ))
    return maps


class Build:
    def __init__(self, dbg=(), upto="all", dbg_scr=False):
        self.upto = upto
        self.dbg_scr = dbg_scr
        self.nc = nc = bass.Bass("TRN2", target_bir_lowering=False)
        self.events = []
        self.count = {}
        self.dbg = dict(dbg)
        self.stack = contextlib.ExitStack()

        g = self.g = {}
        def din(name, shape):
            g[name] = nc.dram_tensor(name, list(shape), F32, kind="ExternalInput").ap()
        for name, shape in [
            ("x_tm", (TH, D)), ("wxc_t", (16, 8, 128, 128)), ("wz_t", (8, 128, E)),
            ("xpw_t", (16, 128, 33)), ("wout_t", (16, 128, D)), ("conv_w", (E, CONV)),
            ("conv_b", (1, E)), ("dparam", (1, E)), ("alog_row", (1, S)),
            ("clip_x", (1, 1)), ("clip_dn", (1, 1)), ("wmod_sh", (D, 768)),
            ("cT", (D, 2)), ("wg_nat", (IQ, D)), ("wu_nat", (IQ, D)),
            ("wd_nat", (128, I)), ("wgT_sh", (128, I)), ("wuT_sh", (128, I)),
            ("wdT_sh", (I, 128)), ("halo", (1, 1)), ("bsel", (1, 1)),
            ("pmask", (1, 8)), ("ordmask_t", (8, 8)),
        ]:
            din(name, shape)
        self.out_x3 = nc.dram_tensor("out_x3", [T, D], F32, kind="ExternalOutput").ap()
        self.dbgo = {}
        for name, shape in self.dbg.items():
            dt = BF16 if name.endswith("_bf") else F32
            self.dbgo[name] = nc.dram_tensor("dbg_" + name, list(shape), dt,
                                             kind="ExternalOutput").ap()

        def scr(name, shape, dt=F32):
            kw = ({"kind": "ExternalOutput"}
                  if self.dbg_scr and not name.startswith("ag") else {})
            g[name] = nc.dram_tensor(name, list(shape), dt, **kw).ap()
        scr("x2_hbm", (T, D))
        scr("sz_hbm", (NCH, 128, 2048), BF16)
        scr("h2_hbm", (TT, 8, 128, 128))
        scr("xq_hbm", (8, 128, T))
        scr("rows_hbm", (24, 4096))
        scr("krow_hbm", (NCH, 2, 2048))
        scr("ag0_in", (1, 2368))
        scr("ag0_out", (NCORE, 2368))
        NQ = 128 * I * 2 + I * 128
        scr("agq_in", (1, NQ), BF16)
        scr("agq_out", (NCORE, NQ), BF16)
        scr("ag1_in", (1, S * E + 16))
        scr("ag1_out", (NCORE, S * E + 16))
        scr("ag2_in", (1, 1))
        scr("ag2_out", (1, 1))
        scr("ag3_in", (1, 1))
        scr("ag3_out", (1, 1))

        g["ident"] = nc.inline_tensor(np.eye(128, dtype=ml_dtypes.bfloat16),
                                      name="identbf").ap()
        g["identf"] = nc.inline_tensor(np.eye(128, dtype=np.float32),
                                       name="identf32").ap()
        g["tril"] = nc.inline_tensor(
            np.ascontiguousarray(np.triu(np.ones((128, 128), np.float32))),
            name="trilf").ap()

        self.semnames = ["dls", "dlg", "dlc", "cc", "pe", "ve", "se", "bar",
                         "pe2", "ve2", "se2"]
        for s in self.semnames:
            self.count[s] = 0
        self.sems = {}

        self.NW = 52736
        self.sb = self.stack.enter_context(nc.sbuf_tensor("arena", [128, self.NW], F32))
        self.sid = self.stack.enter_context(nc.sbuf_tensor("sident", [128, 128], BF16))
        self.sidf = self.stack.enter_context(nc.sbuf_tensor("sidentf", [128, 128], F32))
        self.stril = self.stack.enter_context(nc.sbuf_tensor("stril", [128, 128], F32))
        self.ps = [self.stack.enter_context(nc.psum_tensor(f"ps{i}", [128, 512], F32))
                   for i in range(8)]

        self.off = {}
        self._size = {}
        self.cur = 0
        self._layout()

    def _alloc(self, name, nwords):
        self._size[name] = nwords
        self.off[name] = self.cur
        self.cur += (nwords + 1) & ~1
        assert self.cur <= self.NW, f"arena overflow at {name}: {self.cur}"

    def _alias(self, name, base, word_off, nwords):
        self._size[name] = nwords
        assert word_off + nwords <= self._size[base], (name, base)
        self.off[name] = self.off[base] + word_off

    def _layout(self):
        A, AL = self._alloc, self._alias
        A("mod_cols", 96)
        A("modx", 32)
        A("arow", 16); A("acol", 2); A("arep", 16)
        A("scl_g", 22); A("scl_u", 22); A("coef_dn", 8)
        A("sc", 64)
        A("convw", 80)
        A("tmp0", 1032); A("tmp1", 1032); A("cmb", 128)

        A("R1", 7168)
        AL("modw", "R1", 0, 6144)
        AL("wqstage", "R1", 3074, 1408)
        AL("wqrep", "R1", 4484, 1408)
        AL("wqout", "R1", 5894, 704)
        AL("amaxst", "R1", 3074, 2816)
        AL("xc0", "R1", 0, 2052)
        AL("xcin", "R1", 2054, 2052)
        AL("xcsl", "R1", 4108, 1028)
        AL("wrot", "R1", 5138, 1088)
        AL("repA", "R1", 0, 2048)
        AL("repC", "R1", 2048, 1024)
        AL("e3t", "R1", 3072, 2048)
        AL("e3u", "R1", 5120, 2048)

        A("R2", 4192)
        AL("xp", "R2", 0, 2048)
        AL("r2row", "R2", 0, 2816)
        AL("comb", "R2", 2050, 2080)
        AL("convw2", "R2", 2050, 80)

        A("R3", 9600)
        AL("xrot", "R3", 0, 2064)
        AL("xhalo", "R3", 2066, 1024)
        AL("sums", "R3", 3092, 40)
        AL("ab1", "R3", 3134, 40)
        AL("xaff", "R3", 3176, 1028)
        AL("xpTc", "R3", 0, 548)
        AL("hsn0", "R3", 550, 1024)
        AL("hsn1", "R3", 1576, 1024)
        AL("kbslot", "R3", 2602, 128)
        AL("gslot", "R3", 2732, 260)
        AL("rch", "R3", 2994, 128)
        AL("rows16", "R3", 3124, 96)
        AL("ubufs", "R3", 3222, 128)
        AL("hstate", "R3", 3352, 2048)
        AL("y2r", "R3", 5402, 1024)
        AL("xcin2", "R3", 5402, 2052)
        AL("xrl", "R3", 6428, 1024)
        AL("y2f", "R3", 7454, 1024)
        AL("hinbf", "R3", 8480, 1024)
        AL("ab2", "R3", 9506, 40)
        AL("x2rot", "R3", 0, 2064)
        AL("x2aff", "R3", 3176, 1028)

        A("R4", 8212)
        AL("h1T", "R4", 0, 8212)
        AL("wot", "R4", 0, 8192)
        AL("qx", "R4", 0, 8192)

        A("R6", 4104)
        AL("wzrot", "R6", 0, 4100)
        AL("szrot", "R6", 0, 2052)
        AL("garepf", "R6", 2054, 1024)
        AL("garep", "R6", 3080, 512)
        AL("drep", "R6", 2054, 1024)
        AL("qwrot", "R6", 0, 4100)
        AL("dnbuf", "R6", 516, 2048)
        AL("xpwb", "R6", 2054, 288)

        A("xcsTM", 16384)
        AL("h2TM", "xcsTM", 0, 16384)
        AL("cbrep", "R6", 0, 1024)
        AL("sbrep", "R6", 1026, 1024)
        AL("coefrep", "R6", 3072, 1024)

    def fv(self, name, p0=0, pn=128, o=0, n=None):
        off, sz = self.off[name], self._size[name]
        n = (sz - o) if n is None else n
        assert o + n <= sz, ("fv overflow", name, o, n, sz)
        return self.sb[p0:p0 + pn, off + o:off + o + n]

    def bvv(self, name, p0=0, pn=128, o=0, n=None):
        off, sz = self.off[name], self._size[name]
        nb = 2 * sz
        n = (nb - o) if n is None else n
        base = self.sb[p0:p0 + pn, off:off + sz].bitcast(BF16)
        return base[:, o:o + n]

    def now(self, s):
        return self.count[s]

    def emit(self, eng, fn, waits=(), incs=()):
        for s, n in incs:
            self.count[s] += n
        self.events.append((eng, fn, tuple(waits), tuple(incs)))

    def dma(self, eng, out, in_, waits=(), slow=False, slot="misc"):
        sem = "D_" + slot
        if sem not in self.count:
            self.count[sem] = 0
            self.semnames.append(sem)
        def fn(e, o=out, i=in_, sl=slow):
            return e.dma_start(out=o, in_=i, allow_slow_non_contiguous=sl)
        self.emit(eng, fn, waits, [(sem, 16)])
        return (sem, self.now(sem))

    def cdma(self, eng, out, in_, waits=(), slow=False, slot="chain"):
        if not hasattr(self, "_chain"):
            self._chain = {}
        prev = self._chain.get(slot)
        w = list(waits) + ([prev] if prev else [])
        t = self.dma(eng, out, in_, w, slow, slot)
        self._chain[slot] = t
        return t

    def op(self, eng, fn, waits=(), sem=None, drain=True):
        incs = [(sem, 1)] if sem else []
        self.emit(eng, fn, waits, incs)
        if drain and eng in ("vector", "scalar"):
            self.emit(eng, lambda e: e.drain(), (), ())
        return self.now(sem) if sem else None

    def barrier(self, engines=("tensor", "vector", "scalar", "gpsimd", "sync")):
        for e in engines:
            self.emit(e, lambda eng: eng.nop(), (), [("bar", 1)])
        tgt = self.now("bar")
        for e in engines:
            self.emit(e, lambda eng: eng.nop(), [("bar", tgt)], ())
        return tgt

    def materialize(self):
        nc = self.nc
        for s in self.semnames:
            self.sems[s] = self.stack.enter_context(nc.semaphore(s))
        sems = self.sems
        with nc.Block() as block:
            def run(engname, eng):
                for e, fn, waits, incs in self.events:
                    if e != engname:
                        continue
                    for s, v in waits:
                        eng.wait_ge(sems[s], v)
                    ins = fn(eng)
                    for s, n in incs:
                        ins = ins.then_inc(sems[s], n)

            @block.tensor
            def _(eng):
                run("tensor", eng)

            @block.vector
            def _(eng):
                run("vector", eng)

            @block.scalar
            def _(eng):
                run("scalar", eng)

            @block.gpsimd
            def _(eng):
                run("gpsimd", eng)

            @block.sync
            def _(eng):
                run("sync", eng)

    def p0_consts(self):
        g = self.g
        L = self.ld = {}
        L["sid"] = self.cdma("sync", self.sid[:, :], g["ident"][:, :], slot="cs")
        L["tril"] = self.cdma("sync", self.stril[:, :], g["tril"][:, :], slot="cs")
        L["sidf"] = self.cdma("sync", self.sidf[:, :], g["identf"][:, :], slot="cs")
        arow = self.fv("arow", 0, 1, 0, 16)
        d0 = self.cdma("sync", arow, g["alog_row"][:, :], slot="cs")
        s0 = self.op("scalar", lambda e: e.activation(arow, arow, AF.Exp,
                                                      bias=0.0, scale=1.0),
                     waits=[d0], sem="se")
        v0 = self.op("vector", lambda e: e.tensor_scalar_mul(arow, arow, -1.0),
                     waits=[("se", s0)], sem="ve")
        d1 = self.cdma("sync", g["rows_hbm"][0:1, 0:16], arow, waits=[("ve", v0)],
                      slot="cs")
        L["acol"] = self.cdma("sync", self.fv("acol", 0, 16, 0, 1),
                             g["rows_hbm"][0, 0:16].rearrange("(a b) -> a b", b=1),
                             waits=[d1], slow=True, slot="cs")
        L["acol2"] = self.cdma("sync", self.fv("acol", 16, 16, 0, 1),
                              g["rows_hbm"][0, 0:16].rearrange("(a b) -> a b", b=1),
                              waits=[d1], slow=True, slot="cs")
        L["arep"] = self.cdma("sync", self.fv("arep"),
                             g["rows_hbm"][0:1, 0:16].partition_broadcast(128),
                             waits=[d1], slot="cs")
        L["convw"] = self.cdma("sync", self.fv("convw", 0, 128, 0, 64)
                              .rearrange("p (et c) -> p et c", c=4),
                              g["conv_w"].rearrange("(et p) c -> p et c", p=128),
                              slot="cs")
        L["convb"] = self.cdma("sync", self.fv("convw", 0, 128, 64, 16),
                              g["conv_b"].rearrange("a (et p) -> p (a et)", p=128),
                              slow=True, slot="cs")
        L["xpwb"] = self.dma("gpsimd", self.bvv("xpwb", 0, 128, 0, 16 * 34)
                             .rearrange("p (es m) -> p es m", m=34)[:, :, 0:33],
                             g["xpw_t"].transpose([1, 0, 2]), slot="xpwb")
        L["clips"] = self.cdma("sync", self.fv("sc", 0, 1, 0, 1), g["clip_x"][:, :],
                              slot="cs")
        self.cdma("sync", self.fv("sc", 0, 1, 2, 1), g["clip_dn"][:, :], slot="cs")
        L["halo"] = self.cdma("sync", self.fv("sc", 0, 128, 4, 1),
                             g["halo"][0:1, :].partition_broadcast(128), slot="cs")
        L["bsel"] = self.cdma("sync", self.fv("sc", 0, 128, 6, 1),
                             g["bsel"][0:1, :].partition_broadcast(128), slot="cs")

    def p2a_mod_send(self):
        g = self.g
        ct = self.fv("sc", 0, 128, 8, 16)
        dl = self.cdma("sync", ct.rearrange("p (dt b) -> p dt b", b=2),
                      g["cT"].rearrange("(dt p) b -> p dt b", p=128), slot="cs")
        se0 = self.op("scalar", lambda e: e.activation(ct, ct, AF.Silu,
                                                       bias=0.0, scale=1.0),
                      waits=[dl], sem="se")
        dmw = self.dma("sync", self.fv("modw", 0, 128, 0, 8 * 768)
                       .rearrange("p (dt j) -> p dt j", j=768),
                       g["wmod_sh"].rearrange("(dt p) j -> p dt j", p=128),
                       slot="modw")
        modp = self.fv("sc", 0, 128, 40, 12)
        modw = self.fv("modw", 0, 128, 0, 8 * 768)
        for ccj in range(6):
            for dt in range(8):
                w = []
                if dt == 0:
                    w = [dmw, ("se", se0), ("ve", self.now("ve"))]
                self.op("tensor", lambda e, ccj=ccj, dt=dt: e.matmul(
                    self.ps[0][:, 0:2],
                    modw[:, dt * 768 + ccj * 128: dt * 768 + (ccj + 1) * 128],
                    ct[:, dt * 2: dt * 2 + 2],
                    start=(dt == 0), stop=(dt == 7)),
                    waits=w, sem="pe" if dt == 7 else None)
            pe = self.now("pe")
            self.op("vector", lambda e, ccj=ccj: e.tensor_copy(
                modp.rearrange("p (b c) -> p b c", c=6)[:, :, ccj],
                self.ps[0][:, 0:2]),
                waits=[("pe", pe)], sem="ve")
        vmod = self.now("ve")
        self.pe_mod = self.now("pe")
        dsent = [self.dma("sync", g["ag0_in"][0, 0:1536]
                          .rearrange("(p bc) -> p bc", bc=12), modp,
                          waits=[("ve", vmod)], slot="ag0m")]
        vprev = None
        prevstore = None
        first_amax = True
        for name, nrows, ncols, aoff in (("wg_nat", IQ, D, 1536),
                                         ("wu_nat", IQ, D, 1888),
                                         ("wd_nat", 128, I, 2240)):
            r0 = 0
            while r0 < nrows:
                rn = min(128, nrows - r0)
                stage = self.fv("amaxst", 0, rn, 0, ncols)
                d = self.dma("sync", stage, g[name][r0:r0 + rn, :],
                             waits=([("ve", vprev)] if vprev else
                                    [("pe", self.pe_mod)] if first_amax else []),
                             slot="amaxst")
                first_amax = False
                col = self.fv("sc", 0, rn, 24, 1)
                vprev = self.op("vector", lambda e, st=stage, c=col, rn=rn:
                                e.reduce_max(c, st, AX.X, apply_absolute_value=True),
                                waits=[d] + ([prevstore] if prevstore else []),
                                sem="ve")
                prevstore = self.dma("sync",
                                     g["ag0_in"][0, aoff + r0: aoff + r0 + rn]
                                     .rearrange("(p a) -> p a", a=1), col,
                                     waits=[("ve", vprev)], slot="ag0c")
                dsent.append(prevstore)
                r0 += rn
        agw = [("D_ag0m", self.now("D_ag0m")), ("D_ag0c", self.now("D_ag0c"))]
        self.op("gpsimd", lambda e: e.collective_compute(
            "AllGather", ALU.bypass, replica_groups=[list(range(NCORE))],
            ins=[self.g["ag0_in"][:, :]], outs=[self.g["ag0_out"][:, :]]),
            waits=agw, sem="cc")
        self.cc_ag0 = self.now("cc")

    def p2b_mod_recv(self):
        g = self.g
        d = self.dma("sync",
                     self.fv("mod_cols").rearrange("p (b co cc) -> p b co cc",
                                                   b=2, co=8),
                     g["ag0_out"][:, 0:1536]
                     .rearrange("co (p b cc) -> p b co cc", b=2, cc=6),
                     waits=[("cc", self.cc_ag0)], slow=True, slot="modc")
        mc = self.fv("mod_cols")
        bs = self.fv("sc", 0, 128, 6, 1)
        self.op("vector", lambda e: e.tensor_sub(self.fv("tmp0", 0, 128, 0, 48),
                                                 mc[:, 48:96], mc[:, 0:48]),
                waits=[d, self.ld["bsel"]], sem="ve")
        self.op("vector", lambda e: e.scalar_tensor_tensor(
            mc[:, 0:48], self.fv("tmp0", 0, 128, 0, 48), bs, mc[:, 0:48],
            ALU.mult, ALU.add), sem="ve")
        mx = self.fv("modx")
        self.op("vector", lambda e: e.tensor_scalar_add(mx[:, 0:8], mc[:, 8:16], 1.0),
                sem="ve")
        self.op("vector", lambda e: e.tensor_scalar_add(mx[:, 8:16], mc[:, 32:40], 1.0),
                sem="ve")
        for src, dsto in ((16, 16), (40, 24)):
            t0 = self.fv("tmp0", 0, 128, 0, 8)
            t1 = self.fv("tmp0", 0, 128, 8, 8)
            t2 = self.fv("tmp0", 0, 128, 16, 8)
            self.op("vector", lambda e, s=src, a=t0: e.tensor_single_scalar(
                a, mc[:, s:s + 8], 0.0, ALU.is_ge), sem="ve")
            self.op("vector", lambda e, s=src, a=t1: e.tensor_scalar_max(
                a, mc[:, s:s + 8], 0.25), sem="ve")
            self.op("vector", lambda e, s=src, a=t2: e.tensor_scalar_min(
                a, mc[:, s:s + 8], -0.25), sem="ve")
            self.op("vector", lambda e, a=t1, bq=t2: e.tensor_sub(a, a, bq), sem="ve")
            self.op("vector", lambda e, a=t0, bq=t1: e.tensor_mul(a, a, bq), sem="ve")
            self.op("vector", lambda e, a=t0, bq=t2, do=dsto: e.tensor_add(
                mx[:, do:do + 8], a, bq), sem="ve")
        self.ve_mod = self.now("ve")

    def p2c_quant_weights(self):
        g = self.g
        dr = []
        dr.append(self.cdma("sync", g["rows_hbm"][4, 0:I]
                           .rearrange("(c j) -> c j", j=IQ),
                           g["ag0_out"][:, 1536:1536 + IQ],
                           waits=[("cc", self.cc_ag0)], slot="c4"))
        dr.append(self.cdma("sync", g["rows_hbm"][5, 0:I]
                           .rearrange("(c j) -> c j", j=IQ),
                           g["ag0_out"][:, 1888:1888 + IQ],
                           waits=[("cc", self.cc_ag0)], slot="c4"))
        dr.append(self.cdma("sync", g["rows_hbm"][6, 0:1024]
                           .rearrange("(c j) -> c j", j=128),
                           g["ag0_out"][:, 2240:2240 + 128],
                           waits=[("cc", self.cc_ag0)], slot="c4"))
        vprev = None
        prevrec = None
        rec_t = []
        for i, (src, dst, n) in enumerate(((4, 7, I), (5, 8, I), (6, 9, 1024))):
            row = self.fv("r2row", 0, 1, 0, n)
            d = self.dma("sync", row, g["rows_hbm"][src:src + 1, 0:n],
                         waits=[dr[i]] + ([("ve", vprev)] if vprev else []) +
                               ([prevrec] if prevrec else []),
                         slot="xprow")
            self.op("vector", lambda e, r=row: e.reciprocal(r, r),
                    waits=[d], sem="ve")
            vprev = self.op("vector", lambda e, r=row: e.tensor_scalar_mul(
                r, r, 127.0), sem="ve")
            prevrec = self.cdma("sync", g["rows_hbm"][dst:dst + 1, 0:n], row,
                               waits=[("ve", vprev)], slot="c4")
            rec_t.append(prevrec)
        HF = I // 2
        vq = None
        dq_out = []
        for wname, rrow, qoff in (("wgT_sh", 7, 0), ("wuT_sh", 8, 128 * I)):
            for hv in range(2):
                st = self.fv("wqstage", 0, 128, 0, HF)
                rp = self.fv("wqrep", 0, 128, 0, HF)
                qo = self.bvv("wqout", 0, 128, 0, HF)
                w = [("ve", vq)] if vq else []
                d1 = self.dma("sync", st, g[wname][:, hv * HF:(hv + 1) * HF],
                              waits=w, slot="wqstage")
                d2 = self.dma("sync", rp,
                              g["rows_hbm"][rrow:rrow + 1, hv * HF:(hv + 1) * HF]
                              .partition_broadcast(128),
                              waits=w + [rec_t[0 if rrow == 7 else 1]],
                              slot="wqrep")
                self.op("vector", lambda e, a=st, bq=rp: e.tensor_mul(a, a, bq),
                        waits=[d1, d2, ("D_agq", self.now("D_agq"))
                               if "D_agq" in self.count else d1], sem="ve")
                vq = self.op("vector", lambda e, a=st, q=qo: e.tensor_scalar(
                    q, a, MAGIC, MAGIC, ALU.add, ALU.subtract), sem="ve")
                dq_out.append(self.dma("sync",
                              g["agq_in"][0, qoff + hv * 11 * 16384:
                                          qoff + (hv + 1) * 11 * 16384]
                              .rearrange("(it p i) -> p it i", it=11, i=128),
                              qo.rearrange("p (it i) -> p it i", i=128),
                              waits=[("ve", vq)], slot="agq"))
        row = self.fv("xp", 0, 1, 0, 128)
        dd = self.dma("sync", row, g["ag0_in"][0:1, 2240:2368],
                      waits=[("ve", vq), prevrec], slot="xprow")
        self.op("vector", lambda e, r=row: e.reciprocal(r, r), waits=[dd], sem="ve")
        v = self.op("vector", lambda e, r=row: e.tensor_scalar_mul(r, r, 127.0),
                    sem="ve")
        d9 = self.cdma("sync", g["rows_hbm"][10:11, 0:128], row,
                      waits=[("ve", v)], slot="c4")
        rp = self.fv("wqrep", 0, 128, 0, 128)
        d2 = self.dma("sync", rp, g["rows_hbm"][10:11, 0:128].partition_broadcast(128),
                      waits=[d9, ("ve", vq)], slot="wqrep")
        for hv in range(2):
            st = self.fv("wqstage", 0, 128, 0, HF)
            d1 = self.dma("sync", st.rearrange("p (it h) -> p it h", h=128),
                          g["wdT_sh"][hv * HF:(hv + 1) * HF, :]
                          .rearrange("(it p) h -> p it h", p=128),
                          waits=[("ve", vq)], slot="wqstage")
            qo = self.bvv("wqout", 0, 128, 0, HF)
            self.op("vector", lambda e, a=st, bq=rp: e.tensor_tensor(
                a.rearrange("p (it h) -> p it h", h=128),
                a.rearrange("p (it h) -> p it h", h=128),
                bq.unsqueeze(1).broadcast_to([128, 11, 128]), ALU.mult),
                waits=[d1, d2, ("D_agq", self.now("D_agq"))], sem="ve")
            vq = self.op("vector", lambda e, a=st, q=qo: e.tensor_scalar(
                q, a, MAGIC, MAGIC, ALU.add, ALU.subtract), sem="ve")
            dq_out.append(self.dma("sync",
                          g["agq_in"][0, 2 * 128 * I + hv * 11 * 16384:
                                      2 * 128 * I + (hv + 1) * 11 * 16384]
                          .rearrange("(it p h) -> p it h", it=11, h=128),
                          qo.rearrange("p (it h) -> p it h", h=128),
                          waits=[("ve", vq)], slot="agq"))
        self.op("gpsimd", lambda e: e.collective_compute(
            "AllGather", ALU.bypass, replica_groups=[list(range(NCORE))],
            ins=[g["agq_in"][:, :]], outs=[g["agq_out"][:, :]]),
            waits=[("D_agq", self.now("D_agq"))], sem="cc")
        self.cc_agq = self.now("cc")

    def p1_ln1(self):
        g = self.g
        sums = self.fv("sums", 0, 128, 0, 17)
        sums2 = self.fv("sums", 0, 128, 18, 17)
        for i in range(TT):
            xt = self.fv("xrot", 0, 128, (i % 2) * 1032, 1024)
            w = []
            if i >= 2:
                w = [("ve", self.ve_x_used[i - 2]), ("se", self.se_x_used[i - 2])]
            d = self.dma("sync", xt, g["x_tm"][3 + i * 128: 3 + (i + 1) * 128, :],
                         waits=w, slot="xr%d" % (i % 2))
            buf = self.fv("tmp1", 0, 128, 0, 1024)
            if i == 0:
                self.se_x_used, self.ve_x_used, self.ve_sq = {}, {}, {}
            sqw = [d] + ([("ve", self.ve_sq[i - 1])] if i >= 1 else [])
            s = self.op("scalar", lambda e, x=xt, q=buf: e.activation(
                q, x, AF.Square, bias=0.0, scale=1.0), waits=sqw, sem="se")
            self.se_x_used[i] = s
            self.op("vector", lambda e, x=xt, i=i: e.reduce_sum(
                sums[:, i:i + 1], x, AX.X), waits=[d], sem="ve")
            self.ve_x_used[i] = self.now("ve")
            self.op("vector", lambda e, q=buf, i=i: e.reduce_sum(
                sums2[:, i:i + 1], q, AX.X), waits=[("se", s)], sem="ve")
            self.ve_sq[i] = self.now("ve")
        vz = self.op("vector", lambda e: e.memset(sums[:, 16:17], 0.0), sem="ve")
        vz = self.op("vector", lambda e: e.memset(sums2[:, 16:17], 0.0), sem="ve")
        xh = self.fv("xhalo", 0, 3, 0, 1024)
        dh = self.dma("sync", xh, g["x_tm"][0:3, :], slot="xhalo")
        sqh = self.fv("tmp1", 0, 3, 0, 1024)
        s = self.op("scalar", lambda e: e.activation(sqh, xh, AF.Square,
                                                     bias=0.0, scale=1.0),
                    waits=[dh, ("ve", self.now("ve"))], sem="se")
        self.op("vector", lambda e: e.reduce_sum(sums[0:3, 16:17], xh, AX.X),
                waits=[dh], sem="ve")
        self.op("vector", lambda e: e.reduce_sum(sums2[0:3, 16:17], sqh, AX.X),
                waits=[("se", s)], sem="ve")
        vstats = self.now("ve")
        m = self.fv("cmb", 0, 128, 0, 17)
        var = self.fv("cmb", 0, 128, 18, 17)
        mm = self.fv("cmb", 0, 128, 36, 17)
        var2 = self.fv("cmb", 0, 128, 54, 17)
        sd = self.fv("cmb", 0, 128, 72, 17)
        a1 = self.fv("ab1", 0, 128, 0, 17)
        b1 = self.fv("ab1", 0, 128, 18, 17)
        self.op("vector", lambda e: e.tensor_scalar_mul(m, sums, 1.0 / D),
                waits=[("ve", vstats)], sem="ve")
        self.op("vector", lambda e: e.tensor_scalar(var, sums2, 1.0 / D, EPS,
                                                    ALU.mult, ALU.add), sem="ve")
        self.op("vector", lambda e: e.tensor_mul(mm, m, m), sem="ve")
        self.op("vector", lambda e: e.tensor_sub(var2, var, mm), sem="ve")
        sv = self.op("scalar", lambda e: e.activation(sd, var2, AF.Sqrt,
                                                      bias=0.0, scale=1.0),
                     waits=[("ve", self.now("ve"))], sem="se")
        self.op("vector", lambda e: e.reciprocal(a1, sd),
                waits=[("se", sv)], sem="ve")
        self.op("vector", lambda e: e.tensor_mul(b1, m, a1), sem="ve")
        self.op("vector", lambda e: e.tensor_scalar_mul(b1, b1, -1.0), sem="ve")
        self.ve_ab1 = self.now("ve")

    def p1_affine(self):
        g = self.g
        a1 = self.fv("ab1", 0, 128, 0, 17)
        b1 = self.fv("ab1", 0, 128, 18, 17)
        ps7b = self.ps[7][:, :].bitcast(BF16)
        xaffb = self.bvv("xaff", 0, 128, 0, 2048)
        h1 = self.bvv("h1T", 0, 128, 0, 8 * H1S)
        dre = {}
        self.se_aff, self.pe_tp = {}, {}
        for i in range(TT + 1):
            if i < TT:
                xt = self.fv("xrot", 0, 128, (i % 2) * 1032, 1024)
                w = ([("ve", self.ve_ab1), ("se", self.now("se"))] if i < 2
                     else [("se", self.se_aff[i - 2])])
                dre[i] = self.dma("sync", xt,
                                  g["x_tm"][3 + i * 128: 3 + (i + 1) * 128, :],
                                  waits=w, slot="xr%d" % (i % 2))
                pn, colw = 128, 128
            else:
                xt = self.fv("xhalo", 0, 3, 0, 1024)
                pn, colw = 3, 3
            xafft = xaffb[0:pn, (i % 2) * 1024:(i % 2) * 1024 + 1024]
            sa_ = self.op("scalar", lambda e, x=xt, i=i, xa=xafft, pn=pn: e.activation(
                xa, x, AF.Identity,
                bias=b1[0:pn, i:i + 1], scale=a1[0:pn, i:i + 1]),
                waits=[("ve", self.ve_ab1)] + ([dre[i]] if i < TT else []) +
                      ([("pe", self.pe_tp[i - 2])] if i >= 2 else []),
                sem="se")
            self.se_aff[i] = sa_
            for dt in range(8):
                self.op("tensor", lambda e, dt=dt, pn=pn, colw=colw, xa=xafft: e.transpose(
                    ps7b[:, dt * 128: dt * 128 + colw],
                    xa[0:pn, dt * 128: dt * 128 + 128],
                    self.sid[0:pn, 0:pn]),
                    waits=(([("se", sa_), ("se2", self.now("se2"))] +
                            ([self.ld["sid"]] if i == 0 else [])) if dt == 0 else []),
                    sem="pe" if dt == 7 else None)
            pe = self.now("pe")
            self.pe_tp[i] = pe
            for dt in range(8):
                col0 = (3 + i * 128) if i < TT else 0
                self.op("scalar", lambda e, dt=dt, col0=col0, colw=colw: e.activation(
                    h1[:, dt * H1S + col0: dt * H1S + col0 + colw],
                    ps7b[:, dt * 128: dt * 128 + colw], AF.Identity,
                    bias=self.fv("mod_cols", 0, 128, 0 + dt, 1),
                    scale=self.fv("modx", 0, 128, 0 + dt, 1)),
                    waits=([("pe", pe), ("ve", self.ve_mod)] if dt == 0 else []),
                    sem="se2")
        hl = self.fv("sc", 0, 128, 4, 1)
        vh = self.op("vector", lambda e: e.tensor_scalar(
            h1.rearrange("p (dt t) -> p dt t", t=H1S)[:, :, 0:3],
            h1.rearrange("p (dt t) -> p dt t", t=H1S)[:, :, 0:3],
            hl, None, ALU.mult),
            waits=[("se2", self.now("se2")), self.ld["halo"]], sem="ve")
        self.ve_h1T = self.now("ve")
        self.se2_h1T = self.now("se2")
        if "h1T_bf" in self.dbgo:
            self.dma("sync", self.dbgo["h1T_bf"][:, :]
                     .rearrange("(dt p) t -> p dt t", p=128),
                     h1.rearrange("p (dt t) -> p dt t", t=H1S)[:, :, 0:TH],
                     waits=[("ve", self.ve_h1T), ("se2", self.now("se2"))],
                     slot="dbg")

    def p4_mixer_in(self):
        g = self.g
        h1 = self.bvv("h1T", 0, 128, 0, 8 * H1S)
        xcins = [self.fv("xcin", 0, 128, 0, 2051),
                 self.fv("xcin2", 0, 128, 0, 2051)]
        xc0 = self.fv("xc0", 0, 128, 0, 2048)
        xcsl = self.bvv("xcsl", 0, 128, 0, 2048)
        xpwb = self.bvv("xpwb", 0, 128, 0, 16 * 34)
        ps6b = self.ps[6][:, :].bitcast(BF16)
        chunks = [(0, 512), (512, 512), (1024, 512), (1536, 512), (2048, 3)]
        wx_used = {}
        evac_ve = {}
        se_tap0 = None
        se_silu = {}
        ve_conv = {}
        pe_cons = {}
        se2_tp = None
        for es in range(16):
            half = es % 2
            xcin = xcins[half]
            wr = self.bvv("wrot", 0, 128, half * 1088, 1024)
            w = [("pe", wx_used[es - 2])] if es >= 2 else []
            dwx = self.dma("gpsimd", wr.rearrange("p (K j) -> p K j", j=128),
                           g["wxc_t"][es].transpose([1, 0, 2]),
                           waits=w, slot="wx%d" % half)
            for ci, (c0, n) in enumerate(chunks):
                bank = ci % 2
                for K in range(8):
                    wts = []
                    if K == 0:
                        wts = [dwx]
                        if (es, ci) != (0, 0) and bank in evac_ve:
                            wts.append(("se", evac_ve[bank]))
                    self.op("tensor", lambda e, K=K, c0=c0, n=n, bank=bank, wr=wr:
                            e.matmul(self.ps[bank][:, 0:n],
                                     wr[:, K * 128:(K + 1) * 128],
                                     h1[:, K * H1S + c0: K * H1S + c0 + n],
                                     start=(K == 0), stop=(K == 7)),
                            waits=wts, sem="pe" if K == 7 else None)
                pe = self.now("pe")
                evw = [("pe", pe)]
                if ci == 0 and es >= 2:
                    evw.append(("ve", ve_conv[es - 2]))
                self.op("scalar", lambda e, c0=c0, n=n, bank=bank, xcin=xcin:
                        e.activation(xcin[:, c0:c0 + n], self.ps[bank][:, 0:n],
                                     AF.Identity, bias=0.0, scale=1.0),
                        waits=evw, sem="se")
                evac_ve[bank] = self.now("se")
            wx_used[es] = self.now("pe")
            cw = self.fv("convw", 0, 128, 0, 64)
            cb = self.fv("convw", 0, 128, 64, 16)
            t0w = [("se", self.now("se"))]
            if es == 0:
                t0w += [self.ld["convw"], self.ld["convb"]]
            se_tap0 = self.op("scalar", lambda e, es=es, xcin=xcin: e.activation(
                xc0, xcin[:, 0:2048], AF.Identity,
                bias=cb[:, es:es + 1], scale=cw[:, es * 4: es * 4 + 1]),
                waits=t0w, sem="se")
            for k in (1, 2, 3):
                tw = [("se", se_tap0)] if k == 1 else []
                self.op("vector", lambda e, k=k, es=es, xcin=xcin:
                        e.scalar_tensor_tensor(
                    xc0, xcin[:, k:2048 + k], cw[:, es * 4 + k: es * 4 + k + 1],
                    xc0, ALU.mult, ALU.add), waits=tw, sem="ve")
            ve_conv[es] = self.now("ve")
            vconv = self.now("ve")
            sw = [("ve", vconv)]
            if es > 0:
                sw.append(("pe", pe_cons[es - 1]))
            se_silu[es] = self.op("scalar", lambda e: e.activation(
                xcsl, xc0, AF.Silu, bias=0.0, scale=1.0), waits=sw, sem="se")
            for n4 in range(4):
                self.op("tensor", lambda e, n4=n4, es=es: e.matmul(
                    self.ps[2 + n4][0:33, 0:512],
                    xpwb[:, es * 34: es * 34 + 33],
                    xcsl[:, n4 * 512:(n4 + 1) * 512],
                    start=(es == 0), stop=(es == 15)),
                    waits=([("se", se_silu[es])] +
                           ([self.ld["xpwb"]] if es == 0 else []))
                    if n4 == 0 else [], sem=None)
            xcs = self.bvv("xcsTM", 0, 128, 0, 16 * 2048)
            for b2 in range(2):
                for ti in range(8):
                    tt = b2 * 8 + ti
                    wts = []
                    if ti == 0:
                        wts = [("se", se_silu[es])]
                        if se2_tp is not None:
                            wts.append(("se2", se2_tp))
                    self.op("tensor", lambda e, tt=tt, ti=ti: e.transpose(
                        ps6b[:, ti * 128:(ti + 1) * 128],
                        xcsl[:, tt * 128:(tt + 1) * 128],
                        self.sid[:, :]),
                        waits=wts, sem="pe" if ti == 7 else None)
                pe = self.now("pe")
                self.op("scalar", lambda e, b2=b2, es=es: e.activation(
                    xcs.rearrange("p (ch t) -> p ch t", t=2048)
                    [:, b2 * 8:(b2 + 1) * 8, es * 128:(es + 1) * 128],
                    ps6b[:, 0:1024].rearrange("p (ti t) -> p ti t", t=128),
                    AF.Identity, bias=0.0, scale=1.0),
                    waits=[("pe", pe)], sem="se2")
                se2_tp = self.now("se2")
            pe_cons[es] = self.now("pe")
        pe = self.now("pe")
        for n4 in range(4):
            self.op("vector", lambda e, n4=n4: e.tensor_copy(
                self.fv("xp", 0, 33, n4 * 512, 512), self.ps[2 + n4][0:33, 0:512]),
                waits=[("pe", pe)] if n4 == 0 else [], sem="ve")
        self.ve_xp = self.now("ve")
        self.se2_xcs = self.now("se2")
        if "xp" in self.dbgo:
            self.dma("sync", self.dbgo["xp"][:, :], self.fv("xp", 0, 33, 0, 2048),
                     waits=[("ve", self.ve_xp)], slot="dbg")
        if "xcs_bf" in self.dbgo:
            self.dma("sync", self.dbgo["xcs_bf"][:, :],
                     self.bvv("xcsTM", 0, 128, 0, 16 * 2048),
                     waits=[("se2", self.se2_xcs)], slot="dbg2")

    def p5a_smalls(self):
        g = self.g
        comb = self.fv("comb", 0, 34, 0, 2048)
        d1 = self.cdma("sync", self.fv("comb", 0, 16, 0, 2048),
                      self.fv("xp", 17, 16, 0, 2048),
                      waits=[("ve", self.ve_xp)], slot="c2")
        d2 = self.cdma("sync", self.fv("comb", 16, 16, 0, 2048),
                      self.fv("xp", 1, 16, 0, 2048),
                      waits=[("ve", self.ve_xp)], slot="c2")
        x0 = self.fv("e3t", 64, 1, 0, 2048)
        dx0 = self.cdma("sync", x0, self.fv("xp", 0, 1, 0, 2048),
                       waits=[("ve", self.ve_xp)], slot="c2")
        sc1 = self.fv("e3u", 64, 1, 0, 2048)
        sc2 = self.fv("repA", 64, 1, 0, 2048)
        self.op("vector", lambda e: e.tensor_scalar(sc1, x0, -1.0, None,
                                                    ALU.mult),
                waits=[dx0], sem="ve")
        self.op("vector", lambda e: e.tensor_tensor(sc1, x0, sc1, ALU.min),
                sem="ve")
        sp = self.op("scalar", lambda e: e.activation(sc1, sc1, AF.Exp,
                                                      bias=0.0, scale=1.0),
                     waits=[("ve", self.now("ve"))], sem="se")
        sp = self.op("scalar", lambda e: e.activation(sc1, sc1, AF.Ln,
                                                      bias=1.0, scale=1.0), sem="se")
        self.op("vector", lambda e: e.tensor_scalar_max(sc2, x0, 0.0),
                waits=[("se", sp)], sem="ve")
        vdl = self.op("vector", lambda e: e.tensor_add(sc1, sc1, sc2), sem="ve")
        ddl = self.cdma("sync", self.fv("comb", 32, 1, 0, 2048), sc1,
                       waits=[("ve", vdl)], slot="c2")
        ones = self.fv("e3t", 64, 1, 0, 2048)
        self.op("vector", lambda e: e.memset(ones, 1.0), sem="ve")
        rgc = self.fv("repA", 64, 1, 0, 2048)
        vrg = self.op("vector", lambda e: e.tensor_tensor_scan(
            rgc, ones, sc1, 0.0, ALU.mult, ALU.add),
            waits=[("ve", self.now("ve"))], sem="ve")
        drg = self.cdma("sync", g["rows_hbm"][12:13, 0:2048], rgc,
                       waits=[("ve", vrg)], slot="c2")
        drc2 = self.cdma("sync", self.fv("comb", 33, 1, 0, 2048), rgc,
                        waits=[("ve", vrg)], slot="c2")
        vcb = self.now("ve")
        xpTc = self.fv("xpTc", 0, 128, 0, 16 * 34)
        self.se2_xpT = {}
        for i in range(NCH):
            bank = 6 + (i % 2)
            wts = ([("ve", vcb), d1, d2, ddl, drc2, self.ld["sidf"]]
                   if i < 2 else [("se2", self.se2_xpT[i - 2])])
            self.op("tensor", lambda e, i=i, bank=bank: e.transpose(
                self.ps[bank][:, 0:34],
                comb[:, i * 128:(i + 1) * 128], self.sidf[0:34, 0:34]),
                waits=wts, sem="pe")
            pe = self.now("pe")
            self.op("scalar", lambda e, i=i, bank=bank: e.activation(
                xpTc[:, i * 34:(i + 1) * 34], self.ps[bank][:, 0:34],
                AF.Identity, bias=0.0, scale=1.0),
                waits=[("pe", pe)], sem="se2")
            self.se2_xpT[i] = self.now("se2")
        st = self.fv("cmb", 0, 1, 0, 48)
        self.op("vector", lambda e: e.memset(st[:, 0:1], 0.0), sem="ve")
        dbe = self.cdma("sync", st[:, 16:32],
                       g["rows_hbm"][12, 127:2048:128].rearrange("(a c) -> a c", a=1),
                       waits=[drg], slow=True, slot="c3")
        dbs = self.cdma("sync", st[:, 1:16],
                       g["rows_hbm"][12, 127:1920:128].rearrange("(a c) -> a c", a=1),
                       waits=[drg], slow=True, slot="c3")
        vb = self.op("vector", lambda e: e.tensor_sub(
            st[:, 32:48], st[:, 16:32], st[:, 0:16]),
            waits=[dbe, dbs, ("ve", self.now("ve"))], sem="ve")
        db = self.cdma("sync", g["rows_hbm"][11:12, 0:48], st,
                      waits=[("ve", vb)], slot="c3")
        dbst = self.cdma("sync", self.fv("cmb", 0, 128, 48, 16),
                        g["rows_hbm"][11:12, 0:16].partition_broadcast(128),
                        waits=[db], slot="c3")
        dbnd = self.cdma("sync", self.fv("cmb", 0, 128, 64, 16),
                        g["rows_hbm"][11, 16:32].rearrange("(a c) -> a c", a=1)
                        .partition_broadcast(128),
                        waits=[db], slot="c3")
        drr = self.cdma("sync", self.fv("rows16", 0, 16, 0, 16),
                       g["rows_hbm"][11, 32:48].rearrange("(a c) -> a c", a=1)
                       .partition_broadcast(16),
                       waits=[db], slot="c3")
        dec = self.fv("rows16", 0, 16, 16, 16)
        self.op("scalar", lambda e: e.activation(
            dec, self.fv("rows16", 0, 16, 0, 16), AF.Exp, bias=0.0,
            scale=self.fv("acol", 0, 16, 0, 1)),
            waits=[drr, self.ld["acol"]], sem="se")
        self.se_dec = self.now("se")
        self.d_bst128 = dbst
        self.d_bnd128 = dbnd
        vprev = None
        pka = None
        for i in range(NCH):
            sl = self.fv("rch", 0, 16, 0, 128)
            w = [drg, dbst, self.ld["acol"]]
            if vprev is not None:
                w.append(("ve", vprev))
            if pka is not None:
                w.append(pka)
            dsl = self.dma("sync", sl,
                           g["rows_hbm"][12, i * 128:(i + 1) * 128]
                           .rearrange("(a c) -> a c", a=1)
                           .partition_broadcast(16),
                           waits=w, slot="rch")
            self.op("vector", lambda e, sl=sl, i=i: e.tensor_scalar(
                sl, sl, self.fv("cmb", 0, 16, 48 + i, 1),
                None, ALU.subtract), waits=[dsl], sem="ve")
            vprev = self.op("vector", lambda e, sl=sl: e.tensor_scalar(
                sl, sl, self.fv("acol", 0, 16, 0, 1),
                None, ALU.mult), sem="ve")
            pka = self.dma("sync", g["krow_hbm"][i, 0], sl,
                           waits=[("ve", vprev)], slot="karow")
            self.dma("sync", g["krow_hbm"][i, 1],
                     self.fv("comb", 0, 16, i * 128, 128),
                     waits=[d1], slot="kcrow")
        self.ve_p5a = self.now("ve")

    def p5b_z(self):
        g = self.g
        h1 = self.bvv("h1T", 0, 128, 0, 8 * H1S)
        evac_se = {}
        dst_done = {}
        wz_used = {}
        for j4 in range(4):
            half = j4 % 2
            wz = self.bvv("wzrot", 0, 128, half * 4104, 4096)
            w = [("pe", wz_used[j4 - 2])] if j4 >= 2 else []
            dwz = self.dma("gpsimd", wz.rearrange("p (K j) -> p K j", j=512),
                           g["wz_t"][:, :, j4 * 512:(j4 + 1) * 512]
                           .transpose([1, 0, 2]),
                           waits=w, slot="wz%d" % half)
            for i in range(NCH):
                bank = i % 2
                for K in range(8):
                    wts = []
                    if K == 0:
                        wts = [dwz]
                        key = (bank,)
                        if key in evac_se:
                            wts.append(("se", evac_se[key]))
                    self.op("tensor", lambda e, K=K, i=i, bank=bank, wz=wz:
                            e.matmul(self.ps[bank][:, 0:512],
                                     h1[:, K * H1S + 3 + i * 128:
                                        K * H1S + 3 + (i + 1) * 128],
                                     wz[:, K * 512:(K + 1) * 512],
                                     start=(K == 0), stop=(K == 7)),
                            waits=wts, sem="pe" if K == 7 else None)
                pe = self.now("pe")
                stg = self.fv("xrl", 0, 128, bank * 256, 256).bitcast(BF16)
                sw = [("pe", pe)]
                key2 = (bank, "d")
                if key2 in dst_done:
                    sw.append(dst_done[key2])
                self.op("scalar", lambda e, stg=stg, bank=bank: e.activation(
                    stg, self.ps[bank][:, 0:512], AF.Silu, bias=0.0, scale=1.0),
                    waits=sw, sem="se")
                evac_se[(bank,)] = self.now("se")
                dst_done[(bank, "d")] = self.dma(
                    "sync", g["sz_hbm"][i, :, j4 * 512:(j4 + 1) * 512], stg,
                    waits=[("se", evac_se[(bank,)])], slot="szst%d" % bank)
            wz_used[j4] = self.now("pe")
        self.pe_z_done = self.now("pe")
        self.d_szst_full = [("D_szst0", self.now("D_szst0")),
                            ("D_szst1", self.now("D_szst1"))]

    def p5c_loopA(self):
        g = self.g
        xcs = self.bvv("xcsTM", 0, 128, 0, 16 * 2048)
        xpTc = self.fv("xpTc", 0, 128, 0, 16 * 34)
        hloc = self.fv("hstate", 0, 16, 0, 2048)
        dec = self.fv("rows16", 0, 16, 16, 16)
        vmz = self.op("vector", lambda e: e.memset(hloc, 0.0), sem="ve")
        rec_ve = {}
        for i in range(NCH):
            dend = self.fv("cmb", 0, 128, 80, 1)
            wts = [self.d_bnd128, self.ld["arep"]] if i == 0 else []
            self.op("vector", lambda e, i=i: e.tensor_scalar(
                dend, self.fv("cmb", 0, 128, 64 + i, 1),
                xpTc[:, i * 34 + 33: i * 34 + 34], None, ALU.subtract),
                waits=wts, sem="ve")
            wet = self.fv("cmb", 0, 128, 82, 16)
            sw = self.op("scalar", lambda e: e.activation(
                wet, self.fv("arep"), AF.Exp, bias=0.0, scale=dend),
                waits=[("ve", self.now("ve"))], sem="se")
            dbc = self.fv("cmb", 0, 128, 116, 8).bitcast(BF16)
            self.op("vector", lambda e, i=i: e.tensor_scalar(
                dbc, xpTc[:, i * 34 + 16: i * 34 + 32],
                xpTc[:, i * 34 + 32: i * 34 + 33], None, ALU.mult), sem="ve")
            ub = self.bvv("ubufs", 0, 128, i * 16, 16)
            vu = self.op("vector", lambda e, ub=ub: e.tensor_mul(ub, dbc, wet),
                         waits=[("se", sw)], sem="ve")
            for n4 in range(4):
                bank = 6 + (n4 % 2)
                wts = [("ve", vu)] if n4 == 0 else []
                if (i, n4) >= (0, 2) and (bank, "rec") in rec_ve:
                    wts.append(("ve", rec_ve[(bank, "rec")]))
                self.op("tensor", lambda e, n4=n4, i=i, bank=bank, ub=ub: e.matmul(
                    self.ps[bank][0:16, 0:512], ub,
                    xcs[:, i * 2048 + n4 * 512: i * 2048 + (n4 + 1) * 512],
                    start=True, stop=True),
                    waits=wts, sem="pe")
                pe = self.now("pe")
                self.op("vector", lambda e, n4=n4, i=i, bank=bank:
                        e.scalar_tensor_tensor(
                            hloc[:, n4 * 512:(n4 + 1) * 512],
                            hloc[:, n4 * 512:(n4 + 1) * 512],
                            dec[:, i:i + 1],
                            self.ps[bank][0:16, 0:512], ALU.mult, ALU.add),
                        waits=[("pe", pe)] + ([("ve", vmz), ("se", self.se_dec)]
                                              if (i, n4) == (0, 0) else []),
                        sem="ve")
                rec_ve[(bank, "rec")] = self.now("ve")
        vrec = self.now("ve")
        d1 = self.cdma("sync", g["ag1_in"][0, 0:S * E]
                      .rearrange("(p c) -> p c", p=16), hloc,
                      waits=[("ve", vrec)], slot="c7")
        st2 = self.fv("cmb", 0, 1, 0, 16)
        self.op("vector", lambda e: e.memset(st2, 0.0), sem="ve")
        vr = self.op("vector", lambda e: e.tensor_copy(
            st2[:, 0:1], self.fv("cmb", 0, 1, 31, 1)), sem="ve")
        d2 = self.cdma("sync", g["ag1_in"][0:1, S * E:S * E + 16], st2,
                      waits=[("ve", vr)], slot="c7")
        self.op("gpsimd", lambda e: e.collective_compute(
            "AllGather", ALU.bypass, replica_groups=[list(range(NCORE))],
            ins=[g["ag1_in"][:, :]], outs=[g["ag1_out"][:, :]]),
            waits=[d1, d2], sem="cc")
        self.cc_ag1 = self.now("cc")
        self.d_ag1send = d1

    def p5d_hin(self):
        g = self.g
        r16 = lambda p0, pn, o, n: self.fv("rows16", p0, pn, o, n)
        dom = self.cdma("sync", r16(0, 8, 32, 8), g["ordmask_t"][:, :], slot="c5")
        drc = self.cdma("sync", r16(0, 8, 40, 1),
                       g["ag1_out"][:, S * E:S * E + 1],
                       waits=[("cc", self.cc_ag1)], slow=True, slot="c5")
        self.op("tensor", lambda e: e.matmul(
            self.ps[5][0:8, 0:1], r16(0, 8, 32, 8), r16(0, 8, 40, 1),
            start=True, stop=True), waits=[dom, drc], sem="pe")
        pe = self.now("pe")
        vs = self.op("vector", lambda e: e.tensor_copy(
            r16(0, 8, 42, 1), self.ps[5][0:8, 0:1]), waits=[("pe", pe)], sem="ve")
        ds = self.cdma("sync", g["rows_hbm"][13, 0:8].rearrange("(c a) -> c a", a=1),
                      r16(0, 8, 42, 1), waits=[("ve", vs)], slow=True, slot="c5")
        dsr = self.cdma("sync", r16(0, 16, 44, 8),
                       g["rows_hbm"][13:14, 0:8].partition_broadcast(16),
                       waits=[ds], slot="c5")
        dpm = self.cdma("sync", r16(0, 16, 52, 8),
                       g["pmask"][0:1, :].partition_broadcast(16), slot="c5")
        sc = self.op("scalar", lambda e: e.activation(
            r16(0, 16, 60, 8), r16(0, 16, 44, 8), AF.Exp, bias=0.0,
            scale=self.fv("acol", 0, 16, 0, 1)),
            waits=[dsr, self.ld["acol"]], sem="se")
        vc = self.op("vector", lambda e: e.tensor_mul(
            r16(0, 16, 60, 8), r16(0, 16, 60, 8), r16(0, 16, 52, 8)),
            waits=[("se", sc), dpm], sem="ve")
        hin = self.fv("hstate", 0, 16, 0, 2048)
        vmz = self.op("vector", lambda e: e.memset(hin, 0.0),
                      waits=[self.d_ag1send], sem="ve")
        vprev = None
        for j in range(NCORE):
            half = j % 2
            sl = self.fv("e3t" if half == 0 else "e3u", 0, 16, 0, 2048)
            w = [("cc", self.cc_ag1)]
            if j >= 2:
                w.append(("ve", vprev))
            dj = self.dma("sync", sl, g["ag1_out"][j, 0:S * E]
                          .rearrange("(p c) -> p c", p=16),
                          waits=w, slot="hj%d" % half)
            vprev = self.op("vector", lambda e, sl=sl, j=j: e.scalar_tensor_tensor(
                hin, sl, r16(0, 16, 60 + j, 1), hin, ALU.mult, ALU.add),
                waits=[dj] + ([("ve", vc), ("ve", vmz)] if j == 0 else []),
                sem="ve")
        hb = self.bvv("hinbf", 0, 16, 0, 2048)
        self.op("vector", lambda e: e.tensor_copy(hb, hin), sem="ve")
        self.ve_hin = self.now("ve")

    def p5e_pre(self):
        g = self.g
        wot = self.bvv("wot", 0, 128, 0, 16 * 1024)
        dws = []
        for K in range(16):
            dws.append(self.dma("gpsimd", wot[:, K * 1024:(K + 1) * 1024],
                                g["wout_t"][K], waits=[("pe", self.pe_z_done)]
                                if K == 0 else [], slot="wotl"))
        mx = self.fv("modx")
        dga = self.cdma("sync", g["rows_hbm"][14, 0:1024]
                       .rearrange("(dt p) -> p dt", p=128), mx[:, 16:24],
                       waits=[("ve", self.ve_mod)], slow=True, slot="c6")
        dgr = self.cdma("sync", self.fv("garepf", 0, 128, 0, 1024),
                       g["rows_hbm"][14:15, 0:1024].partition_broadcast(128),
                       waits=[dga], slot="c6")
        gab = self.bvv("garep", 0, 128, 0, 1024)
        vg = self.op("vector", lambda e: e.tensor_copy(
            gab, self.fv("garepf", 0, 128, 0, 1024)), waits=[dgr], sem="ve")
        wfull = ("D_wotl", self.now("D_wotl"))
        for K in range(16):
            self.op("vector", lambda e, K=K: e.tensor_mul(
                wot[:, K * 1024:(K + 1) * 1024],
                wot[:, K * 1024:(K + 1) * 1024], gab),
                waits=[wfull, ("ve", vg)] if K == 0 else [], sem="ve")
        self.ve_wot = self.now("ve")
        self.d_drep = self.dma("gpsimd", self.bvv("drep", 0, 128, 0, 2048),
                               g["dparam"][0:1, :].partition_broadcast(128),
                               waits=[dgr, ("ve", vg)], slot="drep")

    def p5e_loopB(self):
        g = self.g
        xcs = self.bvv("xcsTM", 0, 128, 0, 16 * 2048)
        xpTc = self.fv("xpTc", 0, 128, 0, 16 * 34)
        hloc = self.fv("hstate", 0, 16, 0, 2048)
        dec = self.fv("rows16", 0, 16, 16, 16)
        repA = self.fv("repA", 0, 128, 0, 2048)
        repC = self.bvv("repC", 0, 128, 0, 2048)
        e3t = self.fv("e3t", 0, 128, 0, 2048)
        e3u = self.fv("e3u", 0, 128, 0, 2048)
        wot = self.bvv("wot", 0, 128, 0, 16 * 1024)
        hb = self.bvv("hinbf", 0, 16, 0, 2048)
        drep = self.bvv("drep", 0, 128, 0, 2048)
        ps6b = self.ps[6][:, :].bitcast(BF16)
        ps7b = self.ps[7][:, :].bitcast(BF16)
        vmz = self.op("vector", lambda e: e.memset(hloc, 0.0),
                      waits=[("ve", self.ve_hin)], sem="ve")
        st = {}
        ctx = {}
        hsns = {}

        def emit_hsn(i):
            half = i % 2
            hsn = self.bvv("hsn0" if half == 0 else "hsn1", 0, 16, 0, 2048)
            w = [("ve", vmz)] if i == 0 else [("ve", st[("recb", i - 1, 3)])]
            if ("ymm", i - 2) in st:
                w.append(("pe", st[("ymm", i - 2)]))
            vsn = self.op("scalar", lambda e, hsn=hsn: e.activation(
                hsn, hloc, AF.Identity, bias=0.0, scale=1.0),
                waits=w, sem="se")
            hsns[i] = (hsn, vsn)

        def stageA(i):
            half = i % 2
            if i > 0:
                emit_hsn(i - 1)
            dsz = self.dma("sync", self.bvv("szrot", 0, 128, half * 2052, 2048),
                           g["sz_hbm"][i], waits=[st[("y2c", i - 2)]]
                           if ("y2c", i - 2) in st else [], slot="szl%d" % half)
            dra = self.dma("sync", repA,
                           g["krow_hbm"][i:i + 1, 0, :].partition_broadcast(128),
                           waits=([("ve", st[("xsub", i - 1)])]
                                  if ("xsub", i - 1) in st else
                                  [("D_karow", self.now("D_karow")),
                                   ("D_c2", self.now("D_c2"))]),
                           slot="repA")
            drc = self.dma("gpsimd", repC,
                           g["krow_hbm"][i:i + 1, 1, :].partition_broadcast(128),
                           waits=([("ve", st[("crep", i - 1)])]
                                  if ("crep", i - 1) in st else
                                  [("D_kcrow", self.now("D_kcrow"))]),
                           slot="repC")
            rlc = self.fv("cmb", 0, 128, 114, 1)
            self.op("vector", lambda e, i=i: e.tensor_scalar(
                rlc, xpTc[:, i * 34 + 33: i * 34 + 34],
                self.fv("cmb", 0, 128, 48 + i, 1), None, ALU.subtract),
                waits=[self.d_bst128] if i == 0 else [], sem="ve")
            rA = self.fv("cmb", 0, 128, 82, 16)
            self.op("vector", lambda e: e.tensor_tensor(
                rA, self.fv("arep"), rlc.broadcast_to([128, 16]), ALU.mult),
                sem="ve")
            vx = self.op("vector", lambda e: e.tensor_tensor(
                e3t.rearrange("p (s t) -> p s t", t=128),
                repA.rearrange("p (s t) -> p s t", t=128),
                rA.unsqueeze(2).broadcast_to([128, 16, 128]), ALU.subtract),
                waits=[dra], sem="ve")
            st[("xsub", i)] = self.now("ve")
            self.op("vector", lambda e: e.tensor_scalar_min(e3t, e3t, 0.0),
                    sem="ve")
            sx = self.op("scalar", lambda e: e.activation(
                e3u, e3t, AF.Exp, bias=0.0, scale=1.0),
                waits=[("ve", self.now("ve"))], sem="se")
            dbc = self.fv("cmb", 0, 128, 116, 8).bitcast(BF16)
            self.op("vector", lambda e, i=i: e.tensor_scalar(
                dbc, xpTc[:, i * 34 + 16: i * 34 + 32],
                xpTc[:, i * 34 + 32: i * 34 + 33], None, ALU.mult),
                waits=[("se", sx)], sem="ve")
            self.op("vector", lambda e: e.tensor_tensor(
                e3u.rearrange("p (s t) -> p s t", t=128),
                e3u.rearrange("p (s t) -> p s t", t=128),
                dbc.unsqueeze(2).broadcast_to([128, 16, 128]), ALU.mult),
                sem="ve")
            self.op("vector", lambda e: e.tensor_mul(e3u, e3u, repC),
                    waits=[drc], sem="ve")
            st[("crep", i)] = self.now("ve")
            kbf = self.fv("tmp0", 0, 128, 648, 128)
            self.op("vector", lambda e: e.tensor_reduce(
                kbf, e3u.rearrange("p (s t) -> p t s", t=128), AX.X, ALU.add),
                sem="ve")
            kb = self.bvv("kbslot", 0, 128, half * 128, 128)
            vkb = self.op("vector", lambda e, kb=kb: e.tensor_mul(
                kb, kbf, self.stril[:, :]), waits=[self.ld["tril"]]
                if i == 0 else [], sem="ve")
            st[("kb", i)] = self.now("ve")
            rch = self.fv("rch", 0, 16, 0, 128)
            drch = self.dma("sync", rch,
                            g["rows_hbm"][12, i * 128:(i + 1) * 128]
                            .rearrange("(a c) -> a c", a=1)
                            .partition_broadcast(16),
                            waits=[("se", st[("gexp", i - 1)])]
                            if ("gexp", i - 1) in st else [], slot="rch")
            eg = self.fv("e3t", 0, 16, 0, 128)
            el = self.fv("e3t", 0, 16, 128, 128)
            sg1 = self.op("scalar", lambda e: e.activation(
                eg, rch, AF.Exp, bias=0.0, scale=self.fv("acol", 0, 16, 0, 1)),
                waits=[drch, ("ve", st[("xsub", i)])], sem="se")
            gsl = self.bvv("gslot", 0, 16, half * 128, 128)
            gglsl = self.bvv("gslot", 0, 16, 256 + half * 128, 128)
            vggl = self.op("vector", lambda e, gglsl=gglsl, i=i: e.tensor_mul(
                gglsl, self.fv("comb", 0, 16, i * 128, 128), eg),
                waits=[("se", sg1)], sem="ve")
            self.op("vector", lambda e, i=i: e.tensor_scalar(
                rch, rch, self.fv("cmb", 0, 16, 48 + i, 1), None, ALU.subtract),
                sem="ve")
            sg2 = self.op("scalar", lambda e: e.activation(
                el, rch, AF.Exp, bias=0.0, scale=self.fv("acol", 0, 16, 0, 1)),
                waits=[("ve", self.now("ve"))], sem="se")
            st[("gexp", i)] = self.now("se")
            vgl = self.op("vector", lambda e, gsl=gsl, i=i: e.tensor_mul(
                gsl, self.fv("comb", 0, 16, i * 128, 128), el),
                waits=[("se", sg2)], sem="ve")
            st[("g", i)] = self.now("ve")
            ctx[i] = (dsz, kb, gsl, gglsl)

        def stageB(i):
            half = i % 2
            dsz, kb, gsl, gglsl = ctx[i]
            if i not in hsns:
                emit_hsn(i)
            hsn, vsn = hsns[i]
            t1s = []
            for n4 in range(4):
                t1 = self.fv("tmp0", 0, 128, 520 + 0, 256).bitcast(BF16) \
                    if False else self.fv("tmp1", 0, 128, 0, 256 + n4 * 0)
                t1s.append(None)
            ub = self.bvv("ubufs", 0, 128, i * 16, 16)
            for n4 in range(4):
                bank = 6 + (n4 % 2)
                wts = []
                if ("tp", i - 1) in st and n4 < 2:
                    wts.append(("se2", st[("tp", i - 1)]))
                if ("recb", i, n4 - 2) in st:
                    wts.append(("ve", st[("recb", i, n4 - 2)]))
                if n4 == 0 and ("y2v", i - 1, 3) in st:
                    pass
                self.op("tensor", lambda e, n4=n4, i=i, bank=bank, ub=ub: e.matmul(
                    self.ps[bank][0:16, 0:512], ub,
                    xcs[:, i * 2048 + n4 * 512: i * 2048 + (n4 + 1) * 512],
                    start=True, stop=True), waits=wts, sem="pe")
                pe = self.now("pe")
                self.op("vector", lambda e, n4=n4, i=i, bank=bank:
                        e.scalar_tensor_tensor(
                            hloc[:, n4 * 512:(n4 + 1) * 512],
                            hloc[:, n4 * 512:(n4 + 1) * 512],
                            dec[:, i:i + 1],
                            self.ps[bank][0:16, 0:512], ALU.mult, ALU.add),
                        waits=[("pe", pe), ("se", vsn)], sem="ve")
                st[("recb", i, n4)] = self.now("ve")
            for n4 in range(4):
                bank = 2 + n4
                w1 = [("ve", st[("kb", i)])]
                if ("y2v", i - 1, n4) in st:
                    w1.append(("ve", st[("y2v", i - 1, n4)]))
                self.op("tensor", lambda e, n4=n4, i=i, bank=bank, kb=kb: e.matmul(
                    self.ps[bank][:, 0:512], kb,
                    xcs[:, i * 2048 + n4 * 512: i * 2048 + (n4 + 1) * 512],
                    start=True, stop=False), waits=w1, sem=None)
                self.op("tensor", lambda e, n4=n4, bank=bank, gsl=gsl, hsn=hsn:
                        e.matmul(self.ps[bank][:, 0:512], gsl,
                                 hsn[:, n4 * 512:(n4 + 1) * 512],
                                 start=False, stop=False),
                        waits=[("ve", st[("g", i)]), ("se", vsn)]
                        if n4 == 0 else [], sem=None)
                self.op("tensor", lambda e, n4=n4, bank=bank, gglsl=gglsl:
                        e.matmul(self.ps[bank][:, 0:512], gglsl,
                                 hb[:, n4 * 512:(n4 + 1) * 512],
                                 start=False, stop=True),
                        waits=[("ve", self.ve_hin)] if (i, n4) == (0, 0) else [],
                        sem="pe")
            st[("ymm", i)] = self.now("pe")
            y2sl = self.bvv("y2r", 0, 128, 0, 2048)
            for n4 in range(4):
                bank = 2 + n4
                t1 = self.fv("tmp0", 0, 128, 520, 256).bitcast(BF16)
                t2 = self.fv("tmp1", 0, 128, 520, 512)
                self.op("vector", lambda e, n4=n4, i=i, t1=t1: e.tensor_mul(
                    t1, xcs[:, i * 2048 + n4 * 512: i * 2048 + (n4 + 1) * 512],
                    drep[:, n4 * 512:(n4 + 1) * 512]),
                    waits=[self.d_drep] if (i, n4) == (0, 0) else [], sem="ve")
                self.op("vector", lambda e, bank=bank, t1=t1, t2=t2: e.tensor_add(
                    t2, self.ps[bank][:, 0:512], t1),
                    waits=[("pe", st[("ymm", i)])] if n4 == 0 else [], sem="ve")
                self.op("vector", lambda e, n4=n4, t2=t2, y2sl=y2sl, half=half:
                        e.tensor_mul(y2sl[:, n4 * 512:(n4 + 1) * 512], t2,
                                     self.bvv("szrot", 0, 128, half * 2052, 2048)
                                     [:, n4 * 512:(n4 + 1) * 512]),
                        waits=([dsz] if n4 == 0 else []) +
                        ([self._dbgy] if n4 == 0 and getattr(self, "_dbgy", None)
                         else []), sem="ve")
                st[("y2v", i, n4)] = self.now("ve")
            st[("y2c", i)] = ("ve", self.now("ve"))
            if "y2_bf" in self.dbgo:
                self._dbgy = self.dma("sync", self.dbgo["y2_bf"][i], y2sl,
                                      waits=[st[("y2c", i)]], slot="dbgy")
            for b2 in range(2):
                psb = ps6b if b2 == 0 else ps7b
                for ti in range(8):
                    es = b2 * 8 + ti
                    wts = []
                    if ti == 0:
                        wts = [("ve", st[("y2v", i, 3)]),
                               ("ve", st[("recb", i, 3 if b2 == 0 else 2)])]
                        if ("opj", i - 1) in st:
                            wts.append(("pe", st[("opj", i - 1)]))
                    self.op("tensor", lambda e, es=es, ti=ti, psb=psb, y2sl=y2sl:
                            e.transpose(psb[:, ti * 128:(ti + 1) * 128],
                                        y2sl[:, es * 128:(es + 1) * 128],
                                        self.sid[:, :]),
                            waits=wts, sem="pe" if ti == 7 else None)
                pe = self.now("pe")
                self.op("scalar", lambda e, b2=b2, psb=psb: e.activation(
                    self.bvv("y2f", 0, 128, b2 * 1024, 1024), psb[:, 0:1024],
                    AF.Identity, bias=0.0, scale=1.0),
                    waits=[("pe", pe)], sem="se2")
            st[("tp", i)] = self.now("se2")
            y2f = self.bvv("y2f", 0, 128, 0, 2048)
            dxr = self.dma("sync", self.fv("xrl", 0, 128, 0, 1024),
                           g["x_tm"][3 + i * 128: 3 + (i + 1) * 128, :],
                           waits=[("ve", st[("x2v", i - 1)])]
                           if ("x2v", i - 1) in st else
                           ([("pe", self.pe_z_done)] + self.d_szst_full
                            if i == 0 else []),
                           slot="xrl")
            for dN in range(2):
                for Ke in range(16):
                    wts = []
                    if Ke == 0:
                        wts = [("se2", st[("tp", i)])]
                        if (i, dN) == (0, 0):
                            wts.append(("ve", self.ve_wot))
                        if ("x2v", i - 1) in st:
                            wts.append(("ve", st[("x2v", i - 1)]))
                    self.op("tensor", lambda e, dN=dN, Ke=Ke, y2f=y2f: e.matmul(
                        self.ps[dN][:, 0:512], y2f[:, Ke * 128:(Ke + 1) * 128],
                        wot[:, Ke * 1024 + dN * 512: Ke * 1024 + dN * 512 + 512],
                        start=(Ke == 0), stop=(Ke == 15)),
                        waits=wts, sem="pe" if Ke == 15 else None)
            st[("opj", i)] = self.now("pe")
            pe = self.now("pe")
            for dN in range(2):
                xo = (self.fv("tmp0", 0, 128, 0, 512) if dN == 0
                      else self.fv("tmp1", 0, 128, 0, 512))
                self.op("vector", lambda e, dN=dN, xo=xo: e.tensor_add(
                    xo, self.fv("xrl", 0, 128, dN * 512, 512),
                    self.ps[dN][:, 0:512]),
                    waits=[("pe", pe), dxr] + ([st[("x2st", i - 1, dN)]]
                                               if ("x2st", i - 1, dN) in st
                                               else []), sem="ve")
                st[("x2v", i)] = self.now("ve")
                st[("x2st", i, dN)] = self.dma(
                    "sync", g["x2_hbm"][i * 128:(i + 1) * 128,
                                        dN * 512:dN * 512 + 512], xo,
                    waits=[("ve", self.now("ve"))], slot="x2st%d" % dN)

        stageA(0)
        for i in range(NCH):
            if i + 1 < NCH:
                stageA(i + 1)
            stageB(i)
        self.ve_p5 = self.now("ve")
        self.st_p5 = st

    def mband(self, it):
        if it < 16:
            return self.bvv("xcsTM", 0, 128, it * 2048, 2048)
        return self.bvv("R1", 0, 128, (it - 16) * 2048, 2048)

    def p7_ln2(self):
        g = self.g
        h2 = self.fv("h2TM", 0, 128, 0, 16 * 1024)
        mx = self.fv("modx")
        mc = self.fv("mod_cols")
        d1 = self.cdma("sync", g["rows_hbm"][18, 0:1024]
                      .rearrange("(dt p) -> p dt", p=128), mx[:, 8:16],
                      slow=True, slot="c8")
        d2 = self.cdma("sync", g["rows_hbm"][17, 0:1024]
                      .rearrange("(dt p) -> p dt", p=128), mc[:, 24:32],
                      slow=True, slot="c8")
        dcb = self.cdma("sync", self.fv("cbrep", 0, 128, 0, 1024),
                       g["rows_hbm"][18:19, 0:1024].partition_broadcast(128),
                       waits=[d1], slot="c8")
        dsb = self.cdma("sync", self.fv("sbrep", 0, 128, 0, 1024),
                       g["rows_hbm"][17:18, 0:1024].partition_broadcast(128),
                       waits=[d2], slot="c8")
        x2w = [("D_x2st0", self.now("D_x2st0")),
               ("D_x2st1", self.now("D_x2st1"))]
        amc = self.fv("cmb", 0, 128, 0, 1)
        vam = self.op("vector", lambda e: e.memset(amc, 0.0), sem="ve")
        sums = self.fv("cmb", 0, 128, 48, 16)
        sums2 = self.fv("cmb", 0, 128, 64, 16)
        a2c = self.fv("cmb", 0, 128, 80, 16)
        b2c = self.fv("cmb", 0, 128, 96, 16)
        st = {}
        dl = {}
        for ch in range(NCH):
            h2c = h2[:, ch * 1024:(ch + 1) * 1024]
            w = list(x2w) if ch < 2 else [dl[ch - 2]]
            dl[ch] = self.dma("sync", h2c,
                              g["x2_hbm"][ch * 128:(ch + 1) * 128, :],
                              waits=w, slot="xr%d" % (ch % 2))
            sq = self.fv("tmp0", 0, 128, 0, 1024) if ch % 2 == 0 else \
                self.fv("tmp1", 0, 128, 0, 1024)
            sqs = self.op("scalar", lambda e, x=h2c, q=sq: e.activation(
                q, x, AF.Square, bias=0.0, scale=1.0),
                waits=[dl[ch]] + ([("ve", st[("s2", ch - 2)])]
                                  if ch >= 2 else []),
                sem="se")
            self.op("vector", lambda e, x=h2c, ch=ch: e.reduce_sum(
                sums[:, ch:ch + 1], x, AX.X), waits=[dl[ch]], sem="ve")
            self.op("vector", lambda e, q=sq, ch=ch: e.reduce_sum(
                sums2[:, ch:ch + 1], q, AX.X), waits=[("se", sqs)], sem="ve")
            st[("s2", ch)] = self.now("ve")
        self.op("vector", lambda e: e.tensor_scalar_mul(sums, sums, 1.0 / D),
                sem="ve")
        self.op("vector", lambda e: e.tensor_scalar(
            sums2, sums2, 1.0 / D, EPS, ALU.mult, ALU.add), sem="ve")
        self.op("vector", lambda e: e.tensor_mul(b2c, sums, sums), sem="ve")
        self.op("vector", lambda e: e.tensor_sub(sums2, sums2, b2c), sem="ve")
        sv = self.op("scalar", lambda e: e.activation(
            sums2, sums2, AF.Sqrt, bias=0.0, scale=1.0),
            waits=[("ve", self.now("ve"))], sem="se")
        self.op("vector", lambda e: e.reciprocal(a2c, sums2),
                waits=[("se", sv)], sem="ve")
        self.op("vector", lambda e: e.tensor_mul(b2c, sums, a2c), sem="ve")
        vb2 = self.op("vector", lambda e: e.tensor_scalar_mul(b2c, b2c, -1.0),
                      sem="ve")
        for ch in range(NCH):
            h2c = h2[:, ch * 1024:(ch + 1) * 1024]
            sa_ = self.op("scalar", lambda e, o=h2c, ch=ch: e.activation(
                o, o, AF.Identity, bias=b2c[:, ch:ch + 1],
                scale=a2c[:, ch:ch + 1]),
                waits=[("ve", vb2)] if ch == 0 else [], sem="se")
            self.op("vector", lambda e, o=h2c: e.tensor_mul(
                o, o, self.fv("cbrep", 0, 128, 0, 1024)),
                waits=[("se", sa_), dcb] if ch == 0 else [("se", sa_)],
                sem="ve")
            self.op("vector", lambda e, o=h2c: e.tensor_add(
                o, o, self.fv("sbrep", 0, 128, 0, 1024)),
                waits=[dsb] if ch == 0 else [], sem="ve")
            tc = self.fv("cmb", 0, 128, 14, 1)
            self.op("vector", lambda e, o=h2c: e.reduce_max(
                tc, o, AX.X, apply_absolute_value=True), sem="ve")
            self.op("vector", lambda e: e.tensor_tensor(amc, amc, tc, ALU.max),
                    waits=[("ve", vam)] if ch == 0 else [], sem="ve")
        vamf = self.now("ve")
        dsr = self.cdma("sync", g["rows_hbm"][16, 0:128]
                       .rearrange("(c a) -> c a", a=1), amc,
                       waits=[("ve", vamf)], slow=True, slot="c9")
        arow = self.fv("xp", 0, 1, 0, 128)
        dld = self.cdma("sync", arow, g["rows_hbm"][16:17, 0:128],
                       waits=[dsr], slot="c9")
        gsc = self.fv("xp", 0, 1, 130, 1)
        vg = self.op("vector", lambda e: e.reduce_max(gsc, arow, AX.X),
                     waits=[dld], sem="ve")
        dag = self.cdma("sync", g["ag2_in"][:, :], gsc, waits=[("ve", vg)],
                       slot="c9")
        self.op("gpsimd", lambda e: e.collective_compute(
            "AllReduce", ALU.max, replica_groups=[list(range(NCORE))],
            ins=[g["ag2_in"][:, :]], outs=[g["ag2_out"][:, :]]),
            waits=[dag], sem="cc")
        cc2 = self.now("cc")
        gx = self.fv("cmb", 0, 128, 18, 1)
        dgx = self.cdma("sync", gx, g["ag2_out"][0:1, :].partition_broadcast(128),
                       waits=[("cc", cc2)], slot="c9")
        cxc = self.fv("cmb", 0, 128, 20, 1)
        dcx = self.cdma("sync", cxc, g["clip_x"][0:1, :].partition_broadcast(128),
                       slot="c9")
        ssx = self.op("scalar", lambda e: e.activation(
            cxc, cxc, AF.Sigmoid, bias=0.0, scale=1.0), waits=[dcx], sem="se")
        tsg = self.fv("cmb", 0, 128, 22, 1)
        self.op("vector", lambda e: e.tensor_mul(tsg, cxc, gx),
                waits=[("se", ssx), dgx], sem="ve")
        sxi = self.fv("cmb", 0, 128, 24, 1)
        self.op("vector", lambda e: e.reciprocal(sxi, tsg), sem="ve")
        vsx = self.op("vector", lambda e: e.tensor_scalar_mul(sxi, sxi, 127.0),
                      sem="ve")
        self.tsg_col = tsg
        self.mgc = self.fv("cmb", 0, 128, 42, 1)
        self.nmgc = self.fv("cmb", 0, 128, 44, 1)
        self.op("vector", lambda e: e.memset(self.mgc, MAGIC), sem="ve")
        vmg = self.op("vector", lambda e: e.memset(self.nmgc, -MAGIC), sem="ve")
        self.vmg = vmg
        qx = self.bvv("qx", 0, 128, 0, 16384)
        ps7b = self.ps[7][:, :].bitcast(BF16)
        for ch in range(NCH):
            h2c = h2[:, ch * 1024:(ch + 1) * 1024]
            tq = self.fv("tmp0", 0, 128, 0, 1024) if ch % 2 == 0 else \
                self.fv("tmp1", 0, 128, 0, 1024)
            qs = self.bvv("y2r", 0, 128, (ch % 2) * 1024, 1024)
            w = ([("ve", vsx), ("ve", vmg)] if ch < 2
                 else [("se2", st[("qev", ch - 2)])])
            s1 = self.op("scalar", lambda e, i=h2c, o=tq: e.activation(
                o, i, AF.Identity, bias=self.mgc, scale=sxi),
                waits=w, sem="se")
            s2 = self.op("scalar", lambda e, o=tq, q=qs: e.activation(
                q, o, AF.Identity, bias=self.nmgc, scale=1.0), sem="se")
            vq = self.op("vector", lambda e, q=qs: e.tensor_scalar(
                q, q, 127.0, -128.0, ALU.min, ALU.max),
                waits=[("se", s2)], sem="ve")
            for dt in range(8):
                self.op("tensor", lambda e, dt=dt, q=qs: e.transpose(
                    ps7b[:, dt * 128:(dt + 1) * 128],
                    q[:, dt * 128:(dt + 1) * 128], self.sid[:, :]),
                    waits=[("ve", vq), ("se2", self.now("se2"))]
                    if dt == 0 else [],
                    sem="pe" if dt == 7 else None)
            pe = self.now("pe")
            for dt in range(8):
                self.op("scalar", lambda e, dt=dt, ch=ch: e.activation(
                    qx[:, dt * 2048 + ch * 128: dt * 2048 + (ch + 1) * 128],
                    ps7b[:, dt * 128:(dt + 1) * 128], AF.Identity,
                    bias=0.0, scale=1.0),
                    waits=[("pe", pe)] if dt == 0 else [], sem="se2")
            st[("qev", ch)] = self.now("se2")
        self.se2_qx = self.now("se2")

    def p8a_gateup(self):
        g = self.g
        qx = self.bvv("qx", 0, 128, 0, 16384)
        fcol = self.fv("cmb", 0, 128, 26, 1)
        vf = self.op("vector", lambda e: e.tensor_scalar_mul(
            fcol, self.tsg_col, 1.0 / 16129.0), sem="ve")
        sgc = self.fv("scl_g", 0, 128, 0, 22)
        suc = self.fv("scl_u", 0, 128, 0, 22)
        vsc = []
        for nm, row, dst in (("sg", 4, sgc), ("su", 5, suc)):
            d = self.cdma("sync", dst, g["rows_hbm"][row, 0:I]
                         .rearrange("(it p) -> p it", p=128),
                         slow=True, slot="c10")
            vsc.append(self.op("vector", lambda e, a=dst: e.tensor_scalar(
                a, a, fcol, None, ALU.mult), waits=[d, ("ve", vf)], sem="ve"))
        amc = self.fv("cmb", 0, 128, 28, 1)
        vam = self.op("vector", lambda e: e.memset(amc, 0.0), sem="ve")
        st = {}
        for it in range(IT):
            half = it % 2
            wg = self.bvv("qwrot", 0, 128, half * 4096, 1024)
            wu = self.bvv("qwrot", 0, 128, half * 4096 + 1024, 1024)
            lw = [("cc", self.cc_agq)] if it == 0 else \
                ([("pe", st[("mml", it - 2)])] if it >= 2 else [])
            dg = self.dma("sync", wg.rearrange("p (c i) -> p c i", i=128),
                          g["agq_out"][:, it * 16384:(it + 1) * 16384]
                          .rearrange("c (p i) -> p c i", p=128),
                          waits=lw, slot="qwg%d" % half)
            du = self.dma("sync", wu.rearrange("p (c i) -> p c i", i=128),
                          g["agq_out"][:, 128 * I + it * 16384:
                                       128 * I + (it + 1) * 16384]
                          .rearrange("c (p i) -> p c i", p=128),
                          waits=lw, slot="qwu%d" % half)
            for tb in range(4):
                bg, bu = tb % 2, 2 + tb % 2
                for dt in range(8):
                    w = []
                    if dt == 0:
                        w = [dg] + ([("se", st[("sil", it, tb - 2)])]
                                    if ("sil", it, tb - 2) in st else []) + \
                            ([("se", st[("sil", it - 1, tb + 2)])]
                             if ("sil", it - 1, tb + 2) in st else []) + \
                            ([("se2", self.se2_qx)] if (it, tb) == (0, 0) else [])
                    self.op("tensor", lambda e, dt=dt, tb=tb, bg=bg, wg=wg:
                            e.matmul(self.ps[bg][:, 0:512],
                                     wg[:, dt * 128:(dt + 1) * 128],
                                     qx[:, dt * 2048 + tb * 512:
                                        dt * 2048 + (tb + 1) * 512],
                                     start=(dt == 0), stop=(dt == 7)),
                            waits=w, sem="pe" if dt == 7 else None)
                for dt in range(8):
                    w = []
                    if dt == 0:
                        w = [du] + ([("ve", st[("mst", it, tb - 2)])]
                                    if ("mst", it, tb - 2) in st else []) + \
                            ([("ve", st[("mst", it - 1, tb + 2)])]
                             if ("mst", it - 1, tb + 2) in st else [])
                    self.op("tensor", lambda e, dt=dt, tb=tb, bu=bu, wu=wu:
                            e.matmul(self.ps[bu][:, 0:512],
                                     wu[:, dt * 128:(dt + 1) * 128],
                                     qx[:, dt * 2048 + tb * 512:
                                        dt * 2048 + (tb + 1) * 512],
                                     start=(dt == 0), stop=(dt == 7)),
                            waits=w, sem="pe" if dt == 7 else None)
                pe = self.now("pe")
                if tb == 3:
                    st[("mml", it)] = pe
                sgb = self.fv("tmp0", 0, 128, 0, 512) if tb % 2 == 0 else \
                    self.fv("tmp1", 0, 128, 0, 512)
                ssil = self.op("scalar", lambda e, bg=bg, it=it, o=sgb:
                               e.activation(o, self.ps[bg][:, 0:512], AF.Silu,
                                            bias=0.0,
                                            scale=sgc[:, it:it + 1]),
                               waits=[("pe", pe)] +
                               ([("ve", vsc[0])] if it == 0 and tb == 0 else []),
                               sem="se")
                st[("sil", it, tb)] = ssil
                msl = self.mband(it)[:, tb * 512:(tb + 1) * 512]
                self.op("vector", lambda e, bu=bu, it=it, o=msl, sgb=sgb:
                        e.scalar_tensor_tensor(o, self.ps[bu][:, 0:512],
                                               suc[:, it:it + 1], sgb,
                                               ALU.mult, ALU.mult),
                        waits=[("se", ssil)] +
                        ([("ve", vsc[1])] if it == 0 and tb == 0 else []),
                        sem="ve")
                tc = self.fv("cmb", 0, 128, 30, 1)
                self.op("vector", lambda e, o=msl: e.reduce_max(
                    tc, o, AX.X, apply_absolute_value=True), sem="ve")
                self.op("vector", lambda e: e.tensor_tensor(
                    amc, amc, tc, ALU.max),
                    waits=[("ve", vam)] if (it, tb) == (0, 0) else [], sem="ve")
                st[("mst", it, tb)] = self.now("ve")
        vamf = self.now("ve")
        dsr = self.cdma("sync", g["rows_hbm"][19, 0:128]
                       .rearrange("(c a) -> c a", a=1), amc,
                       waits=[("ve", vamf)], slow=True, slot="c11")
        arow = self.fv("xp", 0, 1, 132, 128)
        dld = self.cdma("sync", arow, g["rows_hbm"][19:20, 0:128],
                       waits=[dsr], slot="c11")
        gsc = self.fv("xp", 0, 1, 262, 1)
        vg = self.op("vector", lambda e: e.reduce_max(gsc, arow, AX.X),
                     waits=[dld], sem="ve")
        dag = self.cdma("sync", g["ag3_in"][:, :], gsc, waits=[("ve", vg)],
                       slot="c11")
        self.op("gpsimd", lambda e: e.collective_compute(
            "AllReduce", ALU.max, replica_groups=[list(range(NCORE))],
            ins=[g["ag3_in"][:, :]], outs=[g["ag3_out"][:, :]]),
            waits=[dag], sem="cc")
        self.cc_ag3 = self.now("cc")

    def p8b_down(self):
        g = self.g
        g2 = self.fv("cmb", 0, 128, 34, 1)
        dg2 = self.cdma("sync", g2, g["ag3_out"][0:1, :].partition_broadcast(128),
                       waits=[("cc", self.cc_ag3)], slot="c11")
        cdc = self.fv("cmb", 0, 128, 36, 1)
        dcd = self.cdma("sync", cdc, g["clip_dn"][0:1, :].partition_broadcast(128),
                       slot="c11")
        ssd = self.op("scalar", lambda e: e.activation(
            cdc, cdc, AF.Sigmoid, bias=0.0, scale=1.0), waits=[dcd], sem="se")
        s2f = self.fv("cmb", 0, 128, 38, 1)
        self.op("vector", lambda e: e.tensor_mul(s2f, cdc, g2),
                waits=[("se", ssd), dg2], sem="ve")
        s2i = self.fv("cmb", 0, 128, 40, 1)
        self.op("vector", lambda e: e.reciprocal(s2i, s2f), sem="ve")
        vs2 = self.op("vector", lambda e: e.tensor_scalar_mul(s2i, s2i, 127.0),
                      sem="ve")
        cd8 = self.fv("coef_dn", 0, 128, 0, 8)
        ddn = self.cdma("sync", cd8, g["rows_hbm"][6, 0:1024]
                       .rearrange("(dt p) -> p dt", p=128), slow=True,
                       slot="c11")
        self.op("vector", lambda e: e.tensor_mul(
            cd8, cd8, self.fv("modx", 0, 128, 24, 8)), waits=[ddn], sem="ve")
        self.op("vector", lambda e: e.tensor_scalar(
            cd8, cd8, s2f, None, ALU.mult), sem="ve")
        vcf = self.op("vector", lambda e: e.tensor_scalar_mul(
            cd8, cd8, 1.0 / 16129.0), sem="ve")
        dcr = self.cdma("sync", g["rows_hbm"][20, 0:1024]
                       .rearrange("(dt p) -> p dt", p=128), cd8,
                       waits=[("ve", vcf)], slow=True, slot="c11")
        dcf = self.cdma("sync", self.fv("coefrep", 0, 128, 0, 1024),
                       g["rows_hbm"][20:21, 0:1024].partition_broadcast(128),
                       waits=[dcr], slot="c11")
        def qdt(hb):
            if hb < 16:
                return self.bvv("R4", 0, 128, hb * 1024, 1024)
            return self.bvv("R6", 0, 128, (hb - 16) * 1024, 1024)
        for hb in range(IT):
            hv, itl = divmod(hb, 11)
            off = 2 * 128 * I + hv * 11 * 16384 + itl * 16384
            self.dma("sync", qdt(hb).rearrange("p (c h) -> p c h", h=128),
                     g["agq_out"][:, off:off + 16384]
                     .rearrange("c (p h) -> p c h", p=128),
                     slot="qdl")
        dqd = [("D_qdl", self.now("D_qdl"))]
        for hb in range(IT):
            band = self.mband(hb)
            for hf in range(2):
                tq = self.fv("tmp0", 0, 128, 0, 1024) if hf == 0 else \
                    self.fv("tmp1", 0, 128, 0, 1024)
                sl = band[:, hf * 1024:(hf + 1) * 1024]
                s1 = self.op("scalar", lambda e, i=sl, o=tq: e.activation(
                    o, i, AF.Identity, bias=self.mgc, scale=s2i),
                    waits=[("ve", vs2)] if hb == 0 and hf == 0 else [],
                    sem="se")
                s2 = self.op("scalar", lambda e, o=tq, q=sl: e.activation(
                    q, o, AF.Identity, bias=self.nmgc, scale=1.0), sem="se")
                self.op("vector", lambda e, q=sl: e.tensor_scalar(
                    q, q, 127.0, -128.0, ALU.min, ALU.max),
                    waits=[("se", s2)], sem="ve")
        vqmf = self.now("ve")
        st = {}
        for ch in range(NCH):
            dxr = self.dma("sync", self.fv("xrl", 0, 128, 0, 1024),
                           g["x2_hbm"][ch * 128:(ch + 1) * 128, :],
                           waits=[("ve", st[("x3v", ch - 1)])]
                           if ("x3v", ch - 1) in st else [], slot="xrl")
            for dN in range(2):
                for hb in range(IT):
                    w = []
                    if hb == 0:
                        w = [("ve", vqmf)] if ch == 0 and dN == 0 else []
                        if ch == 0:
                            w += dqd
                        if ("x3st", ch - 1, dN) in st:
                            w.append(st[("x3st", ch - 1, dN)])
                        if ("ev", ch - 1, dN) in st:
                            w.append(("ve", st[("ev", ch - 1, dN)]))
                    self.op("tensor", lambda e, hb=hb, ch=ch, dN=dN: e.matmul(
                        self.ps[dN][:, 0:512],
                        self.mband(hb)[:, ch * 128:(ch + 1) * 128],
                        qdt(hb)[:, dN * 512:(dN + 1) * 512],
                        start=(hb == 0), stop=(hb == IT - 1)),
                        waits=w, sem="pe" if hb == IT - 1 else None)
                pe = self.now("pe")
                xo = (self.fv("tmp0", 0, 128, 0, 512) if dN == 0
                      else self.fv("tmp1", 0, 128, 0, 512))
                self.op("vector", lambda e, dN=dN, xo=xo: e.tensor_tensor(
                    xo, self.ps[dN][:, 0:512],
                    self.fv("coefrep", 0, 128, dN * 512, 512), ALU.mult),
                    waits=[("pe", pe), dcf] + ([st[("x3st", ch - 1, dN)]]
                                               if ("x3st", ch - 1, dN) in st
                                               else []), sem="ve")
                self.op("vector", lambda e, dN=dN, xo=xo: e.tensor_add(
                    xo, xo, self.fv("xrl", 0, 128, dN * 512, 512)),
                    waits=[dxr], sem="ve")
                st[("ev", ch, dN)] = self.now("ve")
                st[("x3v", ch)] = self.now("ve")
                st[("x3st", ch, dN)] = self.dma(
                    "sync", self.out_x3[ch * 128:(ch + 1) * 128,
                                        dN * 512:(dN + 1) * 512], xo,
                    waits=[("ve", self.now("ve"))], slot="out%d" % dN)

    def emit_all(self):
        phases = [("p0", self.p0_consts, 0), ("p2a", self.p2a_mod_send, 0),
                  ("p1s", self.p1_ln1, 1), ("p2b", self.p2b_mod_recv, 1),
                  ("p2c", self.p2c_quant_weights, 0),
                  ("p1", self.p1_affine, 1), ("p4", self.p4_mixer_in, 1),
                  ("p5a", self.p5a_smalls, 0), ("p5b", self.p5b_z, 1),
                  ("p5c", self.p5c_loopA, 1), ("p5pre", self.p5e_pre, 0),
                  ("p5d", self.p5d_hin, 1), ("p5", self.p5e_loopB, 1),
                  ("p7", self.p7_ln2, 1), ("p8a", self.p8a_gateup, 1),
                  ("p8b", self.p8b_down, 1)]
        for name, fn, bar in phases:
            fn()
            if bar:
                self.barrier()
            if self.upto == name:
                self.final_wait()
                return
        if self.upto in ("all",):
            self.final_wait()
            return
        raise NotImplementedError("later phases pending")

    def final_wait(self):
        waits = [(sname, self.count[sname]) for sname in self.semnames
                 if sname.startswith("D_") and self.count[sname] > 0]
        waits += [("cc", self.now("cc"))]
        self.emit("sync", lambda e: e.nop(), waits, ())


def make_program(dbg=(), upto="all"):
    b = Build(dbg, upto)
    b.emit_all()
    b.materialize()
    b.stack.close()
    return b


def kernel(**inputs):
    maps = _stage(inputs)
    b = make_program()
    res = run_bass_kernel_spmd(b.nc, maps, core_ids=list(range(NCORE)))
    out = np.empty((B, L, D), np.float32)
    for k in range(NCORE):
        bb, q = divmod(k, 4)
        out[bb, q * T:(q + 1) * T] = np.asarray(res.results[k]["out_x3"])
    return out


# version 07
# speedup vs baseline: 1.0143x; 1.0143x over previous
import contextlib
import numpy as np
import ml_dtypes
import concourse.bass as bass
import concourse.mybir as mybir
from concourse.bass_utils import run_bass_kernel_spmd

F32 = mybir.dt.float32
BF16 = mybir.dt.bfloat16
AF = mybir.ActivationFunctionType
ALU = mybir.AluOpType
AX = mybir.AxisListType

NCORE = 8
B, L, D = 2, 8192, 1024
E, S, CONV = 2048, 16, 4
I = 2816
T = 2048
TH = T + 3
C = 128
NCH = T // C
DT = D // 128
ET = E // 128
TT = T // 128
IT = I // 128
MAGIC = 12582912.0
IQ = I // 8
EPS = 1e-5
H1S = 2052


def _stage(inp):
    x = np.asarray(inp["L0_attn_x"])
    w_in = np.asarray(inp["L0_attn_in_proj_w"])
    conv_w = np.asarray(inp["L0_attn_conv_w"]).reshape(E, CONV)
    conv_b = np.asarray(inp["L0_attn_conv_b"]).reshape(1, E)
    xpw = np.asarray(inp["L0_attn_x_proj_w"])
    alog = np.asarray(inp["L0_attn_A_log"])
    dpar = np.asarray(inp["L0_attn_D_param"]).reshape(1, E)
    w_out = np.asarray(inp["L0_attn_out_proj_w"])
    clip_x = np.asarray(inp["L0_ffn_clip_x"]).reshape(1, 1)
    clip_dn = np.asarray(inp["L0_ffn_clip_down"]).reshape(1, 1)
    w_g = np.asarray(inp["L0_ffn_w_gate"])
    w_u = np.asarray(inp["L0_ffn_w_up"])
    w_d = np.asarray(inp["L0_ffn_w_down"])
    cc = np.asarray(inp["L0_c"])
    wmod = np.asarray(inp["L0_wmod"])

    co = np.ascontiguousarray
    wxc_tiled = co(w_in[:E].T.reshape(8, 128, 16, 128).transpose(2, 0, 1, 3))
    wz_t = co(w_in[E:].T.reshape(8, 128, E))
    xpw_t = co(xpw.T.reshape(16, 128, 33))
    wout_t = co(w_out.T.reshape(16, 128, D))
    wgT = w_g.T
    wuT = w_u.T
    wdT = w_d.T

    maps = []
    for k in range(NCORE):
        b, q = divmod(k, 4)
        lo = q * T
        xh = x[b, 0:3] if q == 0 else x[b, lo - 3:lo]
        x_tm = co(np.concatenate([xh, x[b, lo:lo + T]], axis=0))
        pm = np.zeros((1, 8), np.float32)
        om = np.zeros((8, 8), np.float32)
        for j in range(4 * b, k):
            pm[0, j] = 1.0
            for u in range(j + 1, k):
                om[u, j] = 1.0
        maps.append(dict(
            x_tm=x_tm.astype(np.float32),
            wxc_t=wxc_tiled.astype(np.float32),
            wz_t=wz_t.astype(np.float32),
            xpw_t=xpw_t.astype(np.float32),
            wout_t=wout_t.astype(np.float32),
            conv_w=co(conv_w).astype(np.float32),
            conv_b=co(conv_b).astype(np.float32),
            dparam=co(dpar).astype(np.float32),
            alog_row=co(alog[0:1, :]).astype(np.float32),
            clip_x=clip_x.astype(np.float32),
            clip_dn=clip_dn.astype(np.float32),
            wmod_sh=co(wmod[:, k * 768:(k + 1) * 768]).astype(np.float32),
            cT=co(cc.T).astype(np.float32),
            wg_nat=co(w_g[k * IQ:(k + 1) * IQ]).astype(np.float32),
            wu_nat=co(w_u[k * IQ:(k + 1) * IQ]).astype(np.float32),
            wd_nat=co(w_d[k * 128:(k + 1) * 128]).astype(np.float32),
            wgT_sh=co(wgT[k * 128:(k + 1) * 128]).astype(np.float32),
            wuT_sh=co(wuT[k * 128:(k + 1) * 128]).astype(np.float32),
            wdT_sh=co(wdT[:, k * 128:(k + 1) * 128]).astype(np.float32),
            halo=np.array([[0.0 if q == 0 else 1.0]], np.float32),
            bsel=np.array([[float(b)]], np.float32),
            pmask=pm,
            ordmask_t=om,
        ))
    return maps


class Build:
    def __init__(self, dbg=(), upto="all", dbg_scr=False):
        self.upto = upto
        self.dbg_scr = dbg_scr
        self.nc = nc = bass.Bass("TRN2", target_bir_lowering=False)
        self.events = []
        self.count = {}
        self.dbg = dict(dbg)
        self.stack = contextlib.ExitStack()

        g = self.g = {}
        def din(name, shape):
            g[name] = nc.dram_tensor(name, list(shape), F32, kind="ExternalInput").ap()
        for name, shape in [
            ("x_tm", (TH, D)), ("wxc_t", (16, 8, 128, 128)), ("wz_t", (8, 128, E)),
            ("xpw_t", (16, 128, 33)), ("wout_t", (16, 128, D)), ("conv_w", (E, CONV)),
            ("conv_b", (1, E)), ("dparam", (1, E)), ("alog_row", (1, S)),
            ("clip_x", (1, 1)), ("clip_dn", (1, 1)), ("wmod_sh", (D, 768)),
            ("cT", (D, 2)), ("wg_nat", (IQ, D)), ("wu_nat", (IQ, D)),
            ("wd_nat", (128, I)), ("wgT_sh", (128, I)), ("wuT_sh", (128, I)),
            ("wdT_sh", (I, 128)), ("halo", (1, 1)), ("bsel", (1, 1)),
            ("pmask", (1, 8)), ("ordmask_t", (8, 8)),
        ]:
            din(name, shape)
        self.out_x3 = nc.dram_tensor("out_x3", [T, D], F32, kind="ExternalOutput").ap()
        self.dbgo = {}
        for name, shape in self.dbg.items():
            dt = BF16 if name.endswith("_bf") else F32
            self.dbgo[name] = nc.dram_tensor("dbg_" + name, list(shape), dt,
                                             kind="ExternalOutput").ap()

        def scr(name, shape, dt=F32):
            kw = ({"kind": "ExternalOutput"}
                  if self.dbg_scr and not name.startswith("ag") else {})
            g[name] = nc.dram_tensor(name, list(shape), dt, **kw).ap()
        scr("x2_hbm", (T, D))
        scr("sz_hbm", (NCH, 128, 2048), BF16)
        scr("h2_hbm", (TT, 8, 128, 128))
        scr("xq_hbm", (8, 128, T))
        scr("rows_hbm", (24, 4096))
        scr("krow_hbm", (NCH, 2, 2048))
        scr("ag0_in", (1, 2368))
        scr("ag0_out", (NCORE, 2368))
        NQ = 128 * I * 2 + I * 128
        scr("agq_in", (1, NQ), BF16)
        scr("agq_out", (NCORE, NQ), BF16)
        scr("ag1_in", (1, S * E + 16))
        scr("ag1_out", (NCORE, S * E + 16))
        scr("ag2_in", (1, 1))
        scr("ag2_out", (1, 1))
        scr("ag3_in", (1, 1))
        scr("ag3_out", (1, 1))

        g["ident"] = nc.inline_tensor(np.eye(128, dtype=ml_dtypes.bfloat16),
                                      name="identbf").ap()
        g["identf"] = nc.inline_tensor(np.eye(128, dtype=np.float32),
                                       name="identf32").ap()
        g["tril"] = nc.inline_tensor(
            np.ascontiguousarray(np.triu(np.ones((128, 128), np.float32))),
            name="trilf").ap()

        self.semnames = ["dls", "dlg", "dlc", "cc", "pe", "ve", "se", "bar",
                         "pe2", "ve2", "se2"]
        for s in self.semnames:
            self.count[s] = 0
        self.sems = {}

        self.NW = 52736
        self.sb = self.stack.enter_context(nc.sbuf_tensor("arena", [128, self.NW], F32))
        self.sid = self.stack.enter_context(nc.sbuf_tensor("sident", [128, 128], BF16))
        self.sidf = self.stack.enter_context(nc.sbuf_tensor("sidentf", [128, 128], F32))
        self.stril = self.stack.enter_context(nc.sbuf_tensor("stril", [128, 128], F32))
        self.ps = [self.stack.enter_context(nc.psum_tensor(f"ps{i}", [128, 512], F32))
                   for i in range(8)]

        self.off = {}
        self._size = {}
        self.cur = 0
        self._layout()

    def _alloc(self, name, nwords):
        self._size[name] = nwords
        self.off[name] = self.cur
        self.cur += (nwords + 1) & ~1
        assert self.cur <= self.NW, f"arena overflow at {name}: {self.cur}"

    def _alias(self, name, base, word_off, nwords):
        self._size[name] = nwords
        assert word_off + nwords <= self._size[base], (name, base)
        self.off[name] = self.off[base] + word_off

    def _layout(self):
        A, AL = self._alloc, self._alias
        A("mod_cols", 96)
        A("modx", 32)
        A("arow", 16); A("acol", 2); A("arep", 16)
        A("scl_g", 22); A("scl_u", 22); A("coef_dn", 8)
        A("sc", 64)
        A("convw", 80)
        A("tmp0", 1032); A("tmp1", 1032); A("cmb", 128)

        A("R1", 7168)
        AL("modw", "R1", 0, 6144)
        AL("wqstage", "R1", 3074, 1408)
        AL("wqrep", "R1", 4484, 1408)
        AL("wqout", "R1", 5894, 704)
        AL("amaxst", "R1", 3074, 2816)
        AL("xc0", "R1", 0, 2052)
        AL("xcin", "R1", 2054, 2052)
        AL("xcsl", "R1", 4108, 1028)
        AL("wrot", "R1", 5138, 1088)
        AL("repA", "R1", 0, 2048)
        AL("repC", "R1", 2048, 1024)
        AL("e3t", "R1", 3072, 2048)
        AL("e3u", "R1", 5120, 2048)

        A("R2", 4192)
        AL("xp", "R2", 0, 2048)
        AL("r2row", "R2", 0, 2816)
        AL("comb", "R2", 2050, 2080)
        AL("convw2", "R2", 2050, 80)

        A("R3", 9600)
        AL("xrot", "R3", 0, 2064)
        AL("xhalo", "R3", 2066, 1024)
        AL("sums", "R3", 3092, 40)
        AL("ab1", "R3", 3134, 40)
        AL("xaff", "R3", 3176, 1028)
        AL("xpTc", "R3", 0, 548)
        AL("hsn0", "R3", 550, 1024)
        AL("hsn1", "R3", 1576, 1024)
        AL("kbslot", "R3", 2602, 128)
        AL("gslot", "R3", 2732, 260)
        AL("rch", "R3", 2994, 128)
        AL("rows16", "R3", 3124, 96)
        AL("ubufs", "R3", 3222, 128)
        AL("hstate", "R3", 3352, 2048)
        AL("y2r", "R3", 5402, 1024)
        AL("xcin2", "R3", 5402, 2052)
        AL("xrl", "R3", 6428, 1024)
        AL("y2f", "R3", 7454, 1024)
        AL("hinbf", "R3", 8480, 1024)
        AL("ab2", "R3", 9506, 40)
        AL("x2rot", "R3", 0, 2064)
        AL("x2aff", "R3", 3176, 1028)

        A("R4", 8212)
        AL("h1T", "R4", 0, 8212)
        AL("wot", "R4", 0, 8192)
        AL("qx", "R4", 0, 8192)

        A("R6", 4104)
        AL("wzrot", "R6", 0, 4100)
        AL("szrot", "R6", 0, 2052)
        AL("garepf", "R6", 2054, 1024)
        AL("garep", "R6", 3080, 512)
        AL("drep", "R6", 2054, 1024)
        AL("qwrot", "R6", 0, 4100)
        AL("dnbuf", "R6", 516, 2048)
        AL("xpwb", "R6", 2054, 288)

        A("xcsTM", 16384)
        AL("h2TM", "xcsTM", 0, 16384)
        AL("cbrep", "R6", 0, 1024)
        AL("sbrep", "R6", 1026, 1024)
        AL("coefrep", "R6", 3072, 1024)

    def fv(self, name, p0=0, pn=128, o=0, n=None):
        off, sz = self.off[name], self._size[name]
        n = (sz - o) if n is None else n
        assert o + n <= sz, ("fv overflow", name, o, n, sz)
        return self.sb[p0:p0 + pn, off + o:off + o + n]

    def bvv(self, name, p0=0, pn=128, o=0, n=None):
        off, sz = self.off[name], self._size[name]
        nb = 2 * sz
        n = (nb - o) if n is None else n
        base = self.sb[p0:p0 + pn, off:off + sz].bitcast(BF16)
        return base[:, o:o + n]

    def now(self, s):
        return self.count[s]

    def emit(self, eng, fn, waits=(), incs=()):
        for s, n in incs:
            self.count[s] += n
        self.events.append((eng, fn, tuple(waits), tuple(incs)))

    def dma(self, eng, out, in_, waits=(), slow=False, slot="misc"):
        sem = "D_" + slot
        if sem not in self.count:
            self.count[sem] = 0
            self.semnames.append(sem)
        def fn(e, o=out, i=in_, sl=slow):
            return e.dma_start(out=o, in_=i, allow_slow_non_contiguous=sl)
        self.emit(eng, fn, waits, [(sem, 16)])
        return (sem, self.now(sem))

    def cdma(self, eng, out, in_, waits=(), slow=False, slot="chain"):
        if not hasattr(self, "_chain"):
            self._chain = {}
        prev = self._chain.get(slot)
        w = list(waits) + ([prev] if prev else [])
        t = self.dma(eng, out, in_, w, slow, slot)
        self._chain[slot] = t
        return t

    def op(self, eng, fn, waits=(), sem=None, drain=True):
        incs = [(sem, 1)] if sem else []
        self.emit(eng, fn, waits, incs)
        if drain and eng in ("vector", "scalar"):
            self.emit(eng, lambda e: e.drain(), (), ())
        return self.now(sem) if sem else None

    def barrier(self, engines=("tensor", "vector", "scalar", "gpsimd", "sync")):
        for e in engines:
            self.emit(e, lambda eng: eng.nop(), (), [("bar", 1)])
        tgt = self.now("bar")
        for e in engines:
            self.emit(e, lambda eng: eng.nop(), [("bar", tgt)], ())
        return tgt

    def materialize(self):
        nc = self.nc
        for s in self.semnames:
            self.sems[s] = self.stack.enter_context(nc.semaphore(s))
        sems = self.sems
        with nc.Block() as block:
            def run(engname, eng):
                for e, fn, waits, incs in self.events:
                    if e != engname:
                        continue
                    for s, v in waits:
                        eng.wait_ge(sems[s], v)
                    ins = fn(eng)
                    for s, n in incs:
                        ins = ins.then_inc(sems[s], n)

            @block.tensor
            def _(eng):
                run("tensor", eng)

            @block.vector
            def _(eng):
                run("vector", eng)

            @block.scalar
            def _(eng):
                run("scalar", eng)

            @block.gpsimd
            def _(eng):
                run("gpsimd", eng)

            @block.sync
            def _(eng):
                run("sync", eng)

    def p0_consts(self):
        g = self.g
        L = self.ld = {}
        L["sid"] = self.cdma("sync", self.sid[:, :], g["ident"][:, :], slot="cs")
        L["tril"] = self.cdma("sync", self.stril[:, :], g["tril"][:, :], slot="cs")
        L["sidf"] = self.cdma("sync", self.sidf[:, :], g["identf"][:, :], slot="cs")
        arow = self.fv("arow", 0, 1, 0, 16)
        d0 = self.cdma("sync", arow, g["alog_row"][:, :], slot="cs")
        s0 = self.op("scalar", lambda e: e.activation(arow, arow, AF.Exp,
                                                      bias=0.0, scale=1.0),
                     waits=[d0], sem="se")
        v0 = self.op("vector", lambda e: e.tensor_scalar_mul(arow, arow, -1.0),
                     waits=[("se", s0)], sem="ve")
        d1 = self.cdma("sync", g["rows_hbm"][0:1, 0:16], arow, waits=[("ve", v0)],
                      slot="cs")
        L["acol"] = self.cdma("sync", self.fv("acol", 0, 16, 0, 1),
                             g["rows_hbm"][0, 0:16].rearrange("(a b) -> a b", b=1),
                             waits=[d1], slow=True, slot="cs")
        L["acol2"] = self.cdma("sync", self.fv("acol", 16, 16, 0, 1),
                              g["rows_hbm"][0, 0:16].rearrange("(a b) -> a b", b=1),
                              waits=[d1], slow=True, slot="cs")
        L["arep"] = self.cdma("sync", self.fv("arep"),
                             g["rows_hbm"][0:1, 0:16].partition_broadcast(128),
                             waits=[d1], slot="cs")
        L["convw"] = self.cdma("sync", self.fv("convw", 0, 128, 0, 64)
                              .rearrange("p (et c) -> p et c", c=4),
                              g["conv_w"].rearrange("(et p) c -> p et c", p=128),
                              slot="cs")
        L["convb"] = self.cdma("sync", self.fv("convw", 0, 128, 64, 16),
                              g["conv_b"].rearrange("a (et p) -> p (a et)", p=128),
                              slow=True, slot="cs")
        L["xpwb"] = self.dma("gpsimd", self.bvv("xpwb", 0, 128, 0, 16 * 34)
                             .rearrange("p (es m) -> p es m", m=34)[:, :, 0:33],
                             g["xpw_t"].transpose([1, 0, 2]), slot="xpwb")
        L["clips"] = self.cdma("sync", self.fv("sc", 0, 1, 0, 1), g["clip_x"][:, :],
                              slot="cs")
        self.cdma("sync", self.fv("sc", 0, 1, 2, 1), g["clip_dn"][:, :], slot="cs")
        L["halo"] = self.cdma("sync", self.fv("sc", 0, 128, 4, 1),
                             g["halo"][0:1, :].partition_broadcast(128), slot="cs")
        L["bsel"] = self.cdma("sync", self.fv("sc", 0, 128, 6, 1),
                             g["bsel"][0:1, :].partition_broadcast(128), slot="cs")

    def p2a_mod_send(self):
        g = self.g
        ct = self.fv("sc", 0, 128, 8, 16)
        dl = self.cdma("sync", ct.rearrange("p (dt b) -> p dt b", b=2),
                      g["cT"].rearrange("(dt p) b -> p dt b", p=128), slot="cs")
        se0 = self.op("scalar", lambda e: e.activation(ct, ct, AF.Silu,
                                                       bias=0.0, scale=1.0),
                      waits=[dl], sem="se")
        dmw = self.dma("sync", self.fv("modw", 0, 128, 0, 8 * 768)
                       .rearrange("p (dt j) -> p dt j", j=768),
                       g["wmod_sh"].rearrange("(dt p) j -> p dt j", p=128),
                       slot="modw")
        modp = self.fv("sc", 0, 128, 40, 12)
        modw = self.fv("modw", 0, 128, 0, 8 * 768)
        for ccj in range(6):
            for dt in range(8):
                w = []
                if dt == 0:
                    w = [dmw, ("se", se0), ("ve", self.now("ve"))]
                self.op("tensor", lambda e, ccj=ccj, dt=dt: e.matmul(
                    self.ps[0][:, 0:2],
                    modw[:, dt * 768 + ccj * 128: dt * 768 + (ccj + 1) * 128],
                    ct[:, dt * 2: dt * 2 + 2],
                    start=(dt == 0), stop=(dt == 7)),
                    waits=w, sem="pe" if dt == 7 else None)
            pe = self.now("pe")
            self.op("vector", lambda e, ccj=ccj: e.tensor_copy(
                modp.rearrange("p (b c) -> p b c", c=6)[:, :, ccj],
                self.ps[0][:, 0:2]),
                waits=[("pe", pe)], sem="ve")
        vmod = self.now("ve")
        self.pe_mod = self.now("pe")
        dsent = [self.dma("sync", g["ag0_in"][0, 0:1536]
                          .rearrange("(p bc) -> p bc", bc=12), modp,
                          waits=[("ve", vmod)], slot="ag0m")]
        vprev = None
        prevstore = None
        first_amax = True
        for name, nrows, ncols, aoff in (("wg_nat", IQ, D, 1536),
                                         ("wu_nat", IQ, D, 1888),
                                         ("wd_nat", 128, I, 2240)):
            r0 = 0
            while r0 < nrows:
                rn = min(128, nrows - r0)
                stage = self.fv("amaxst", 0, rn, 0, ncols)
                d = self.dma("sync", stage, g[name][r0:r0 + rn, :],
                             waits=([("ve", vprev)] if vprev else
                                    [("pe", self.pe_mod)] if first_amax else []),
                             slot="amaxst")
                first_amax = False
                col = self.fv("sc", 0, rn, 24, 1)
                vprev = self.op("vector", lambda e, st=stage, c=col, rn=rn:
                                e.reduce_max(c, st, AX.X, apply_absolute_value=True),
                                waits=[d] + ([prevstore] if prevstore else []),
                                sem="ve")
                prevstore = self.dma("sync",
                                     g["ag0_in"][0, aoff + r0: aoff + r0 + rn]
                                     .rearrange("(p a) -> p a", a=1), col,
                                     waits=[("ve", vprev)], slot="ag0c")
                dsent.append(prevstore)
                r0 += rn
        agw = [("D_ag0m", self.now("D_ag0m")), ("D_ag0c", self.now("D_ag0c"))]
        self.op("gpsimd", lambda e: e.collective_compute(
            "AllGather", ALU.bypass, replica_groups=[list(range(NCORE))],
            ins=[self.g["ag0_in"][:, :]], outs=[self.g["ag0_out"][:, :]]),
            waits=agw, sem="cc")
        self.cc_ag0 = self.now("cc")

    def p2b_mod_recv(self):
        g = self.g
        d = self.dma("sync",
                     self.fv("mod_cols").rearrange("p (b co cc) -> p b co cc",
                                                   b=2, co=8),
                     g["ag0_out"][:, 0:1536]
                     .rearrange("co (p b cc) -> p b co cc", b=2, cc=6),
                     waits=[("cc", self.cc_ag0)], slow=True, slot="modc")
        mc = self.fv("mod_cols")
        bs = self.fv("sc", 0, 128, 6, 1)
        self.op("vector", lambda e: e.tensor_sub(self.fv("tmp0", 0, 128, 0, 48),
                                                 mc[:, 48:96], mc[:, 0:48]),
                waits=[d, self.ld["bsel"]], sem="ve")
        self.op("vector", lambda e: e.scalar_tensor_tensor(
            mc[:, 0:48], self.fv("tmp0", 0, 128, 0, 48), bs, mc[:, 0:48],
            ALU.mult, ALU.add), sem="ve")
        mx = self.fv("modx")
        self.op("vector", lambda e: e.tensor_scalar_add(mx[:, 0:8], mc[:, 8:16], 1.0),
                sem="ve")
        self.op("vector", lambda e: e.tensor_scalar_add(mx[:, 8:16], mc[:, 32:40], 1.0),
                sem="ve")
        for src, dsto in ((16, 16), (40, 24)):
            t0 = self.fv("tmp0", 0, 128, 0, 8)
            t1 = self.fv("tmp0", 0, 128, 8, 8)
            t2 = self.fv("tmp0", 0, 128, 16, 8)
            self.op("vector", lambda e, s=src, a=t0: e.tensor_single_scalar(
                a, mc[:, s:s + 8], 0.0, ALU.is_ge), sem="ve")
            self.op("vector", lambda e, s=src, a=t1: e.tensor_scalar_max(
                a, mc[:, s:s + 8], 0.25), sem="ve")
            self.op("vector", lambda e, s=src, a=t2: e.tensor_scalar_min(
                a, mc[:, s:s + 8], -0.25), sem="ve")
            self.op("vector", lambda e, a=t1, bq=t2: e.tensor_sub(a, a, bq), sem="ve")
            self.op("vector", lambda e, a=t0, bq=t1: e.tensor_mul(a, a, bq), sem="ve")
            self.op("vector", lambda e, a=t0, bq=t2, do=dsto: e.tensor_add(
                mx[:, do:do + 8], a, bq), sem="ve")
        self.ve_mod = self.now("ve")

    def p2c_quant_weights(self):
        g = self.g
        dr = []
        dr.append(self.cdma("sync", g["rows_hbm"][4, 0:I]
                           .rearrange("(c j) -> c j", j=IQ),
                           g["ag0_out"][:, 1536:1536 + IQ],
                           waits=[("cc", self.cc_ag0)], slot="c4"))
        dr.append(self.cdma("sync", g["rows_hbm"][5, 0:I]
                           .rearrange("(c j) -> c j", j=IQ),
                           g["ag0_out"][:, 1888:1888 + IQ],
                           waits=[("cc", self.cc_ag0)], slot="c4"))
        dr.append(self.cdma("sync", g["rows_hbm"][6, 0:1024]
                           .rearrange("(c j) -> c j", j=128),
                           g["ag0_out"][:, 2240:2240 + 128],
                           waits=[("cc", self.cc_ag0)], slot="c4"))
        vprev = None
        prevrec = None
        rec_t = []
        for i, (src, dst, n) in enumerate(((4, 7, I), (5, 8, I), (6, 9, 1024))):
            row = self.fv("r2row", 0, 1, 0, n)
            d = self.dma("sync", row, g["rows_hbm"][src:src + 1, 0:n],
                         waits=[dr[i]] + ([("ve", vprev)] if vprev else []) +
                               ([prevrec] if prevrec else []),
                         slot="xprow")
            self.op("vector", lambda e, r=row: e.reciprocal(r, r),
                    waits=[d], sem="ve")
            vprev = self.op("vector", lambda e, r=row: e.tensor_scalar_mul(
                r, r, 127.0), sem="ve")
            prevrec = self.cdma("sync", g["rows_hbm"][dst:dst + 1, 0:n], row,
                               waits=[("ve", vprev)], slot="c4")
            rec_t.append(prevrec)
        HF = I // 2
        vq = None
        dq_out = []
        for wname, rrow, qoff in (("wgT_sh", 7, 0), ("wuT_sh", 8, 128 * I)):
            for hv in range(2):
                st = self.fv("wqstage", 0, 128, 0, HF)
                rp = self.fv("wqrep", 0, 128, 0, HF)
                qo = self.bvv("wqout", 0, 128, 0, HF)
                w = [("ve", vq)] if vq else []
                d1 = self.dma("sync", st, g[wname][:, hv * HF:(hv + 1) * HF],
                              waits=w, slot="wqstage")
                d2 = self.dma("sync", rp,
                              g["rows_hbm"][rrow:rrow + 1, hv * HF:(hv + 1) * HF]
                              .partition_broadcast(128),
                              waits=w + [rec_t[0 if rrow == 7 else 1]],
                              slot="wqrep")
                self.op("vector", lambda e, a=st, bq=rp: e.tensor_mul(a, a, bq),
                        waits=[d1, d2, ("D_agq", self.now("D_agq"))
                               if "D_agq" in self.count else d1], sem="ve")
                vq = self.op("vector", lambda e, a=st, q=qo: e.tensor_scalar(
                    q, a, MAGIC, MAGIC, ALU.add, ALU.subtract), sem="ve")
                dq_out.append(self.dma("sync",
                              g["agq_in"][0, qoff + hv * 11 * 16384:
                                          qoff + (hv + 1) * 11 * 16384]
                              .rearrange("(it p i) -> p it i", it=11, i=128),
                              qo.rearrange("p (it i) -> p it i", i=128),
                              waits=[("ve", vq)], slot="agq"))
        row = self.fv("xp", 0, 1, 0, 128)
        dd = self.dma("sync", row, g["ag0_in"][0:1, 2240:2368],
                      waits=[("ve", vq), prevrec], slot="xprow")
        self.op("vector", lambda e, r=row: e.reciprocal(r, r), waits=[dd], sem="ve")
        v = self.op("vector", lambda e, r=row: e.tensor_scalar_mul(r, r, 127.0),
                    sem="ve")
        d9 = self.cdma("sync", g["rows_hbm"][10:11, 0:128], row,
                      waits=[("ve", v)], slot="c4")
        rp = self.fv("wqrep", 0, 128, 0, 128)
        d2 = self.dma("sync", rp, g["rows_hbm"][10:11, 0:128].partition_broadcast(128),
                      waits=[d9, ("ve", vq)], slot="wqrep")
        for hv in range(2):
            st = self.fv("wqstage", 0, 128, 0, HF)
            d1 = self.dma("sync", st.rearrange("p (it h) -> p it h", h=128),
                          g["wdT_sh"][hv * HF:(hv + 1) * HF, :]
                          .rearrange("(it p) h -> p it h", p=128),
                          waits=[("ve", vq)], slot="wqstage")
            qo = self.bvv("wqout", 0, 128, 0, HF)
            self.op("vector", lambda e, a=st, bq=rp: e.tensor_tensor(
                a.rearrange("p (it h) -> p it h", h=128),
                a.rearrange("p (it h) -> p it h", h=128),
                bq.unsqueeze(1).broadcast_to([128, 11, 128]), ALU.mult),
                waits=[d1, d2, ("D_agq", self.now("D_agq"))], sem="ve")
            vq = self.op("vector", lambda e, a=st, q=qo: e.tensor_scalar(
                q, a, MAGIC, MAGIC, ALU.add, ALU.subtract), sem="ve")
            dq_out.append(self.dma("sync",
                          g["agq_in"][0, 2 * 128 * I + hv * 11 * 16384:
                                      2 * 128 * I + (hv + 1) * 11 * 16384]
                          .rearrange("(it p h) -> p it h", it=11, h=128),
                          qo.rearrange("p (it h) -> p it h", h=128),
                          waits=[("ve", vq)], slot="agq"))
        self.op("gpsimd", lambda e: e.collective_compute(
            "AllGather", ALU.bypass, replica_groups=[list(range(NCORE))],
            ins=[g["agq_in"][:, :]], outs=[g["agq_out"][:, :]]),
            waits=[("D_agq", self.now("D_agq"))], sem="cc")
        self.cc_agq = self.now("cc")

    def p1_ln1(self):
        g = self.g
        sums = self.fv("sums", 0, 128, 0, 17)
        sums2 = self.fv("sums", 0, 128, 18, 17)
        for i in range(TT):
            xt = self.fv("xrot", 0, 128, (i % 2) * 1032, 1024)
            w = []
            if i >= 2:
                w = [("ve", self.ve_x_used[i - 2]), ("se", self.se_x_used[i - 2])]
            d = self.dma("sync", xt, g["x_tm"][3 + i * 128: 3 + (i + 1) * 128, :],
                         waits=w, slot="xr%d" % (i % 2))
            buf = self.fv("tmp1", 0, 128, 0, 1024)
            if i == 0:
                self.se_x_used, self.ve_x_used, self.ve_sq = {}, {}, {}
            sqw = [d] + ([("ve", self.ve_sq[i - 1])] if i >= 1 else [])
            s = self.op("scalar", lambda e, x=xt, q=buf: e.activation(
                q, x, AF.Square, bias=0.0, scale=1.0), waits=sqw, sem="se")
            self.se_x_used[i] = s
            self.op("vector", lambda e, x=xt, i=i: e.reduce_sum(
                sums[:, i:i + 1], x, AX.X), waits=[d], sem="ve")
            self.ve_x_used[i] = self.now("ve")
            self.op("vector", lambda e, q=buf, i=i: e.reduce_sum(
                sums2[:, i:i + 1], q, AX.X), waits=[("se", s)], sem="ve")
            self.ve_sq[i] = self.now("ve")
        vz = self.op("vector", lambda e: e.memset(sums[:, 16:17], 0.0), sem="ve")
        vz = self.op("vector", lambda e: e.memset(sums2[:, 16:17], 0.0), sem="ve")
        xh = self.fv("xhalo", 0, 3, 0, 1024)
        dh = self.dma("sync", xh, g["x_tm"][0:3, :], slot="xhalo")
        sqh = self.fv("tmp1", 0, 3, 0, 1024)
        s = self.op("scalar", lambda e: e.activation(sqh, xh, AF.Square,
                                                     bias=0.0, scale=1.0),
                    waits=[dh, ("ve", self.now("ve"))], sem="se")
        self.op("vector", lambda e: e.reduce_sum(sums[0:3, 16:17], xh, AX.X),
                waits=[dh], sem="ve")
        self.op("vector", lambda e: e.reduce_sum(sums2[0:3, 16:17], sqh, AX.X),
                waits=[("se", s)], sem="ve")
        vstats = self.now("ve")
        m = self.fv("cmb", 0, 128, 0, 17)
        var = self.fv("cmb", 0, 128, 18, 17)
        mm = self.fv("cmb", 0, 128, 36, 17)
        var2 = self.fv("cmb", 0, 128, 54, 17)
        sd = self.fv("cmb", 0, 128, 72, 17)
        a1 = self.fv("ab1", 0, 128, 0, 17)
        b1 = self.fv("ab1", 0, 128, 18, 17)
        self.op("vector", lambda e: e.tensor_scalar_mul(m, sums, 1.0 / D),
                waits=[("ve", vstats)], sem="ve")
        self.op("vector", lambda e: e.tensor_scalar(var, sums2, 1.0 / D, EPS,
                                                    ALU.mult, ALU.add), sem="ve")
        self.op("vector", lambda e: e.tensor_mul(mm, m, m), sem="ve")
        self.op("vector", lambda e: e.tensor_sub(var2, var, mm), sem="ve")
        sv = self.op("scalar", lambda e: e.activation(sd, var2, AF.Sqrt,
                                                      bias=0.0, scale=1.0),
                     waits=[("ve", self.now("ve"))], sem="se")
        self.op("vector", lambda e: e.reciprocal(a1, sd),
                waits=[("se", sv)], sem="ve")
        self.op("vector", lambda e: e.tensor_mul(b1, m, a1), sem="ve")
        self.op("vector", lambda e: e.tensor_scalar_mul(b1, b1, -1.0), sem="ve")
        self.ve_ab1 = self.now("ve")

    def p1_affine(self):
        g = self.g
        a1 = self.fv("ab1", 0, 128, 0, 17)
        b1 = self.fv("ab1", 0, 128, 18, 17)
        ps7b = self.ps[7][:, :].bitcast(BF16)
        xaffb = self.bvv("xaff", 0, 128, 0, 2048)
        h1 = self.bvv("h1T", 0, 128, 0, 8 * H1S)
        dre = {}
        self.se_aff, self.pe_tp = {}, {}
        for i in range(TT + 1):
            if i < TT:
                xt = self.fv("xrot", 0, 128, (i % 2) * 1032, 1024)
                w = ([("ve", self.ve_ab1), ("se", self.now("se"))] if i < 2
                     else [("se", self.se_aff[i - 2])])
                dre[i] = self.dma("sync", xt,
                                  g["x_tm"][3 + i * 128: 3 + (i + 1) * 128, :],
                                  waits=w, slot="xr%d" % (i % 2))
                pn, colw = 128, 128
            else:
                xt = self.fv("xhalo", 0, 3, 0, 1024)
                pn, colw = 3, 3
            xafft = xaffb[0:pn, (i % 2) * 1024:(i % 2) * 1024 + 1024]
            sa_ = self.op("scalar", lambda e, x=xt, i=i, xa=xafft, pn=pn: e.activation(
                xa, x, AF.Identity,
                bias=b1[0:pn, i:i + 1], scale=a1[0:pn, i:i + 1]),
                waits=[("ve", self.ve_ab1)] + ([dre[i]] if i < TT else []) +
                      ([("pe", self.pe_tp[i - 2])] if i >= 2 else []),
                sem="se")
            self.se_aff[i] = sa_
            for dt in range(8):
                self.op("tensor", lambda e, dt=dt, pn=pn, colw=colw, xa=xafft: e.transpose(
                    ps7b[:, dt * 128: dt * 128 + colw],
                    xa[0:pn, dt * 128: dt * 128 + 128],
                    self.sid[0:pn, 0:pn]),
                    waits=(([("se", sa_), ("se2", self.now("se2"))] +
                            ([self.ld["sid"]] if i == 0 else [])) if dt == 0 else []),
                    sem="pe" if dt == 7 else None)
            pe = self.now("pe")
            self.pe_tp[i] = pe
            for dt in range(8):
                col0 = (3 + i * 128) if i < TT else 0
                self.op("scalar", lambda e, dt=dt, col0=col0, colw=colw: e.activation(
                    h1[:, dt * H1S + col0: dt * H1S + col0 + colw],
                    ps7b[:, dt * 128: dt * 128 + colw], AF.Identity,
                    bias=self.fv("mod_cols", 0, 128, 0 + dt, 1),
                    scale=self.fv("modx", 0, 128, 0 + dt, 1)),
                    waits=([("pe", pe), ("ve", self.ve_mod)] if dt == 0 else []),
                    sem="se2")
        hl = self.fv("sc", 0, 128, 4, 1)
        vh = self.op("vector", lambda e: e.tensor_scalar(
            h1.rearrange("p (dt t) -> p dt t", t=H1S)[:, :, 0:3],
            h1.rearrange("p (dt t) -> p dt t", t=H1S)[:, :, 0:3],
            hl, None, ALU.mult),
            waits=[("se2", self.now("se2")), self.ld["halo"]], sem="ve")
        self.ve_h1T = self.now("ve")
        self.se2_h1T = self.now("se2")
        if "h1T_bf" in self.dbgo:
            self.dma("sync", self.dbgo["h1T_bf"][:, :]
                     .rearrange("(dt p) t -> p dt t", p=128),
                     h1.rearrange("p (dt t) -> p dt t", t=H1S)[:, :, 0:TH],
                     waits=[("ve", self.ve_h1T), ("se2", self.now("se2"))],
                     slot="dbg")

    def p4_mixer_in(self):
        g = self.g
        h1 = self.bvv("h1T", 0, 128, 0, 8 * H1S)
        xcins = [self.fv("xcin", 0, 128, 0, 2051),
                 self.fv("xcin2", 0, 128, 0, 2051)]
        xc0 = self.fv("xc0", 0, 128, 0, 2048)
        xcsl = self.bvv("xcsl", 0, 128, 0, 2048)
        xpwb = self.bvv("xpwb", 0, 128, 0, 16 * 34)
        ps6b = self.ps[6][:, :].bitcast(BF16)
        chunks = [(0, 512), (512, 512), (1024, 512), (1536, 512), (2048, 3)]
        wx_used = {}
        evac_ve = {}
        se_tap0 = None
        se_silu = {}
        ve_conv = {}
        pe_cons = {}
        se2_tp = None
        for es in range(16):
            half = es % 2
            xcin = xcins[half]
            wr = self.bvv("wrot", 0, 128, half * 1088, 1024)
            w = [("pe", wx_used[es - 2])] if es >= 2 else []
            dwx = self.dma("gpsimd", wr.rearrange("p (K j) -> p K j", j=128),
                           g["wxc_t"][es].transpose([1, 0, 2]),
                           waits=w, slot="wx%d" % half)
            for ci, (c0, n) in enumerate(chunks):
                bank = ci % 2
                for K in range(8):
                    wts = []
                    if K == 0:
                        wts = [dwx]
                        if (es, ci) != (0, 0) and bank in evac_ve:
                            wts.append(("se", evac_ve[bank]))
                    self.op("tensor", lambda e, K=K, c0=c0, n=n, bank=bank, wr=wr:
                            e.matmul(self.ps[bank][:, 0:n],
                                     wr[:, K * 128:(K + 1) * 128],
                                     h1[:, K * H1S + c0: K * H1S + c0 + n],
                                     start=(K == 0), stop=(K == 7)),
                            waits=wts, sem="pe" if K == 7 else None)
                pe = self.now("pe")
                evw = [("pe", pe)]
                if ci == 0 and es >= 2:
                    evw.append(("ve", ve_conv[es - 2]))
                self.op("scalar", lambda e, c0=c0, n=n, bank=bank, xcin=xcin:
                        e.activation(xcin[:, c0:c0 + n], self.ps[bank][:, 0:n],
                                     AF.Identity, bias=0.0, scale=1.0),
                        waits=evw, sem="se")
                evac_ve[bank] = self.now("se")
            wx_used[es] = self.now("pe")
            cw = self.fv("convw", 0, 128, 0, 64)
            cb = self.fv("convw", 0, 128, 64, 16)
            t0w = [("se", self.now("se"))]
            if es == 0:
                t0w += [self.ld["convw"], self.ld["convb"]]
            se_tap0 = self.op("scalar", lambda e, es=es, xcin=xcin: e.activation(
                xc0, xcin[:, 0:2048], AF.Identity,
                bias=cb[:, es:es + 1], scale=cw[:, es * 4: es * 4 + 1]),
                waits=t0w, sem="se")
            for k in (1, 2, 3):
                tw = [("se", se_tap0)] if k == 1 else []
                self.op("vector", lambda e, k=k, es=es, xcin=xcin:
                        e.scalar_tensor_tensor(
                    xc0, xcin[:, k:2048 + k], cw[:, es * 4 + k: es * 4 + k + 1],
                    xc0, ALU.mult, ALU.add), waits=tw, sem="ve")
            ve_conv[es] = self.now("ve")
            vconv = self.now("ve")
            sw = [("ve", vconv)]
            if es > 0:
                sw.append(("pe", pe_cons[es - 1]))
            se_silu[es] = self.op("scalar", lambda e: e.activation(
                xcsl, xc0, AF.Silu, bias=0.0, scale=1.0), waits=sw, sem="se")
            for n4 in range(4):
                self.op("tensor", lambda e, n4=n4, es=es: e.matmul(
                    self.ps[2 + n4][0:33, 0:512],
                    xpwb[:, es * 34: es * 34 + 33],
                    xcsl[:, n4 * 512:(n4 + 1) * 512],
                    start=(es == 0), stop=(es == 15)),
                    waits=([("se", se_silu[es])] +
                           ([self.ld["xpwb"]] if es == 0 else []))
                    if n4 == 0 else [], sem=None)
            xcs = self.bvv("xcsTM", 0, 128, 0, 16 * 2048)
            for b2 in range(2):
                for ti in range(8):
                    tt = b2 * 8 + ti
                    wts = []
                    if ti == 0:
                        wts = [("se", se_silu[es])]
                        if se2_tp is not None:
                            wts.append(("se2", se2_tp))
                    self.op("tensor", lambda e, tt=tt, ti=ti: e.transpose(
                        ps6b[:, ti * 128:(ti + 1) * 128],
                        xcsl[:, tt * 128:(tt + 1) * 128],
                        self.sid[:, :]),
                        waits=wts, sem="pe" if ti == 7 else None)
                pe = self.now("pe")
                self.op("scalar", lambda e, b2=b2, es=es: e.activation(
                    xcs.rearrange("p (ch t) -> p ch t", t=2048)
                    [:, b2 * 8:(b2 + 1) * 8, es * 128:(es + 1) * 128],
                    ps6b[:, 0:1024].rearrange("p (ti t) -> p ti t", t=128),
                    AF.Identity, bias=0.0, scale=1.0),
                    waits=[("pe", pe)], sem="se2")
                se2_tp = self.now("se2")
            pe_cons[es] = self.now("pe")
        pe = self.now("pe")
        for n4 in range(4):
            self.op("vector", lambda e, n4=n4: e.tensor_copy(
                self.fv("xp", 0, 33, n4 * 512, 512), self.ps[2 + n4][0:33, 0:512]),
                waits=[("pe", pe)] if n4 == 0 else [], sem="ve")
        self.ve_xp = self.now("ve")
        self.se2_xcs = self.now("se2")
        if "xp" in self.dbgo:
            self.dma("sync", self.dbgo["xp"][:, :], self.fv("xp", 0, 33, 0, 2048),
                     waits=[("ve", self.ve_xp)], slot="dbg")
        if "xcs_bf" in self.dbgo:
            self.dma("sync", self.dbgo["xcs_bf"][:, :],
                     self.bvv("xcsTM", 0, 128, 0, 16 * 2048),
                     waits=[("se2", self.se2_xcs)], slot="dbg2")

    def p5a_smalls(self):
        g = self.g
        comb = self.fv("comb", 0, 34, 0, 2048)
        d1 = self.cdma("sync", self.fv("comb", 0, 16, 0, 2048),
                      self.fv("xp", 17, 16, 0, 2048),
                      waits=[("ve", self.ve_xp)], slot="c2")
        d2 = self.cdma("sync", self.fv("comb", 16, 16, 0, 2048),
                      self.fv("xp", 1, 16, 0, 2048),
                      waits=[("ve", self.ve_xp)], slot="c2")
        x0 = self.fv("e3t", 64, 1, 0, 2048)
        dx0 = self.cdma("sync", x0, self.fv("xp", 0, 1, 0, 2048),
                       waits=[("ve", self.ve_xp)], slot="c2")
        sc1 = self.fv("e3u", 64, 1, 0, 2048)
        sc2 = self.fv("repA", 64, 1, 0, 2048)
        self.op("vector", lambda e: e.tensor_scalar(sc1, x0, -1.0, None,
                                                    ALU.mult),
                waits=[dx0], sem="ve")
        self.op("vector", lambda e: e.tensor_tensor(sc1, x0, sc1, ALU.min),
                sem="ve")
        sp = self.op("scalar", lambda e: e.activation(sc1, sc1, AF.Exp,
                                                      bias=0.0, scale=1.0),
                     waits=[("ve", self.now("ve"))], sem="se")
        sp = self.op("scalar", lambda e: e.activation(sc1, sc1, AF.Ln,
                                                      bias=1.0, scale=1.0), sem="se")
        self.op("vector", lambda e: e.tensor_scalar_max(sc2, x0, 0.0),
                waits=[("se", sp)], sem="ve")
        vdl = self.op("vector", lambda e: e.tensor_add(sc1, sc1, sc2), sem="ve")
        ddl = self.cdma("sync", self.fv("comb", 32, 1, 0, 2048), sc1,
                       waits=[("ve", vdl)], slot="c2")
        ones = self.fv("e3t", 64, 1, 0, 2048)
        self.op("vector", lambda e: e.memset(ones, 1.0), sem="ve")
        rgc = self.fv("repA", 64, 1, 0, 2048)
        vrg = self.op("vector", lambda e: e.tensor_tensor_scan(
            rgc, ones, sc1, 0.0, ALU.mult, ALU.add),
            waits=[("ve", self.now("ve"))], sem="ve")
        drg = self.cdma("sync", g["rows_hbm"][12:13, 0:2048], rgc,
                       waits=[("ve", vrg)], slot="c2")
        drc2 = self.cdma("sync", self.fv("comb", 33, 1, 0, 2048), rgc,
                        waits=[("ve", vrg)], slot="c2")
        vcb = self.now("ve")
        xpTc = self.fv("xpTc", 0, 128, 0, 16 * 34)
        self.se2_xpT = {}
        for i in range(NCH):
            bank = 6 + (i % 2)
            wts = ([("ve", vcb), d1, d2, ddl, drc2, self.ld["sidf"]]
                   if i < 2 else [("se2", self.se2_xpT[i - 2])])
            self.op("tensor", lambda e, i=i, bank=bank: e.transpose(
                self.ps[bank][:, 0:34],
                comb[:, i * 128:(i + 1) * 128], self.sidf[0:34, 0:34]),
                waits=wts, sem="pe")
            pe = self.now("pe")
            self.op("scalar", lambda e, i=i, bank=bank: e.activation(
                xpTc[:, i * 34:(i + 1) * 34], self.ps[bank][:, 0:34],
                AF.Identity, bias=0.0, scale=1.0),
                waits=[("pe", pe)], sem="se2")
            self.se2_xpT[i] = self.now("se2")
        st = self.fv("cmb", 0, 1, 0, 48)
        self.op("vector", lambda e: e.memset(st[:, 0:1], 0.0), sem="ve")
        dbe = self.cdma("sync", st[:, 16:32],
                       g["rows_hbm"][12, 127:2048:128].rearrange("(a c) -> a c", a=1),
                       waits=[drg], slow=True, slot="c3")
        dbs = self.cdma("sync", st[:, 1:16],
                       g["rows_hbm"][12, 127:1920:128].rearrange("(a c) -> a c", a=1),
                       waits=[drg], slow=True, slot="c3")
        vb = self.op("vector", lambda e: e.tensor_sub(
            st[:, 32:48], st[:, 16:32], st[:, 0:16]),
            waits=[dbe, dbs, ("ve", self.now("ve"))], sem="ve")
        db = self.cdma("sync", g["rows_hbm"][11:12, 0:48], st,
                      waits=[("ve", vb)], slot="c3")
        dbst = self.cdma("sync", self.fv("cmb", 0, 128, 48, 16),
                        g["rows_hbm"][11:12, 0:16].partition_broadcast(128),
                        waits=[db], slot="c3")
        dbnd = self.cdma("sync", self.fv("cmb", 0, 128, 64, 16),
                        g["rows_hbm"][11, 16:32].rearrange("(a c) -> a c", a=1)
                        .partition_broadcast(128),
                        waits=[db], slot="c3")
        drr = self.cdma("sync", self.fv("rows16", 0, 16, 0, 16),
                       g["rows_hbm"][11, 32:48].rearrange("(a c) -> a c", a=1)
                       .partition_broadcast(16),
                       waits=[db], slot="c3")
        dec = self.fv("rows16", 0, 16, 16, 16)
        self.op("scalar", lambda e: e.activation(
            dec, self.fv("rows16", 0, 16, 0, 16), AF.Exp, bias=0.0,
            scale=self.fv("acol", 0, 16, 0, 1)),
            waits=[drr, self.ld["acol"]], sem="se")
        self.se_dec = self.now("se")
        self.d_bst128 = dbst
        self.d_bnd128 = dbnd
        vprev = None
        pka = None
        for i in range(NCH):
            sl = self.fv("rch", 0, 16, 0, 128)
            w = [drg, dbst, self.ld["acol"]]
            if vprev is not None:
                w.append(("ve", vprev))
            if pka is not None:
                w.append(pka)
            dsl = self.dma("sync", sl,
                           g["rows_hbm"][12, i * 128:(i + 1) * 128]
                           .rearrange("(a c) -> a c", a=1)
                           .partition_broadcast(16),
                           waits=w, slot="rch")
            self.op("vector", lambda e, sl=sl, i=i: e.tensor_scalar(
                sl, sl, self.fv("cmb", 0, 16, 48 + i, 1),
                None, ALU.subtract), waits=[dsl], sem="ve")
            vprev = self.op("vector", lambda e, sl=sl: e.tensor_scalar(
                sl, sl, self.fv("acol", 0, 16, 0, 1),
                None, ALU.mult), sem="ve")
            pka = self.dma("sync", g["krow_hbm"][i, 0], sl,
                           waits=[("ve", vprev)], slot="karow")
            self.dma("sync", g["krow_hbm"][i, 1],
                     self.fv("comb", 0, 16, i * 128, 128),
                     waits=[d1], slot="kcrow")
        self.ve_p5a = self.now("ve")

    def p5b_z(self):
        g = self.g
        h1 = self.bvv("h1T", 0, 128, 0, 8 * H1S)
        evac_se = {}
        dst_done = {}
        wz_used = {}
        for j4 in range(4):
            half = j4 % 2
            wz = self.bvv("wzrot", 0, 128, half * 4104, 4096)
            w = [("pe", wz_used[j4 - 2])] if j4 >= 2 else []
            dwz = self.dma("gpsimd", wz.rearrange("p (K j) -> p K j", j=512),
                           g["wz_t"][:, :, j4 * 512:(j4 + 1) * 512]
                           .transpose([1, 0, 2]),
                           waits=w, slot="wz%d" % half)
            for i in range(NCH):
                bank = i % 2
                for K in range(8):
                    wts = []
                    if K == 0:
                        wts = [dwz]
                        key = (bank,)
                        if key in evac_se:
                            wts.append(("se", evac_se[key]))
                    self.op("tensor", lambda e, K=K, i=i, bank=bank, wz=wz:
                            e.matmul(self.ps[bank][:, 0:512],
                                     h1[:, K * H1S + 3 + i * 128:
                                        K * H1S + 3 + (i + 1) * 128],
                                     wz[:, K * 512:(K + 1) * 512],
                                     start=(K == 0), stop=(K == 7)),
                            waits=wts, sem="pe" if K == 7 else None)
                pe = self.now("pe")
                stg = self.fv("xrl", 0, 128, bank * 256, 256).bitcast(BF16)
                sw = [("pe", pe)]
                key2 = (bank, "d")
                if key2 in dst_done:
                    sw.append(dst_done[key2])
                self.op("scalar", lambda e, stg=stg, bank=bank: e.activation(
                    stg, self.ps[bank][:, 0:512], AF.Silu, bias=0.0, scale=1.0),
                    waits=sw, sem="se")
                evac_se[(bank,)] = self.now("se")
                dst_done[(bank, "d")] = self.dma(
                    "sync", g["sz_hbm"][i, :, j4 * 512:(j4 + 1) * 512], stg,
                    waits=[("se", evac_se[(bank,)])], slot="szst%d" % bank)
            wz_used[j4] = self.now("pe")
        self.pe_z_done = self.now("pe")
        self.d_szst_full = [("D_szst0", self.now("D_szst0")),
                            ("D_szst1", self.now("D_szst1"))]

    def p5c_loopA(self):
        g = self.g
        xcs = self.bvv("xcsTM", 0, 128, 0, 16 * 2048)
        xpTc = self.fv("xpTc", 0, 128, 0, 16 * 34)
        hloc = self.fv("hstate", 0, 16, 0, 2048)
        dec = self.fv("rows16", 0, 16, 16, 16)
        vmz = self.op("vector", lambda e: e.memset(hloc, 0.0), sem="ve")
        rec_ve = {}
        for i in range(NCH):
            dend = self.fv("cmb", 0, 128, 80, 1)
            wts = [self.d_bnd128, self.ld["arep"]] if i == 0 else []
            wts = wts + [("se2", self.se2_xpT[i])]
            self.op("vector", lambda e, i=i: e.tensor_scalar(
                dend, self.fv("cmb", 0, 128, 64 + i, 1),
                xpTc[:, i * 34 + 33: i * 34 + 34], None, ALU.subtract),
                waits=wts, sem="ve")
            wet = self.fv("cmb", 0, 128, 82, 16)
            sw = self.op("scalar", lambda e: e.activation(
                wet, self.fv("arep"), AF.Exp, bias=0.0, scale=dend),
                waits=[("ve", self.now("ve"))], sem="se")
            dbc = self.fv("cmb", 0, 128, 116, 8).bitcast(BF16)
            self.op("vector", lambda e, i=i: e.tensor_scalar(
                dbc, xpTc[:, i * 34 + 16: i * 34 + 32],
                xpTc[:, i * 34 + 32: i * 34 + 33], None, ALU.mult), sem="ve")
            ub = self.bvv("ubufs", 0, 128, i * 16, 16)
            vu = self.op("vector", lambda e, ub=ub: e.tensor_mul(ub, dbc, wet),
                         waits=[("se", sw)], sem="ve")
            for n4 in range(4):
                bank = 6 + (n4 % 2)
                wts = [("ve", vu)] if n4 == 0 else []
                if (i, n4) == (0, 0):
                    wts.append(("se2", self.se2_xpT[NCH - 1]))
                if (i, n4) >= (0, 2) and (bank, "rec") in rec_ve:
                    wts.append(("ve", rec_ve[(bank, "rec")]))
                self.op("tensor", lambda e, n4=n4, i=i, bank=bank, ub=ub: e.matmul(
                    self.ps[bank][0:16, 0:512], ub,
                    xcs[:, i * 2048 + n4 * 512: i * 2048 + (n4 + 1) * 512],
                    start=True, stop=True),
                    waits=wts, sem="pe")
                pe = self.now("pe")
                self.op("vector", lambda e, n4=n4, i=i, bank=bank:
                        e.scalar_tensor_tensor(
                            hloc[:, n4 * 512:(n4 + 1) * 512],
                            hloc[:, n4 * 512:(n4 + 1) * 512],
                            dec[:, i:i + 1],
                            self.ps[bank][0:16, 0:512], ALU.mult, ALU.add),
                        waits=[("pe", pe)] + ([("ve", vmz), ("se", self.se_dec)]
                                              if (i, n4) == (0, 0) else []),
                        sem="ve")
                rec_ve[(bank, "rec")] = self.now("ve")
        vrec = self.now("ve")
        d1 = self.cdma("sync", g["ag1_in"][0, 0:S * E]
                      .rearrange("(p c) -> p c", p=16), hloc,
                      waits=[("ve", vrec)], slot="c7")
        st2 = self.fv("cmb", 0, 1, 0, 16)
        self.op("vector", lambda e: e.memset(st2, 0.0), sem="ve")
        vr = self.op("vector", lambda e: e.tensor_copy(
            st2[:, 0:1], self.fv("cmb", 0, 1, 31, 1)),
            waits=[("D_c3", self.now("D_c3"))], sem="ve")
        d2 = self.cdma("sync", g["ag1_in"][0:1, S * E:S * E + 16], st2,
                      waits=[("ve", vr)], slot="c7")
        self.op("gpsimd", lambda e: e.collective_compute(
            "AllGather", ALU.bypass, replica_groups=[list(range(NCORE))],
            ins=[g["ag1_in"][:, :]], outs=[g["ag1_out"][:, :]]),
            waits=[d1, d2], sem="cc")
        self.cc_ag1 = self.now("cc")
        self.d_ag1send = d1

    def p5d_hin(self):
        g = self.g
        r16 = lambda p0, pn, o, n: self.fv("rows16", p0, pn, o, n)
        dom = self.cdma("sync", r16(0, 8, 32, 8), g["ordmask_t"][:, :], slot="c5")
        drc = self.cdma("sync", r16(0, 8, 40, 1),
                       g["ag1_out"][:, S * E:S * E + 1],
                       waits=[("cc", self.cc_ag1)], slow=True, slot="c5")
        self.op("tensor", lambda e: e.matmul(
            self.ps[5][0:8, 0:1], r16(0, 8, 32, 8), r16(0, 8, 40, 1),
            start=True, stop=True), waits=[dom, drc], sem="pe")
        pe = self.now("pe")
        vs = self.op("vector", lambda e: e.tensor_copy(
            r16(0, 8, 42, 1), self.ps[5][0:8, 0:1]), waits=[("pe", pe)], sem="ve")
        ds = self.cdma("sync", g["rows_hbm"][13, 0:8].rearrange("(c a) -> c a", a=1),
                      r16(0, 8, 42, 1), waits=[("ve", vs)], slow=True, slot="c5")
        dsr = self.cdma("sync", r16(0, 16, 44, 8),
                       g["rows_hbm"][13:14, 0:8].partition_broadcast(16),
                       waits=[ds], slot="c5")
        dpm = self.cdma("sync", r16(0, 16, 52, 8),
                       g["pmask"][0:1, :].partition_broadcast(16), slot="c5")
        sc = self.op("scalar", lambda e: e.activation(
            r16(0, 16, 60, 8), r16(0, 16, 44, 8), AF.Exp, bias=0.0,
            scale=self.fv("acol", 0, 16, 0, 1)),
            waits=[dsr, self.ld["acol"]], sem="se")
        vc = self.op("vector", lambda e: e.tensor_mul(
            r16(0, 16, 60, 8), r16(0, 16, 60, 8), r16(0, 16, 52, 8)),
            waits=[("se", sc), dpm], sem="ve")
        hin = self.fv("hstate", 0, 16, 0, 2048)
        vmz = self.op("vector", lambda e: e.memset(hin, 0.0),
                      waits=[self.d_ag1send], sem="ve")
        vprev = None
        for j in range(NCORE):
            half = j % 2
            sl = self.fv("e3t" if half == 0 else "e3u", 0, 16, 0, 2048)
            w = [("cc", self.cc_ag1)]
            if j >= 2:
                w.append(("ve", vprev))
            dj = self.dma("sync", sl, g["ag1_out"][j, 0:S * E]
                          .rearrange("(p c) -> p c", p=16),
                          waits=w, slot="hj%d" % half)
            vprev = self.op("vector", lambda e, sl=sl, j=j: e.scalar_tensor_tensor(
                hin, sl, r16(0, 16, 60 + j, 1), hin, ALU.mult, ALU.add),
                waits=[dj] + ([("ve", vc), ("ve", vmz)] if j == 0 else []),
                sem="ve")
        hb = self.bvv("hinbf", 0, 16, 0, 2048)
        self.op("vector", lambda e: e.tensor_copy(hb, hin), sem="ve")
        self.ve_hin = self.now("ve")

    def p5e_pre(self):
        g = self.g
        wot = self.bvv("wot", 0, 128, 0, 16 * 1024)
        dws = []
        for K in range(16):
            dws.append(self.dma("gpsimd", wot[:, K * 1024:(K + 1) * 1024],
                                g["wout_t"][K], waits=[("pe", self.pe_z_done)]
                                if K == 0 else [], slot="wotl"))
        mx = self.fv("modx")
        dga = self.cdma("sync", g["rows_hbm"][14, 0:1024]
                       .rearrange("(dt p) -> p dt", p=128), mx[:, 16:24],
                       waits=[("ve", self.ve_mod)], slow=True, slot="c6")
        dgr = self.cdma("sync", self.fv("garepf", 0, 128, 0, 1024),
                       g["rows_hbm"][14:15, 0:1024].partition_broadcast(128),
                       waits=[dga], slot="c6")
        gab = self.bvv("garep", 0, 128, 0, 1024)
        vg = self.op("vector", lambda e: e.tensor_copy(
            gab, self.fv("garepf", 0, 128, 0, 1024)), waits=[dgr], sem="ve")
        wfull = ("D_wotl", self.now("D_wotl"))
        for K in range(16):
            self.op("vector", lambda e, K=K: e.tensor_mul(
                wot[:, K * 1024:(K + 1) * 1024],
                wot[:, K * 1024:(K + 1) * 1024], gab),
                waits=[wfull, ("ve", vg)] if K == 0 else [], sem="ve")
        self.ve_wot = self.now("ve")
        self.d_drep = self.dma("gpsimd", self.bvv("drep", 0, 128, 0, 2048),
                               g["dparam"][0:1, :].partition_broadcast(128),
                               waits=[dgr, ("ve", vg)], slot="drep")

    def p5e_loopB(self):
        g = self.g
        xcs = self.bvv("xcsTM", 0, 128, 0, 16 * 2048)
        xpTc = self.fv("xpTc", 0, 128, 0, 16 * 34)
        hloc = self.fv("hstate", 0, 16, 0, 2048)
        dec = self.fv("rows16", 0, 16, 16, 16)
        repA = self.fv("repA", 0, 128, 0, 2048)
        repC = self.bvv("repC", 0, 128, 0, 2048)
        e3t = self.fv("e3t", 0, 128, 0, 2048)
        e3u = self.fv("e3u", 0, 128, 0, 2048)
        wot = self.bvv("wot", 0, 128, 0, 16 * 1024)
        hb = self.bvv("hinbf", 0, 16, 0, 2048)
        drep = self.bvv("drep", 0, 128, 0, 2048)
        ps6b = self.ps[6][:, :].bitcast(BF16)
        ps7b = self.ps[7][:, :].bitcast(BF16)
        vmz = self.op("vector", lambda e: e.memset(hloc, 0.0),
                      waits=[("ve", self.ve_hin)], sem="ve")
        st = {}
        ctx = {}
        hsns = {}

        def emit_hsn(i):
            half = i % 2
            hsn = self.bvv("hsn0" if half == 0 else "hsn1", 0, 16, 0, 2048)
            w = [("ve", vmz)] if i == 0 else [("ve", st[("recb", i - 1, 3)])]
            if ("ymm", i - 2) in st:
                w.append(("pe", st[("ymm", i - 2)]))
            vsn = self.op("scalar", lambda e, hsn=hsn: e.activation(
                hsn, hloc, AF.Identity, bias=0.0, scale=1.0),
                waits=w, sem="se")
            hsns[i] = (hsn, vsn)

        def stageA(i):
            half = i % 2
            if i > 0:
                emit_hsn(i - 1)
            dsz = self.dma("sync", self.bvv("szrot", 0, 128, half * 2052, 2048),
                           g["sz_hbm"][i], waits=[st[("y2c", i - 2)]]
                           if ("y2c", i - 2) in st else [], slot="szl%d" % half)
            dra = self.dma("sync", repA,
                           g["krow_hbm"][i:i + 1, 0, :].partition_broadcast(128),
                           waits=([("ve", st[("xsub", i - 1)])]
                                  if ("xsub", i - 1) in st else
                                  [("D_karow", self.now("D_karow")),
                                   ("D_c2", self.now("D_c2"))]),
                           slot="repA")
            drc = self.dma("gpsimd", repC,
                           g["krow_hbm"][i:i + 1, 1, :].partition_broadcast(128),
                           waits=([("ve", st[("crep", i - 1)])]
                                  if ("crep", i - 1) in st else
                                  [("D_kcrow", self.now("D_kcrow"))]),
                           slot="repC")
            rlc = self.fv("cmb", 0, 128, 114, 1)
            self.op("vector", lambda e, i=i: e.tensor_scalar(
                rlc, xpTc[:, i * 34 + 33: i * 34 + 34],
                self.fv("cmb", 0, 128, 48 + i, 1), None, ALU.subtract),
                waits=[self.d_bst128] if i == 0 else [], sem="ve")
            rA = self.fv("cmb", 0, 128, 82, 16)
            self.op("vector", lambda e: e.tensor_tensor(
                rA, self.fv("arep"), rlc.broadcast_to([128, 16]), ALU.mult),
                sem="ve")
            vx = self.op("vector", lambda e: e.tensor_tensor(
                e3t.rearrange("p (s t) -> p s t", t=128),
                repA.rearrange("p (s t) -> p s t", t=128),
                rA.unsqueeze(2).broadcast_to([128, 16, 128]), ALU.subtract),
                waits=[dra], sem="ve")
            st[("xsub", i)] = self.now("ve")
            self.op("vector", lambda e: e.tensor_scalar_min(e3t, e3t, 0.0),
                    sem="ve")
            sx = self.op("scalar", lambda e: e.activation(
                e3u, e3t, AF.Exp, bias=0.0, scale=1.0),
                waits=[("ve", self.now("ve"))], sem="se")
            dbc = self.fv("cmb", 0, 128, 116, 8).bitcast(BF16)
            self.op("vector", lambda e, i=i: e.tensor_scalar(
                dbc, xpTc[:, i * 34 + 16: i * 34 + 32],
                xpTc[:, i * 34 + 32: i * 34 + 33], None, ALU.mult),
                waits=[("se", sx)], sem="ve")
            self.op("vector", lambda e: e.tensor_tensor(
                e3u.rearrange("p (s t) -> p s t", t=128),
                e3u.rearrange("p (s t) -> p s t", t=128),
                dbc.unsqueeze(2).broadcast_to([128, 16, 128]), ALU.mult),
                sem="ve")
            self.op("vector", lambda e: e.tensor_mul(e3u, e3u, repC),
                    waits=[drc], sem="ve")
            st[("crep", i)] = self.now("ve")
            kbf = self.fv("tmp0", 0, 128, 648, 128)
            self.op("vector", lambda e: e.tensor_reduce(
                kbf, e3u.rearrange("p (s t) -> p t s", t=128), AX.X, ALU.add),
                sem="ve")
            kb = self.bvv("kbslot", 0, 128, half * 128, 128)
            vkb = self.op("vector", lambda e, kb=kb: e.tensor_mul(
                kb, kbf, self.stril[:, :]), waits=[self.ld["tril"]]
                if i == 0 else [], sem="ve")
            st[("kb", i)] = self.now("ve")
            rch = self.fv("rch", 0, 16, 0, 128)
            drch = self.dma("sync", rch,
                            g["rows_hbm"][12, i * 128:(i + 1) * 128]
                            .rearrange("(a c) -> a c", a=1)
                            .partition_broadcast(16),
                            waits=[("se", st[("gexp", i - 1)])]
                            if ("gexp", i - 1) in st else [], slot="rch")
            eg = self.fv("e3t", 0, 16, 0, 128)
            el = self.fv("e3t", 0, 16, 128, 128)
            sg1 = self.op("scalar", lambda e: e.activation(
                eg, rch, AF.Exp, bias=0.0, scale=self.fv("acol", 0, 16, 0, 1)),
                waits=[drch, ("ve", st[("xsub", i)])], sem="se")
            gsl = self.bvv("gslot", 0, 16, half * 128, 128)
            gglsl = self.bvv("gslot", 0, 16, 256 + half * 128, 128)
            vggl = self.op("vector", lambda e, gglsl=gglsl, i=i: e.tensor_mul(
                gglsl, self.fv("comb", 0, 16, i * 128, 128), eg),
                waits=[("se", sg1)], sem="ve")
            self.op("vector", lambda e, i=i: e.tensor_scalar(
                rch, rch, self.fv("cmb", 0, 16, 48 + i, 1), None, ALU.subtract),
                sem="ve")
            sg2 = self.op("scalar", lambda e: e.activation(
                el, rch, AF.Exp, bias=0.0, scale=self.fv("acol", 0, 16, 0, 1)),
                waits=[("ve", self.now("ve"))], sem="se")
            st[("gexp", i)] = self.now("se")
            vgl = self.op("vector", lambda e, gsl=gsl, i=i: e.tensor_mul(
                gsl, self.fv("comb", 0, 16, i * 128, 128), el),
                waits=[("se", sg2)], sem="ve")
            st[("g", i)] = self.now("ve")
            ctx[i] = (dsz, kb, gsl, gglsl)

        def stageB(i):
            half = i % 2
            dsz, kb, gsl, gglsl = ctx[i]
            if i not in hsns:
                emit_hsn(i)
            hsn, vsn = hsns[i]
            for n4 in range(4):
                bank = 2 + n4
                w1 = [("ve", st[("kb", i)])]
                if ("y2v", i - 1, n4) in st:
                    w1.append(("ve", st[("y2v", i - 1, n4)]))
                self.op("tensor", lambda e, n4=n4, i=i, bank=bank, kb=kb: e.matmul(
                    self.ps[bank][:, 0:512], kb,
                    xcs[:, i * 2048 + n4 * 512: i * 2048 + (n4 + 1) * 512],
                    start=True, stop=False), waits=w1, sem=None)
                self.op("tensor", lambda e, n4=n4, bank=bank, gsl=gsl, hsn=hsn:
                        e.matmul(self.ps[bank][:, 0:512], gsl,
                                 hsn[:, n4 * 512:(n4 + 1) * 512],
                                 start=False, stop=False),
                        waits=[("ve", st[("g", i)]), ("se", vsn)]
                        if n4 == 0 else [], sem=None)
                self.op("tensor", lambda e, n4=n4, bank=bank, gglsl=gglsl:
                        e.matmul(self.ps[bank][:, 0:512], gglsl,
                                 hb[:, n4 * 512:(n4 + 1) * 512],
                                 start=False, stop=True),
                        waits=[("ve", self.ve_hin)] if (i, n4) == (0, 0) else [],
                        sem="pe")
            st[("ymm", i)] = self.now("pe")
            ub = self.bvv("ubufs", 0, 128, i * 16, 16)
            for n4 in range(4):
                bank = 6 + (n4 % 2)
                wts = []
                if ("tp", i - 1) in st and n4 < 2:
                    wts.append(("se2", st[("tp", i - 1)]))
                if ("recb", i, n4 - 2) in st:
                    wts.append(("ve", st[("recb", i, n4 - 2)]))
                self.op("tensor", lambda e, n4=n4, i=i, bank=bank, ub=ub: e.matmul(
                    self.ps[bank][0:16, 0:512], ub,
                    xcs[:, i * 2048 + n4 * 512: i * 2048 + (n4 + 1) * 512],
                    start=True, stop=True), waits=wts, sem="pe")
                pe = self.now("pe")
                self.op("vector", lambda e, n4=n4, i=i, bank=bank:
                        e.scalar_tensor_tensor(
                            hloc[:, n4 * 512:(n4 + 1) * 512],
                            hloc[:, n4 * 512:(n4 + 1) * 512],
                            dec[:, i:i + 1],
                            self.ps[bank][0:16, 0:512], ALU.mult, ALU.add),
                        waits=[("pe", pe), ("se", vsn)], sem="ve")
                st[("recb", i, n4)] = self.now("ve")
            y2sl = self.bvv("y2r", 0, 128, 0, 2048)
            for n4 in range(4):
                bank = 2 + n4
                t1 = self.fv("tmp0", 0, 128, 520, 256).bitcast(BF16)
                t2 = self.fv("tmp1", 0, 128, 520, 512)
                self.op("vector", lambda e, n4=n4, i=i, t1=t1: e.tensor_mul(
                    t1, xcs[:, i * 2048 + n4 * 512: i * 2048 + (n4 + 1) * 512],
                    drep[:, n4 * 512:(n4 + 1) * 512]),
                    waits=[self.d_drep] if (i, n4) == (0, 0) else [], sem="ve")
                self.op("vector", lambda e, bank=bank, t1=t1, t2=t2: e.tensor_add(
                    t2, self.ps[bank][:, 0:512], t1),
                    waits=[("pe", st[("ymm", i)])] if n4 == 0 else [], sem="ve")
                self.op("vector", lambda e, n4=n4, t2=t2, y2sl=y2sl, half=half:
                        e.tensor_mul(y2sl[:, n4 * 512:(n4 + 1) * 512], t2,
                                     self.bvv("szrot", 0, 128, half * 2052, 2048)
                                     [:, n4 * 512:(n4 + 1) * 512]),
                        waits=([dsz] if n4 == 0 else []) +
                        ([self._dbgy] if n4 == 0 and getattr(self, "_dbgy", None)
                         else []), sem="ve")
                st[("y2v", i, n4)] = self.now("ve")
            st[("y2c", i)] = ("ve", self.now("ve"))
            if "y2_bf" in self.dbgo:
                self._dbgy = self.dma("sync", self.dbgo["y2_bf"][i], y2sl,
                                      waits=[st[("y2c", i)]], slot="dbgy")
            for b2 in range(2):
                psb = ps6b if b2 == 0 else ps7b
                for ti in range(8):
                    es = b2 * 8 + ti
                    wts = []
                    if ti == 0:
                        wts = [("ve", st[("y2v", i, 3)]),
                               ("ve", st[("recb", i, 3 if b2 == 0 else 2)])]
                        if ("opj", i - 1) in st:
                            wts.append(("pe", st[("opj", i - 1)]))
                    self.op("tensor", lambda e, es=es, ti=ti, psb=psb, y2sl=y2sl:
                            e.transpose(psb[:, ti * 128:(ti + 1) * 128],
                                        y2sl[:, es * 128:(es + 1) * 128],
                                        self.sid[:, :]),
                            waits=wts, sem="pe" if ti == 7 else None)
                pe = self.now("pe")
                self.op("scalar", lambda e, b2=b2, psb=psb: e.activation(
                    self.bvv("y2f", 0, 128, b2 * 1024, 1024), psb[:, 0:1024],
                    AF.Identity, bias=0.0, scale=1.0),
                    waits=[("pe", pe)], sem="se2")
            st[("tp", i)] = self.now("se2")
            y2f = self.bvv("y2f", 0, 128, 0, 2048)
            dxr = self.dma("sync", self.fv("xrl", 0, 128, 0, 1024),
                           g["x_tm"][3 + i * 128: 3 + (i + 1) * 128, :],
                           waits=[("ve", st[("x2v", i - 1)])]
                           if ("x2v", i - 1) in st else
                           ([("pe", self.pe_z_done)] + self.d_szst_full
                            if i == 0 else []),
                           slot="xrl")
            for dN in range(2):
                for Ke in range(16):
                    wts = []
                    if Ke == 0:
                        wts = [("se2", st[("tp", i)])]
                        if (i, dN) == (0, 0):
                            wts.append(("ve", self.ve_wot))
                        if ("x2v", i - 1) in st:
                            wts.append(("ve", st[("x2v", i - 1)]))
                    self.op("tensor", lambda e, dN=dN, Ke=Ke, y2f=y2f: e.matmul(
                        self.ps[dN][:, 0:512], y2f[:, Ke * 128:(Ke + 1) * 128],
                        wot[:, Ke * 1024 + dN * 512: Ke * 1024 + dN * 512 + 512],
                        start=(Ke == 0), stop=(Ke == 15)),
                        waits=wts, sem="pe" if Ke == 15 else None)
            st[("opj", i)] = self.now("pe")
            pe = self.now("pe")
            for dN in range(2):
                xo = (self.fv("tmp0", 0, 128, 0, 512) if dN == 0
                      else self.fv("tmp1", 0, 128, 0, 512))
                self.op("vector", lambda e, dN=dN, xo=xo: e.tensor_add(
                    xo, self.fv("xrl", 0, 128, dN * 512, 512),
                    self.ps[dN][:, 0:512]),
                    waits=[("pe", pe), dxr] + ([st[("x2st", i - 1, dN)]]
                                               if ("x2st", i - 1, dN) in st
                                               else []), sem="ve")
                st[("x2v", i)] = self.now("ve")
                st[("x2st", i, dN)] = self.dma(
                    "sync", g["x2_hbm"][i * 128:(i + 1) * 128,
                                        dN * 512:dN * 512 + 512], xo,
                    waits=[("ve", self.now("ve"))], slot="x2st%d" % dN)

        stageA(0)
        for i in range(NCH):
            if i + 1 < NCH:
                stageA(i + 1)
            stageB(i)
        self.ve_p5 = self.now("ve")
        self.st_p5 = st

    def mband(self, it):
        if it < 16:
            return self.bvv("xcsTM", 0, 128, it * 2048, 2048)
        return self.bvv("R1", 0, 128, (it - 16) * 2048, 2048)

    def p7_ln2(self):
        g = self.g
        h2 = self.fv("h2TM", 0, 128, 0, 16 * 1024)
        mx = self.fv("modx")
        mc = self.fv("mod_cols")
        d1 = self.cdma("sync", g["rows_hbm"][18, 0:1024]
                      .rearrange("(dt p) -> p dt", p=128), mx[:, 8:16],
                      slow=True, slot="c8")
        d2 = self.cdma("sync", g["rows_hbm"][17, 0:1024]
                      .rearrange("(dt p) -> p dt", p=128), mc[:, 24:32],
                      slow=True, slot="c8")
        dcb = self.cdma("sync", self.fv("cbrep", 0, 128, 0, 1024),
                       g["rows_hbm"][18:19, 0:1024].partition_broadcast(128),
                       waits=[d1], slot="c8")
        dsb = self.cdma("sync", self.fv("sbrep", 0, 128, 0, 1024),
                       g["rows_hbm"][17:18, 0:1024].partition_broadcast(128),
                       waits=[d2], slot="c8")
        x2w = [("D_x2st0", self.now("D_x2st0")),
               ("D_x2st1", self.now("D_x2st1"))]
        amc = self.fv("cmb", 0, 128, 0, 1)
        vam = self.op("vector", lambda e: e.memset(amc, 0.0), sem="ve")
        sums = self.fv("cmb", 0, 128, 48, 16)
        sums2 = self.fv("cmb", 0, 128, 64, 16)
        a2c = self.fv("cmb", 0, 128, 80, 16)
        b2c = self.fv("cmb", 0, 128, 96, 16)
        st = {}
        dl = {}
        for ch in range(NCH):
            h2c = h2[:, ch * 1024:(ch + 1) * 1024]
            w = list(x2w) if ch < 2 else [dl[ch - 2]]
            dl[ch] = self.dma("sync", h2c,
                              g["x2_hbm"][ch * 128:(ch + 1) * 128, :],
                              waits=w, slot="xr%d" % (ch % 2))
            sq = self.fv("tmp0", 0, 128, 0, 1024) if ch % 2 == 0 else \
                self.fv("tmp1", 0, 128, 0, 1024)
            sqs = self.op("scalar", lambda e, x=h2c, q=sq: e.activation(
                q, x, AF.Square, bias=0.0, scale=1.0),
                waits=[dl[ch]] + ([("ve", st[("s2", ch - 2)])]
                                  if ch >= 2 else []),
                sem="se")
            self.op("vector", lambda e, x=h2c, ch=ch: e.reduce_sum(
                sums[:, ch:ch + 1], x, AX.X), waits=[dl[ch]], sem="ve")
            self.op("vector", lambda e, q=sq, ch=ch: e.reduce_sum(
                sums2[:, ch:ch + 1], q, AX.X), waits=[("se", sqs)], sem="ve")
            st[("s2", ch)] = self.now("ve")
        self.op("vector", lambda e: e.tensor_scalar_mul(sums, sums, 1.0 / D),
                sem="ve")
        self.op("vector", lambda e: e.tensor_scalar(
            sums2, sums2, 1.0 / D, EPS, ALU.mult, ALU.add), sem="ve")
        self.op("vector", lambda e: e.tensor_mul(b2c, sums, sums), sem="ve")
        self.op("vector", lambda e: e.tensor_sub(sums2, sums2, b2c), sem="ve")
        sv = self.op("scalar", lambda e: e.activation(
            sums2, sums2, AF.Sqrt, bias=0.0, scale=1.0),
            waits=[("ve", self.now("ve"))], sem="se")
        self.op("vector", lambda e: e.reciprocal(a2c, sums2),
                waits=[("se", sv)], sem="ve")
        self.op("vector", lambda e: e.tensor_mul(b2c, sums, a2c), sem="ve")
        vb2 = self.op("vector", lambda e: e.tensor_scalar_mul(b2c, b2c, -1.0),
                      sem="ve")
        for ch in range(NCH):
            h2c = h2[:, ch * 1024:(ch + 1) * 1024]
            sa_ = self.op("scalar", lambda e, o=h2c, ch=ch: e.activation(
                o, o, AF.Identity, bias=b2c[:, ch:ch + 1],
                scale=a2c[:, ch:ch + 1]),
                waits=[("ve", vb2)] if ch == 0 else [], sem="se")
            self.op("vector", lambda e, o=h2c: e.tensor_mul(
                o, o, self.fv("cbrep", 0, 128, 0, 1024)),
                waits=[("se", sa_), dcb] if ch == 0 else [("se", sa_)],
                sem="ve")
            self.op("vector", lambda e, o=h2c: e.tensor_add(
                o, o, self.fv("sbrep", 0, 128, 0, 1024)),
                waits=[dsb] if ch == 0 else [], sem="ve")
            tc = self.fv("cmb", 0, 128, 14, 1)
            self.op("vector", lambda e, o=h2c: e.reduce_max(
                tc, o, AX.X, apply_absolute_value=True), sem="ve")
            self.op("vector", lambda e: e.tensor_tensor(amc, amc, tc, ALU.max),
                    waits=[("ve", vam)] if ch == 0 else [], sem="ve")
        vamf = self.now("ve")
        dsr = self.cdma("sync", g["rows_hbm"][16, 0:128]
                       .rearrange("(c a) -> c a", a=1), amc,
                       waits=[("ve", vamf)], slow=True, slot="c9")
        arow = self.fv("xp", 0, 1, 0, 128)
        dld = self.cdma("sync", arow, g["rows_hbm"][16:17, 0:128],
                       waits=[dsr], slot="c9")
        gsc = self.fv("xp", 0, 1, 130, 1)
        vg = self.op("vector", lambda e: e.reduce_max(gsc, arow, AX.X),
                     waits=[dld], sem="ve")
        dag = self.cdma("sync", g["ag2_in"][:, :], gsc, waits=[("ve", vg)],
                       slot="c9")
        self.op("gpsimd", lambda e: e.collective_compute(
            "AllReduce", ALU.max, replica_groups=[list(range(NCORE))],
            ins=[g["ag2_in"][:, :]], outs=[g["ag2_out"][:, :]]),
            waits=[dag], sem="cc")
        cc2 = self.now("cc")
        gx = self.fv("cmb", 0, 128, 18, 1)
        dgx = self.cdma("sync", gx, g["ag2_out"][0:1, :].partition_broadcast(128),
                       waits=[("cc", cc2)], slot="c9")
        cxc = self.fv("cmb", 0, 128, 20, 1)
        dcx = self.cdma("sync", cxc, g["clip_x"][0:1, :].partition_broadcast(128),
                       slot="c9")
        ssx = self.op("scalar", lambda e: e.activation(
            cxc, cxc, AF.Sigmoid, bias=0.0, scale=1.0), waits=[dcx], sem="se")
        tsg = self.fv("cmb", 0, 128, 22, 1)
        self.op("vector", lambda e: e.tensor_mul(tsg, cxc, gx),
                waits=[("se", ssx), dgx], sem="ve")
        sxi = self.fv("cmb", 0, 128, 24, 1)
        self.op("vector", lambda e: e.reciprocal(sxi, tsg), sem="ve")
        vsx = self.op("vector", lambda e: e.tensor_scalar_mul(sxi, sxi, 127.0),
                      sem="ve")
        self.tsg_col = tsg
        self.mgc = self.fv("cmb", 0, 128, 42, 1)
        self.nmgc = self.fv("cmb", 0, 128, 44, 1)
        self.op("vector", lambda e: e.memset(self.mgc, MAGIC), sem="ve")
        vmg = self.op("vector", lambda e: e.memset(self.nmgc, -MAGIC), sem="ve")
        self.vmg = vmg
        qx = self.bvv("qx", 0, 128, 0, 16384)
        ps7b = self.ps[7][:, :].bitcast(BF16)
        for ch in range(NCH):
            h2c = h2[:, ch * 1024:(ch + 1) * 1024]
            tq = self.fv("tmp0", 0, 128, 0, 1024) if ch % 2 == 0 else \
                self.fv("tmp1", 0, 128, 0, 1024)
            qs = self.bvv("y2r", 0, 128, (ch % 2) * 1024, 1024)
            w = ([("ve", vsx), ("ve", vmg)] if ch < 2
                 else [("se2", st[("qev", ch - 2)])])
            s1 = self.op("scalar", lambda e, i=h2c, o=tq: e.activation(
                o, i, AF.Identity, bias=self.mgc, scale=sxi),
                waits=w, sem="se")
            s2 = self.op("scalar", lambda e, o=tq, q=qs: e.activation(
                q, o, AF.Identity, bias=self.nmgc, scale=1.0), sem="se")
            vq = self.op("vector", lambda e, q=qs: e.tensor_scalar(
                q, q, 127.0, -128.0, ALU.min, ALU.max),
                waits=[("se", s2)], sem="ve")
            for dt in range(8):
                self.op("tensor", lambda e, dt=dt, q=qs: e.transpose(
                    ps7b[:, dt * 128:(dt + 1) * 128],
                    q[:, dt * 128:(dt + 1) * 128], self.sid[:, :]),
                    waits=[("ve", vq), ("se2", self.now("se2"))]
                    if dt == 0 else [],
                    sem="pe" if dt == 7 else None)
            pe = self.now("pe")
            for dt in range(8):
                self.op("scalar", lambda e, dt=dt, ch=ch: e.activation(
                    qx[:, dt * 2048 + ch * 128: dt * 2048 + (ch + 1) * 128],
                    ps7b[:, dt * 128:(dt + 1) * 128], AF.Identity,
                    bias=0.0, scale=1.0),
                    waits=[("pe", pe)] if dt == 0 else [], sem="se2")
            st[("qev", ch)] = self.now("se2")
        self.se2_qx = self.now("se2")

    def p8a_gateup(self):
        g = self.g
        qx = self.bvv("qx", 0, 128, 0, 16384)
        fcol = self.fv("cmb", 0, 128, 26, 1)
        vf = self.op("vector", lambda e: e.tensor_scalar_mul(
            fcol, self.tsg_col, 1.0 / 16129.0), sem="ve")
        sgc = self.fv("scl_g", 0, 128, 0, 22)
        suc = self.fv("scl_u", 0, 128, 0, 22)
        vsc = []
        for nm, row, dst in (("sg", 4, sgc), ("su", 5, suc)):
            d = self.cdma("sync", dst, g["rows_hbm"][row, 0:I]
                         .rearrange("(it p) -> p it", p=128),
                         slow=True, slot="c10")
            vsc.append(self.op("vector", lambda e, a=dst: e.tensor_scalar(
                a, a, fcol, None, ALU.mult), waits=[d, ("ve", vf)], sem="ve"))
        amc = self.fv("cmb", 0, 128, 28, 1)
        vam = self.op("vector", lambda e: e.memset(amc, 0.0), sem="ve")
        st = {}
        for it in range(IT):
            half = it % 2
            wg = self.bvv("qwrot", 0, 128, half * 4096, 1024)
            wu = self.bvv("qwrot", 0, 128, half * 4096 + 1024, 1024)
            lw = [("cc", self.cc_agq)] if it == 0 else \
                ([("pe", st[("mml", it - 2)])] if it >= 2 else [])
            dg = self.dma("sync", wg.rearrange("p (c i) -> p c i", i=128),
                          g["agq_out"][:, it * 16384:(it + 1) * 16384]
                          .rearrange("c (p i) -> p c i", p=128),
                          waits=lw, slot="qwg%d" % half)
            du = self.dma("sync", wu.rearrange("p (c i) -> p c i", i=128),
                          g["agq_out"][:, 128 * I + it * 16384:
                                       128 * I + (it + 1) * 16384]
                          .rearrange("c (p i) -> p c i", p=128),
                          waits=lw, slot="qwu%d" % half)
            for tb in range(4):
                bg, bu = tb % 2, 2 + tb % 2
                for dt in range(8):
                    w = []
                    if dt == 0:
                        w = [dg] + ([("se", st[("sil", it, tb - 2)])]
                                    if ("sil", it, tb - 2) in st else []) + \
                            ([("se", st[("sil", it - 1, tb + 2)])]
                             if ("sil", it - 1, tb + 2) in st else []) + \
                            ([("se2", self.se2_qx)] if (it, tb) == (0, 0) else [])
                    self.op("tensor", lambda e, dt=dt, tb=tb, bg=bg, wg=wg:
                            e.matmul(self.ps[bg][:, 0:512],
                                     wg[:, dt * 128:(dt + 1) * 128],
                                     qx[:, dt * 2048 + tb * 512:
                                        dt * 2048 + (tb + 1) * 512],
                                     start=(dt == 0), stop=(dt == 7)),
                            waits=w, sem="pe" if dt == 7 else None)
                for dt in range(8):
                    w = []
                    if dt == 0:
                        w = [du] + ([("ve", st[("mst", it, tb - 2)])]
                                    if ("mst", it, tb - 2) in st else []) + \
                            ([("ve", st[("mst", it - 1, tb + 2)])]
                             if ("mst", it - 1, tb + 2) in st else [])
                    self.op("tensor", lambda e, dt=dt, tb=tb, bu=bu, wu=wu:
                            e.matmul(self.ps[bu][:, 0:512],
                                     wu[:, dt * 128:(dt + 1) * 128],
                                     qx[:, dt * 2048 + tb * 512:
                                        dt * 2048 + (tb + 1) * 512],
                                     start=(dt == 0), stop=(dt == 7)),
                            waits=w, sem="pe" if dt == 7 else None)
                pe = self.now("pe")
                if tb == 3:
                    st[("mml", it)] = pe
                sgb = self.fv("tmp0", 0, 128, 0, 512) if tb % 2 == 0 else \
                    self.fv("tmp1", 0, 128, 0, 512)
                ssil = self.op("scalar", lambda e, bg=bg, it=it, o=sgb:
                               e.activation(o, self.ps[bg][:, 0:512], AF.Silu,
                                            bias=0.0,
                                            scale=sgc[:, it:it + 1]),
                               waits=[("pe", pe)] +
                               ([("ve", vsc[0])] if it == 0 and tb == 0 else []),
                               sem="se")
                st[("sil", it, tb)] = ssil
                msl = self.mband(it)[:, tb * 512:(tb + 1) * 512]
                self.op("vector", lambda e, bu=bu, it=it, o=msl, sgb=sgb:
                        e.scalar_tensor_tensor(o, self.ps[bu][:, 0:512],
                                               suc[:, it:it + 1], sgb,
                                               ALU.mult, ALU.mult),
                        waits=[("se", ssil)] +
                        ([("ve", vsc[1])] if it == 0 and tb == 0 else []),
                        sem="ve")
                tc = self.fv("cmb", 0, 128, 30, 1)
                self.op("vector", lambda e, o=msl: e.reduce_max(
                    tc, o, AX.X, apply_absolute_value=True), sem="ve")
                self.op("vector", lambda e: e.tensor_tensor(
                    amc, amc, tc, ALU.max),
                    waits=[("ve", vam)] if (it, tb) == (0, 0) else [], sem="ve")
                st[("mst", it, tb)] = self.now("ve")
        vamf = self.now("ve")
        dsr = self.cdma("sync", g["rows_hbm"][19, 0:128]
                       .rearrange("(c a) -> c a", a=1), amc,
                       waits=[("ve", vamf)], slow=True, slot="c11")
        arow = self.fv("xp", 0, 1, 132, 128)
        dld = self.cdma("sync", arow, g["rows_hbm"][19:20, 0:128],
                       waits=[dsr], slot="c11")
        gsc = self.fv("xp", 0, 1, 262, 1)
        vg = self.op("vector", lambda e: e.reduce_max(gsc, arow, AX.X),
                     waits=[dld], sem="ve")
        dag = self.cdma("sync", g["ag3_in"][:, :], gsc, waits=[("ve", vg)],
                       slot="c11")
        self.op("gpsimd", lambda e: e.collective_compute(
            "AllReduce", ALU.max, replica_groups=[list(range(NCORE))],
            ins=[g["ag3_in"][:, :]], outs=[g["ag3_out"][:, :]]),
            waits=[dag], sem="cc")
        self.cc_ag3 = self.now("cc")

    def p8b_down(self):
        g = self.g
        g2 = self.fv("cmb", 0, 128, 34, 1)
        dg2 = self.cdma("sync", g2, g["ag3_out"][0:1, :].partition_broadcast(128),
                       waits=[("cc", self.cc_ag3)], slot="c11")
        cdc = self.fv("cmb", 0, 128, 36, 1)
        dcd = self.cdma("sync", cdc, g["clip_dn"][0:1, :].partition_broadcast(128),
                       slot="c11")
        ssd = self.op("scalar", lambda e: e.activation(
            cdc, cdc, AF.Sigmoid, bias=0.0, scale=1.0), waits=[dcd], sem="se")
        s2f = self.fv("cmb", 0, 128, 38, 1)
        self.op("vector", lambda e: e.tensor_mul(s2f, cdc, g2),
                waits=[("se", ssd), dg2], sem="ve")
        s2i = self.fv("cmb", 0, 128, 40, 1)
        self.op("vector", lambda e: e.reciprocal(s2i, s2f), sem="ve")
        vs2 = self.op("vector", lambda e: e.tensor_scalar_mul(s2i, s2i, 127.0),
                      sem="ve")
        cd8 = self.fv("coef_dn", 0, 128, 0, 8)
        ddn = self.cdma("sync", cd8, g["rows_hbm"][6, 0:1024]
                       .rearrange("(dt p) -> p dt", p=128), slow=True,
                       slot="c11")
        self.op("vector", lambda e: e.tensor_mul(
            cd8, cd8, self.fv("modx", 0, 128, 24, 8)), waits=[ddn], sem="ve")
        self.op("vector", lambda e: e.tensor_scalar(
            cd8, cd8, s2f, None, ALU.mult), sem="ve")
        vcf = self.op("vector", lambda e: e.tensor_scalar_mul(
            cd8, cd8, 1.0 / 16129.0), sem="ve")
        dcr = self.cdma("sync", g["rows_hbm"][20, 0:1024]
                       .rearrange("(dt p) -> p dt", p=128), cd8,
                       waits=[("ve", vcf)], slow=True, slot="c11")
        dcf = self.cdma("sync", self.fv("coefrep", 0, 128, 0, 1024),
                       g["rows_hbm"][20:21, 0:1024].partition_broadcast(128),
                       waits=[dcr], slot="c11")
        def qdt(hb):
            if hb < 16:
                return self.bvv("R4", 0, 128, hb * 1024, 1024)
            return self.bvv("R6", 0, 128, (hb - 16) * 1024, 1024)
        for hb in range(IT):
            hv, itl = divmod(hb, 11)
            off = 2 * 128 * I + hv * 11 * 16384 + itl * 16384
            self.dma("sync", qdt(hb).rearrange("p (c h) -> p c h", h=128),
                     g["agq_out"][:, off:off + 16384]
                     .rearrange("c (p h) -> p c h", p=128),
                     slot="qdl")
        dqd = [("D_qdl", self.now("D_qdl"))]
        for hb in range(IT):
            band = self.mband(hb)
            for hf in range(2):
                tq = self.fv("tmp0", 0, 128, 0, 1024) if hf == 0 else \
                    self.fv("tmp1", 0, 128, 0, 1024)
                sl = band[:, hf * 1024:(hf + 1) * 1024]
                s1 = self.op("scalar", lambda e, i=sl, o=tq: e.activation(
                    o, i, AF.Identity, bias=self.mgc, scale=s2i),
                    waits=[("ve", vs2)] if hb == 0 and hf == 0 else [],
                    sem="se")
                s2 = self.op("scalar", lambda e, o=tq, q=sl: e.activation(
                    q, o, AF.Identity, bias=self.nmgc, scale=1.0), sem="se")
                self.op("vector", lambda e, q=sl: e.tensor_scalar(
                    q, q, 127.0, -128.0, ALU.min, ALU.max),
                    waits=[("se", s2)], sem="ve")
        vqmf = self.now("ve")
        st = {}
        for ch in range(NCH):
            dxr = self.dma("sync", self.fv("xrl", 0, 128, 0, 1024),
                           g["x2_hbm"][ch * 128:(ch + 1) * 128, :],
                           waits=[("ve", st[("x3v", ch - 1)])]
                           if ("x3v", ch - 1) in st else [], slot="xrl")
            for dN in range(2):
                for hb in range(IT):
                    w = []
                    if hb == 0:
                        w = [("ve", vqmf)] if ch == 0 and dN == 0 else []
                        if ch == 0:
                            w += dqd
                        if ("x3st", ch - 1, dN) in st:
                            w.append(st[("x3st", ch - 1, dN)])
                        if ("ev", ch - 1, dN) in st:
                            w.append(("ve", st[("ev", ch - 1, dN)]))
                    self.op("tensor", lambda e, hb=hb, ch=ch, dN=dN: e.matmul(
                        self.ps[dN][:, 0:512],
                        self.mband(hb)[:, ch * 128:(ch + 1) * 128],
                        qdt(hb)[:, dN * 512:(dN + 1) * 512],
                        start=(hb == 0), stop=(hb == IT - 1)),
                        waits=w, sem="pe" if hb == IT - 1 else None)
                pe = self.now("pe")
                xo = (self.fv("tmp0", 0, 128, 0, 512) if dN == 0
                      else self.fv("tmp1", 0, 128, 0, 512))
                self.op("vector", lambda e, dN=dN, xo=xo: e.tensor_tensor(
                    xo, self.ps[dN][:, 0:512],
                    self.fv("coefrep", 0, 128, dN * 512, 512), ALU.mult),
                    waits=[("pe", pe), dcf] + ([st[("x3st", ch - 1, dN)]]
                                               if ("x3st", ch - 1, dN) in st
                                               else []), sem="ve")
                self.op("vector", lambda e, dN=dN, xo=xo: e.tensor_add(
                    xo, xo, self.fv("xrl", 0, 128, dN * 512, 512)),
                    waits=[dxr], sem="ve")
                st[("ev", ch, dN)] = self.now("ve")
                st[("x3v", ch)] = self.now("ve")
                st[("x3st", ch, dN)] = self.dma(
                    "sync", self.out_x3[ch * 128:(ch + 1) * 128,
                                        dN * 512:(dN + 1) * 512], xo,
                    waits=[("ve", self.now("ve"))], slot="out%d" % dN)

    def emit_all(self):
        phases = [("p0", self.p0_consts, 0), ("p2a", self.p2a_mod_send, 0),
                  ("p1s", self.p1_ln1, 1), ("p2b", self.p2b_mod_recv, 1),
                  ("p2c", self.p2c_quant_weights, 0),
                  ("p1", self.p1_affine, 1), ("p4", self.p4_mixer_in, 1),
                  ("p5a", self.p5a_smalls, 0), ("p5b", self.p5b_z, 0),
                  ("p5c", self.p5c_loopA, 1), ("p5pre", self.p5e_pre, 0),
                  ("p5d", self.p5d_hin, 1), ("p5", self.p5e_loopB, 1),
                  ("p7", self.p7_ln2, 1), ("p8a", self.p8a_gateup, 1),
                  ("p8b", self.p8b_down, 1)]
        for name, fn, bar in phases:
            fn()
            if bar:
                self.barrier()
            if self.upto == name:
                self.final_wait()
                return
        if self.upto in ("all",):
            self.final_wait()
            return
        raise NotImplementedError("later phases pending")

    def final_wait(self):
        waits = [(sname, self.count[sname]) for sname in self.semnames
                 if sname.startswith("D_") and self.count[sname] > 0]
        waits += [("cc", self.now("cc"))]
        self.emit("sync", lambda e: e.nop(), waits, ())


def make_program(dbg=(), upto="all"):
    b = Build(dbg, upto)
    b.emit_all()
    b.materialize()
    b.stack.close()
    return b


def kernel(**inputs):
    maps = _stage(inputs)
    b = make_program()
    res = run_bass_kernel_spmd(b.nc, maps, core_ids=list(range(NCORE)))
    out = np.empty((B, L, D), np.float32)
    for k in range(NCORE):
        bb, q = divmod(k, 4)
        out[bb, q * T:(q + 1) * T] = np.asarray(res.results[k]["out_x3"])
    return out


# version 08
# speedup vs baseline: 1.0152x; 1.0009x over previous
import contextlib
import numpy as np
import ml_dtypes
import concourse.bass as bass
import concourse.mybir as mybir
from concourse.bass_utils import run_bass_kernel_spmd

F32 = mybir.dt.float32
BF16 = mybir.dt.bfloat16
AF = mybir.ActivationFunctionType
ALU = mybir.AluOpType
AX = mybir.AxisListType

NCORE = 8
B, L, D = 2, 8192, 1024
E, S, CONV = 2048, 16, 4
I = 2816
T = 2048
TH = T + 3
C = 128
NCH = T // C
DT = D // 128
ET = E // 128
TT = T // 128
IT = I // 128
MAGIC = 12582912.0
IQ = I // 8
EPS = 1e-5
H1S = 2052


def _stage(inp):
    x = np.asarray(inp["L0_attn_x"])
    w_in = np.asarray(inp["L0_attn_in_proj_w"])
    conv_w = np.asarray(inp["L0_attn_conv_w"]).reshape(E, CONV)
    conv_b = np.asarray(inp["L0_attn_conv_b"]).reshape(1, E)
    xpw = np.asarray(inp["L0_attn_x_proj_w"])
    alog = np.asarray(inp["L0_attn_A_log"])
    dpar = np.asarray(inp["L0_attn_D_param"]).reshape(1, E)
    w_out = np.asarray(inp["L0_attn_out_proj_w"])
    clip_x = np.asarray(inp["L0_ffn_clip_x"]).reshape(1, 1)
    clip_dn = np.asarray(inp["L0_ffn_clip_down"]).reshape(1, 1)
    w_g = np.asarray(inp["L0_ffn_w_gate"])
    w_u = np.asarray(inp["L0_ffn_w_up"])
    w_d = np.asarray(inp["L0_ffn_w_down"])
    cc = np.asarray(inp["L0_c"])
    wmod = np.asarray(inp["L0_wmod"])

    co = np.ascontiguousarray
    wxc_tiled = co(w_in[:E].T.reshape(8, 128, 16, 128).transpose(2, 0, 1, 3))
    wz_t = co(w_in[E:].T.reshape(8, 128, E))
    xpw_t = co(xpw.T.reshape(16, 128, 33))
    wout_t = co(w_out.T.reshape(16, 128, D))
    wgT = w_g.T
    wuT = w_u.T
    wdT = w_d.T

    maps = []
    for k in range(NCORE):
        b, q = divmod(k, 4)
        lo = q * T
        xh = x[b, 0:3] if q == 0 else x[b, lo - 3:lo]
        x_tm = co(np.concatenate([xh, x[b, lo:lo + T]], axis=0))
        pm = np.zeros((1, 8), np.float32)
        om = np.zeros((8, 8), np.float32)
        for j in range(4 * b, k):
            pm[0, j] = 1.0
            for u in range(j + 1, k):
                om[u, j] = 1.0
        maps.append(dict(
            x_tm=x_tm.astype(np.float32),
            wxc_t=wxc_tiled.astype(np.float32),
            wz_t=wz_t.astype(np.float32),
            xpw_t=xpw_t.astype(np.float32),
            wout_t=wout_t.astype(np.float32),
            conv_w=co(conv_w).astype(np.float32),
            conv_b=co(conv_b).astype(np.float32),
            dparam=co(dpar).astype(np.float32),
            alog_row=co(alog[0:1, :]).astype(np.float32),
            clip_x=clip_x.astype(np.float32),
            clip_dn=clip_dn.astype(np.float32),
            wmod_sh=co(wmod[:, k * 768:(k + 1) * 768]).astype(np.float32),
            cT=co(cc.T).astype(np.float32),
            wg_nat=co(w_g[k * IQ:(k + 1) * IQ]).astype(np.float32),
            wu_nat=co(w_u[k * IQ:(k + 1) * IQ]).astype(np.float32),
            wd_nat=co(w_d[k * 128:(k + 1) * 128]).astype(np.float32),
            wgT_sh=co(wgT[k * 128:(k + 1) * 128]).astype(np.float32),
            wuT_sh=co(wuT[k * 128:(k + 1) * 128]).astype(np.float32),
            wdT_sh=co(wdT[:, k * 128:(k + 1) * 128]).astype(np.float32),
            halo=np.array([[0.0 if q == 0 else 1.0]], np.float32),
            bsel=np.array([[float(b)]], np.float32),
            pmask=pm,
            ordmask_t=om,
        ))
    return maps


class Build:
    def __init__(self, dbg=(), upto="all", dbg_scr=False):
        self.upto = upto
        self.dbg_scr = dbg_scr
        self.nc = nc = bass.Bass("TRN2", target_bir_lowering=False)
        self.events = []
        self.count = {}
        self.dbg = dict(dbg)
        self.stack = contextlib.ExitStack()

        g = self.g = {}
        def din(name, shape):
            g[name] = nc.dram_tensor(name, list(shape), F32, kind="ExternalInput").ap()
        for name, shape in [
            ("x_tm", (TH, D)), ("wxc_t", (16, 8, 128, 128)), ("wz_t", (8, 128, E)),
            ("xpw_t", (16, 128, 33)), ("wout_t", (16, 128, D)), ("conv_w", (E, CONV)),
            ("conv_b", (1, E)), ("dparam", (1, E)), ("alog_row", (1, S)),
            ("clip_x", (1, 1)), ("clip_dn", (1, 1)), ("wmod_sh", (D, 768)),
            ("cT", (D, 2)), ("wg_nat", (IQ, D)), ("wu_nat", (IQ, D)),
            ("wd_nat", (128, I)), ("wgT_sh", (128, I)), ("wuT_sh", (128, I)),
            ("wdT_sh", (I, 128)), ("halo", (1, 1)), ("bsel", (1, 1)),
            ("pmask", (1, 8)), ("ordmask_t", (8, 8)),
        ]:
            din(name, shape)
        self.out_x3 = nc.dram_tensor("out_x3", [T, D], F32, kind="ExternalOutput").ap()
        self.dbgo = {}
        for name, shape in self.dbg.items():
            dt = BF16 if name.endswith("_bf") else F32
            self.dbgo[name] = nc.dram_tensor("dbg_" + name, list(shape), dt,
                                             kind="ExternalOutput").ap()

        def scr(name, shape, dt=F32):
            kw = ({"kind": "ExternalOutput"}
                  if self.dbg_scr and not name.startswith("ag") else {})
            g[name] = nc.dram_tensor(name, list(shape), dt, **kw).ap()
        scr("x2_hbm", (T, D))
        scr("sz_hbm", (NCH, 128, 2048), BF16)
        scr("h2_hbm", (TT, 8, 128, 128))
        scr("xq_hbm", (8, 128, T))
        scr("rows_hbm", (24, 4096))
        scr("krow_hbm", (NCH, 2, 2048))
        scr("ag0_in", (1, 2368))
        scr("ag0_out", (NCORE, 2368))
        NQ = 128 * I * 2 + I * 128
        scr("agq_in", (1, NQ), BF16)
        scr("agq_out", (NCORE, NQ), BF16)
        scr("ag1_in", (1, S * E + 16))
        scr("ag1_out", (NCORE, S * E + 16))
        scr("ag2_in", (1, 1))
        scr("ag2_out", (1, 1))
        scr("ag3_in", (1, 1))
        scr("ag3_out", (1, 1))

        g["ident"] = nc.inline_tensor(np.eye(128, dtype=ml_dtypes.bfloat16),
                                      name="identbf").ap()
        g["identf"] = nc.inline_tensor(np.eye(128, dtype=np.float32),
                                       name="identf32").ap()
        g["tril"] = nc.inline_tensor(
            np.ascontiguousarray(np.triu(np.ones((128, 128), np.float32))),
            name="trilf").ap()

        self.semnames = ["dls", "dlg", "dlc", "cc", "pe", "ve", "se", "bar",
                         "pe2", "ve2", "se2"]
        for s in self.semnames:
            self.count[s] = 0
        self.sems = {}

        self.NW = 52736
        self.sb = self.stack.enter_context(nc.sbuf_tensor("arena", [128, self.NW], F32))
        self.sid = self.stack.enter_context(nc.sbuf_tensor("sident", [128, 128], BF16))
        self.sidf = self.stack.enter_context(nc.sbuf_tensor("sidentf", [128, 128], F32))
        self.stril = self.stack.enter_context(nc.sbuf_tensor("stril", [128, 128], F32))
        self.ps = [self.stack.enter_context(nc.psum_tensor(f"ps{i}", [128, 512], F32))
                   for i in range(8)]

        self.off = {}
        self._size = {}
        self.cur = 0
        self._layout()

    def _alloc(self, name, nwords):
        self._size[name] = nwords
        self.off[name] = self.cur
        self.cur += (nwords + 1) & ~1
        assert self.cur <= self.NW, f"arena overflow at {name}: {self.cur}"

    def _alias(self, name, base, word_off, nwords):
        self._size[name] = nwords
        assert word_off + nwords <= self._size[base], (name, base)
        self.off[name] = self.off[base] + word_off

    def _layout(self):
        A, AL = self._alloc, self._alias
        A("mod_cols", 96)
        A("modx", 32)
        A("arow", 16); A("acol", 2); A("arep", 16)
        A("scl_g", 22); A("scl_u", 22); A("coef_dn", 8)
        A("sc", 64)
        A("convw", 80)
        A("tmp0", 1032); A("tmp1", 1032); A("cmb", 128)

        A("R1", 7168)
        AL("modw", "R1", 0, 6144)
        AL("wqstage", "R1", 3074, 1408)
        AL("wqrep", "R1", 4484, 1408)
        AL("wqout", "R1", 5894, 704)
        AL("amaxst", "R1", 3074, 2816)
        AL("xc0", "R1", 0, 2052)
        AL("xcin", "R1", 2054, 2052)
        AL("xcsl", "R1", 4108, 1028)
        AL("wrot", "R1", 5138, 1088)
        AL("repA", "R1", 0, 2048)
        AL("repC", "R1", 2048, 1024)
        AL("e3t", "R1", 3072, 2048)
        AL("e3u", "R1", 5120, 2048)

        A("R2", 4192)
        AL("xp", "R2", 0, 2048)
        AL("r2row", "R2", 0, 2816)
        AL("comb", "R2", 2050, 2080)
        AL("convw2", "R2", 2050, 80)

        A("R3", 9600)
        AL("xrot", "R3", 0, 2064)
        AL("xhalo", "R3", 2066, 1024)
        AL("sums", "R3", 3092, 40)
        AL("ab1", "R3", 3134, 40)
        AL("xaff", "R3", 3176, 1028)
        AL("xpTc", "R3", 0, 548)
        AL("hsn0", "R3", 550, 1024)
        AL("hsn1", "R3", 1576, 1024)
        AL("kbslot", "R3", 2602, 128)
        AL("gslot", "R3", 2732, 260)
        AL("rch", "R3", 2994, 128)
        AL("rows16", "R3", 3124, 96)
        AL("ubufs", "R3", 3222, 128)
        AL("hstate", "R3", 3352, 2048)
        AL("y2r", "R3", 5402, 1024)
        AL("xcin2", "R3", 5402, 2052)
        AL("xrl", "R3", 6428, 1024)
        AL("y2f", "R3", 7454, 1024)
        AL("hinbf", "R3", 8480, 1024)
        AL("ab2", "R3", 9506, 40)
        AL("x2rot", "R3", 0, 2064)
        AL("x2aff", "R3", 3176, 1028)

        A("R4", 8212)
        AL("h1T", "R4", 0, 8212)
        AL("wot", "R4", 0, 8192)
        AL("qx", "R4", 0, 8192)

        A("R6", 4104)
        AL("wzrot", "R6", 0, 4100)
        AL("szrot", "R6", 0, 2052)
        AL("garepf", "R6", 2054, 1024)
        AL("garep", "R6", 3080, 512)
        AL("drep", "R6", 2054, 1024)
        AL("qwrot", "R6", 0, 4100)
        AL("dnbuf", "R6", 516, 2048)
        AL("xpwb", "R6", 2054, 288)

        A("xcsTM", 16384)
        AL("h2TM", "xcsTM", 0, 16384)
        AL("cbrep", "R6", 0, 1024)
        AL("sbrep", "R6", 1026, 1024)
        AL("coefrep", "R6", 3072, 1024)

    def fv(self, name, p0=0, pn=128, o=0, n=None):
        off, sz = self.off[name], self._size[name]
        n = (sz - o) if n is None else n
        assert o + n <= sz, ("fv overflow", name, o, n, sz)
        return self.sb[p0:p0 + pn, off + o:off + o + n]

    def bvv(self, name, p0=0, pn=128, o=0, n=None):
        off, sz = self.off[name], self._size[name]
        nb = 2 * sz
        n = (nb - o) if n is None else n
        base = self.sb[p0:p0 + pn, off:off + sz].bitcast(BF16)
        return base[:, o:o + n]

    def now(self, s):
        return self.count[s]

    def emit(self, eng, fn, waits=(), incs=()):
        for s, n in incs:
            self.count[s] += n
        self.events.append((eng, fn, tuple(waits), tuple(incs)))

    def dma(self, eng, out, in_, waits=(), slow=False, slot="misc"):
        sem = "D_" + slot
        if sem not in self.count:
            self.count[sem] = 0
            self.semnames.append(sem)
        def fn(e, o=out, i=in_, sl=slow):
            return e.dma_start(out=o, in_=i, allow_slow_non_contiguous=sl)
        self.emit(eng, fn, waits, [(sem, 16)])
        return (sem, self.now(sem))

    def cdma(self, eng, out, in_, waits=(), slow=False, slot="chain"):
        if not hasattr(self, "_chain"):
            self._chain = {}
        prev = self._chain.get(slot)
        w = list(waits) + ([prev] if prev else [])
        t = self.dma(eng, out, in_, w, slow, slot)
        self._chain[slot] = t
        return t

    def op(self, eng, fn, waits=(), sem=None, drain=True):
        incs = [(sem, 1)] if sem else []
        self.emit(eng, fn, waits, incs)
        if drain and eng in ("vector", "scalar"):
            self.emit(eng, lambda e: e.drain(), (), ())
        return self.now(sem) if sem else None

    def barrier(self, engines=("tensor", "vector", "scalar", "gpsimd", "sync")):
        for e in engines:
            self.emit(e, lambda eng: eng.nop(), (), [("bar", 1)])
        tgt = self.now("bar")
        for e in engines:
            self.emit(e, lambda eng: eng.nop(), [("bar", tgt)], ())
        return tgt

    def materialize(self):
        nc = self.nc
        for s in self.semnames:
            self.sems[s] = self.stack.enter_context(nc.semaphore(s))
        sems = self.sems
        with nc.Block() as block:
            def run(engname, eng):
                for e, fn, waits, incs in self.events:
                    if e != engname:
                        continue
                    for s, v in waits:
                        eng.wait_ge(sems[s], v)
                    ins = fn(eng)
                    for s, n in incs:
                        ins = ins.then_inc(sems[s], n)

            @block.tensor
            def _(eng):
                run("tensor", eng)

            @block.vector
            def _(eng):
                run("vector", eng)

            @block.scalar
            def _(eng):
                run("scalar", eng)

            @block.gpsimd
            def _(eng):
                run("gpsimd", eng)

            @block.sync
            def _(eng):
                run("sync", eng)

    def p0_consts(self):
        g = self.g
        L = self.ld = {}
        L["sid"] = self.cdma("sync", self.sid[:, :], g["ident"][:, :], slot="cs")
        L["tril"] = self.cdma("sync", self.stril[:, :], g["tril"][:, :], slot="cs")
        L["sidf"] = self.cdma("sync", self.sidf[:, :], g["identf"][:, :], slot="cs")
        arow = self.fv("arow", 0, 1, 0, 16)
        d0 = self.cdma("sync", arow, g["alog_row"][:, :], slot="cs")
        s0 = self.op("scalar", lambda e: e.activation(arow, arow, AF.Exp,
                                                      bias=0.0, scale=1.0),
                     waits=[d0], sem="se")
        v0 = self.op("vector", lambda e: e.tensor_scalar_mul(arow, arow, -1.0),
                     waits=[("se", s0)], sem="ve")
        d1 = self.cdma("sync", g["rows_hbm"][0:1, 0:16], arow, waits=[("ve", v0)],
                      slot="cs")
        L["acol"] = self.cdma("sync", self.fv("acol", 0, 16, 0, 1),
                             g["rows_hbm"][0, 0:16].rearrange("(a b) -> a b", b=1),
                             waits=[d1], slow=True, slot="cs")
        L["acol2"] = self.cdma("sync", self.fv("acol", 16, 16, 0, 1),
                              g["rows_hbm"][0, 0:16].rearrange("(a b) -> a b", b=1),
                              waits=[d1], slow=True, slot="cs")
        L["arep"] = self.cdma("sync", self.fv("arep"),
                             g["rows_hbm"][0:1, 0:16].partition_broadcast(128),
                             waits=[d1], slot="cs")
        L["convw"] = self.cdma("sync", self.fv("convw", 0, 128, 0, 64)
                              .rearrange("p (et c) -> p et c", c=4),
                              g["conv_w"].rearrange("(et p) c -> p et c", p=128),
                              slot="cs")
        L["convb"] = self.cdma("sync", self.fv("convw", 0, 128, 64, 16),
                              g["conv_b"].rearrange("a (et p) -> p (a et)", p=128),
                              slow=True, slot="cs")
        L["xpwb"] = self.dma("gpsimd", self.bvv("xpwb", 0, 128, 0, 16 * 34)
                             .rearrange("p (es m) -> p es m", m=34)[:, :, 0:33],
                             g["xpw_t"].transpose([1, 0, 2]), slot="xpwb")
        L["clips"] = self.cdma("sync", self.fv("sc", 0, 1, 0, 1), g["clip_x"][:, :],
                              slot="cs")
        self.cdma("sync", self.fv("sc", 0, 1, 2, 1), g["clip_dn"][:, :], slot="cs")
        L["halo"] = self.cdma("sync", self.fv("sc", 0, 128, 4, 1),
                             g["halo"][0:1, :].partition_broadcast(128), slot="cs")
        L["bsel"] = self.cdma("sync", self.fv("sc", 0, 128, 6, 1),
                             g["bsel"][0:1, :].partition_broadcast(128), slot="cs")

    def p2a_mod_send(self):
        g = self.g
        ct = self.fv("sc", 0, 128, 8, 16)
        dl = self.cdma("sync", ct.rearrange("p (dt b) -> p dt b", b=2),
                      g["cT"].rearrange("(dt p) b -> p dt b", p=128), slot="cs")
        se0 = self.op("scalar", lambda e: e.activation(ct, ct, AF.Silu,
                                                       bias=0.0, scale=1.0),
                      waits=[dl], sem="se")
        dmw = self.dma("sync", self.fv("modw", 0, 128, 0, 8 * 768)
                       .rearrange("p (dt j) -> p dt j", j=768),
                       g["wmod_sh"].rearrange("(dt p) j -> p dt j", p=128),
                       slot="modw")
        modp = self.fv("sc", 0, 128, 40, 12)
        modw = self.fv("modw", 0, 128, 0, 8 * 768)
        for ccj in range(6):
            for dt in range(8):
                w = []
                if dt == 0:
                    w = [dmw, ("se", se0), ("ve", self.now("ve"))]
                self.op("tensor", lambda e, ccj=ccj, dt=dt: e.matmul(
                    self.ps[0][:, 0:2],
                    modw[:, dt * 768 + ccj * 128: dt * 768 + (ccj + 1) * 128],
                    ct[:, dt * 2: dt * 2 + 2],
                    start=(dt == 0), stop=(dt == 7)),
                    waits=w, sem="pe" if dt == 7 else None)
            pe = self.now("pe")
            self.op("vector", lambda e, ccj=ccj: e.tensor_copy(
                modp.rearrange("p (b c) -> p b c", c=6)[:, :, ccj],
                self.ps[0][:, 0:2]),
                waits=[("pe", pe)], sem="ve")
        vmod = self.now("ve")
        self.pe_mod = self.now("pe")
        dsent = [self.dma("sync", g["ag0_in"][0, 0:1536]
                          .rearrange("(p bc) -> p bc", bc=12), modp,
                          waits=[("ve", vmod)], slot="ag0m")]
        vprev = None
        prevstore = None
        first_amax = True
        for name, nrows, ncols, aoff in (("wg_nat", IQ, D, 1536),
                                         ("wu_nat", IQ, D, 1888),
                                         ("wd_nat", 128, I, 2240)):
            r0 = 0
            while r0 < nrows:
                rn = min(128, nrows - r0)
                stage = self.fv("amaxst", 0, rn, 0, ncols)
                d = self.dma("sync", stage, g[name][r0:r0 + rn, :],
                             waits=([("ve", vprev)] if vprev else
                                    [("pe", self.pe_mod)] if first_amax else []),
                             slot="amaxst")
                first_amax = False
                col = self.fv("sc", 0, rn, 24, 1)
                vprev = self.op("vector", lambda e, st=stage, c=col, rn=rn:
                                e.reduce_max(c, st, AX.X, apply_absolute_value=True),
                                waits=[d] + ([prevstore] if prevstore else []),
                                sem="ve")
                prevstore = self.dma("sync",
                                     g["ag0_in"][0, aoff + r0: aoff + r0 + rn]
                                     .rearrange("(p a) -> p a", a=1), col,
                                     waits=[("ve", vprev)], slot="ag0c")
                dsent.append(prevstore)
                r0 += rn
        agw = [("D_ag0m", self.now("D_ag0m")), ("D_ag0c", self.now("D_ag0c"))]
        self.op("gpsimd", lambda e: e.collective_compute(
            "AllGather", ALU.bypass, replica_groups=[list(range(NCORE))],
            ins=[self.g["ag0_in"][:, :]], outs=[self.g["ag0_out"][:, :]]),
            waits=agw, sem="cc")
        self.cc_ag0 = self.now("cc")

    def p2b_mod_recv(self):
        g = self.g
        d = self.dma("sync",
                     self.fv("mod_cols").rearrange("p (b co cc) -> p b co cc",
                                                   b=2, co=8),
                     g["ag0_out"][:, 0:1536]
                     .rearrange("co (p b cc) -> p b co cc", b=2, cc=6),
                     waits=[("cc", self.cc_ag0)], slow=True, slot="modc")
        mc = self.fv("mod_cols")
        bs = self.fv("sc", 0, 128, 6, 1)
        self.op("vector", lambda e: e.tensor_sub(self.fv("tmp0", 0, 128, 0, 48),
                                                 mc[:, 48:96], mc[:, 0:48]),
                waits=[d, self.ld["bsel"]], sem="ve")
        self.op("vector", lambda e: e.scalar_tensor_tensor(
            mc[:, 0:48], self.fv("tmp0", 0, 128, 0, 48), bs, mc[:, 0:48],
            ALU.mult, ALU.add), sem="ve")
        mx = self.fv("modx")
        self.op("vector", lambda e: e.tensor_scalar_add(mx[:, 0:8], mc[:, 8:16], 1.0),
                sem="ve")
        self.op("vector", lambda e: e.tensor_scalar_add(mx[:, 8:16], mc[:, 32:40], 1.0),
                sem="ve")
        for src, dsto in ((16, 16), (40, 24)):
            t0 = self.fv("tmp0", 0, 128, 0, 8)
            t1 = self.fv("tmp0", 0, 128, 8, 8)
            t2 = self.fv("tmp0", 0, 128, 16, 8)
            self.op("vector", lambda e, s=src, a=t0: e.tensor_single_scalar(
                a, mc[:, s:s + 8], 0.0, ALU.is_ge), sem="ve")
            self.op("vector", lambda e, s=src, a=t1: e.tensor_scalar_max(
                a, mc[:, s:s + 8], 0.25), sem="ve")
            self.op("vector", lambda e, s=src, a=t2: e.tensor_scalar_min(
                a, mc[:, s:s + 8], -0.25), sem="ve")
            self.op("vector", lambda e, a=t1, bq=t2: e.tensor_sub(a, a, bq), sem="ve")
            self.op("vector", lambda e, a=t0, bq=t1: e.tensor_mul(a, a, bq), sem="ve")
            self.op("vector", lambda e, a=t0, bq=t2, do=dsto: e.tensor_add(
                mx[:, do:do + 8], a, bq), sem="ve")
        self.ve_mod = self.now("ve")

    def p2c_quant_weights(self):
        g = self.g
        dr = []
        dr.append(self.cdma("sync", g["rows_hbm"][4, 0:I]
                           .rearrange("(c j) -> c j", j=IQ),
                           g["ag0_out"][:, 1536:1536 + IQ],
                           waits=[("cc", self.cc_ag0)], slot="c4"))
        dr.append(self.cdma("sync", g["rows_hbm"][5, 0:I]
                           .rearrange("(c j) -> c j", j=IQ),
                           g["ag0_out"][:, 1888:1888 + IQ],
                           waits=[("cc", self.cc_ag0)], slot="c4"))
        dr.append(self.cdma("sync", g["rows_hbm"][6, 0:1024]
                           .rearrange("(c j) -> c j", j=128),
                           g["ag0_out"][:, 2240:2240 + 128],
                           waits=[("cc", self.cc_ag0)], slot="c4"))
        vprev = None
        prevrec = None
        rec_t = []
        for i, (src, dst, n) in enumerate(((4, 7, I), (5, 8, I), (6, 9, 1024))):
            row = self.fv("r2row", 0, 1, 0, n)
            d = self.dma("sync", row, g["rows_hbm"][src:src + 1, 0:n],
                         waits=[dr[i]] + ([("ve", vprev)] if vprev else []) +
                               ([prevrec] if prevrec else []),
                         slot="xprow")
            self.op("vector", lambda e, r=row: e.reciprocal(r, r),
                    waits=[d], sem="ve")
            vprev = self.op("vector", lambda e, r=row: e.tensor_scalar_mul(
                r, r, 127.0), sem="ve")
            prevrec = self.cdma("sync", g["rows_hbm"][dst:dst + 1, 0:n], row,
                               waits=[("ve", vprev)], slot="c4")
            rec_t.append(prevrec)
        HF = I // 2
        vq = None
        dq_out = []
        for wname, rrow, qoff in (("wgT_sh", 7, 0), ("wuT_sh", 8, 128 * I)):
            for hv in range(2):
                st = self.fv("wqstage", 0, 128, 0, HF)
                rp = self.fv("wqrep", 0, 128, 0, HF)
                qo = self.bvv("wqout", 0, 128, 0, HF)
                w = [("ve", vq)] if vq else []
                d1 = self.dma("sync", st, g[wname][:, hv * HF:(hv + 1) * HF],
                              waits=w, slot="wqstage")
                d2 = self.dma("sync", rp,
                              g["rows_hbm"][rrow:rrow + 1, hv * HF:(hv + 1) * HF]
                              .partition_broadcast(128),
                              waits=w + [rec_t[0 if rrow == 7 else 1]],
                              slot="wqrep")
                self.op("vector", lambda e, a=st, bq=rp: e.tensor_mul(a, a, bq),
                        waits=[d1, d2, ("D_agq", self.now("D_agq"))
                               if "D_agq" in self.count else d1], sem="ve")
                vq = self.op("vector", lambda e, a=st, q=qo: e.tensor_scalar(
                    q, a, MAGIC, MAGIC, ALU.add, ALU.subtract), sem="ve")
                dq_out.append(self.dma("sync",
                              g["agq_in"][0, qoff + hv * 11 * 16384:
                                          qoff + (hv + 1) * 11 * 16384]
                              .rearrange("(it p i) -> p it i", it=11, i=128),
                              qo.rearrange("p (it i) -> p it i", i=128),
                              waits=[("ve", vq)], slot="agq"))
        row = self.fv("xp", 0, 1, 0, 128)
        dd = self.dma("sync", row, g["ag0_in"][0:1, 2240:2368],
                      waits=[("ve", vq), prevrec], slot="xprow")
        self.op("vector", lambda e, r=row: e.reciprocal(r, r), waits=[dd], sem="ve")
        v = self.op("vector", lambda e, r=row: e.tensor_scalar_mul(r, r, 127.0),
                    sem="ve")
        d9 = self.cdma("sync", g["rows_hbm"][10:11, 0:128], row,
                      waits=[("ve", v)], slot="c4")
        rp = self.fv("wqrep", 0, 128, 0, 128)
        d2 = self.dma("sync", rp, g["rows_hbm"][10:11, 0:128].partition_broadcast(128),
                      waits=[d9, ("ve", vq)], slot="wqrep")
        for hv in range(2):
            st = self.fv("wqstage", 0, 128, 0, HF)
            d1 = self.dma("sync", st.rearrange("p (it h) -> p it h", h=128),
                          g["wdT_sh"][hv * HF:(hv + 1) * HF, :]
                          .rearrange("(it p) h -> p it h", p=128),
                          waits=[("ve", vq)], slot="wqstage")
            qo = self.bvv("wqout", 0, 128, 0, HF)
            self.op("vector", lambda e, a=st, bq=rp: e.tensor_tensor(
                a.rearrange("p (it h) -> p it h", h=128),
                a.rearrange("p (it h) -> p it h", h=128),
                bq.unsqueeze(1).broadcast_to([128, 11, 128]), ALU.mult),
                waits=[d1, d2, ("D_agq", self.now("D_agq"))], sem="ve")
            vq = self.op("vector", lambda e, a=st, q=qo: e.tensor_scalar(
                q, a, MAGIC, MAGIC, ALU.add, ALU.subtract), sem="ve")
            dq_out.append(self.dma("sync",
                          g["agq_in"][0, 2 * 128 * I + hv * 11 * 16384:
                                      2 * 128 * I + (hv + 1) * 11 * 16384]
                          .rearrange("(it p h) -> p it h", it=11, h=128),
                          qo.rearrange("p (it h) -> p it h", h=128),
                          waits=[("ve", vq)], slot="agq"))
        self.op("gpsimd", lambda e: e.collective_compute(
            "AllGather", ALU.bypass, replica_groups=[list(range(NCORE))],
            ins=[g["agq_in"][:, :]], outs=[g["agq_out"][:, :]]),
            waits=[("D_agq", self.now("D_agq"))], sem="cc")
        self.cc_agq = self.now("cc")

    def p1_ln1(self):
        g = self.g
        sums = self.fv("sums", 0, 128, 0, 17)
        sums2 = self.fv("sums", 0, 128, 18, 17)
        for i in range(TT):
            xt = self.fv("xrot", 0, 128, (i % 2) * 1032, 1024)
            w = []
            if i >= 2:
                w = [("ve", self.ve_x_used[i - 2]), ("se", self.se_x_used[i - 2])]
            d = self.dma("sync", xt, g["x_tm"][3 + i * 128: 3 + (i + 1) * 128, :],
                         waits=w, slot="xr%d" % (i % 2))
            buf = self.fv("tmp1", 0, 128, 0, 1024)
            if i == 0:
                self.se_x_used, self.ve_x_used, self.ve_sq = {}, {}, {}
            sqw = [d] + ([("ve", self.ve_sq[i - 1])] if i >= 1 else [])
            s = self.op("scalar", lambda e, x=xt, q=buf: e.activation(
                q, x, AF.Square, bias=0.0, scale=1.0), waits=sqw, sem="se")
            self.se_x_used[i] = s
            self.op("vector", lambda e, x=xt, i=i: e.reduce_sum(
                sums[:, i:i + 1], x, AX.X), waits=[d], sem="ve")
            self.ve_x_used[i] = self.now("ve")
            self.op("vector", lambda e, q=buf, i=i: e.reduce_sum(
                sums2[:, i:i + 1], q, AX.X), waits=[("se", s)], sem="ve")
            self.ve_sq[i] = self.now("ve")
        vz = self.op("vector", lambda e: e.memset(sums[:, 16:17], 0.0), sem="ve")
        vz = self.op("vector", lambda e: e.memset(sums2[:, 16:17], 0.0), sem="ve")
        xh = self.fv("xhalo", 0, 3, 0, 1024)
        dh = self.dma("sync", xh, g["x_tm"][0:3, :], slot="xhalo")
        sqh = self.fv("tmp1", 0, 3, 0, 1024)
        s = self.op("scalar", lambda e: e.activation(sqh, xh, AF.Square,
                                                     bias=0.0, scale=1.0),
                    waits=[dh, ("ve", self.now("ve"))], sem="se")
        self.op("vector", lambda e: e.reduce_sum(sums[0:3, 16:17], xh, AX.X),
                waits=[dh], sem="ve")
        self.op("vector", lambda e: e.reduce_sum(sums2[0:3, 16:17], sqh, AX.X),
                waits=[("se", s)], sem="ve")
        vstats = self.now("ve")
        m = self.fv("cmb", 0, 128, 0, 17)
        var = self.fv("cmb", 0, 128, 18, 17)
        mm = self.fv("cmb", 0, 128, 36, 17)
        var2 = self.fv("cmb", 0, 128, 54, 17)
        sd = self.fv("cmb", 0, 128, 72, 17)
        a1 = self.fv("ab1", 0, 128, 0, 17)
        b1 = self.fv("ab1", 0, 128, 18, 17)
        self.op("vector", lambda e: e.tensor_scalar_mul(m, sums, 1.0 / D),
                waits=[("ve", vstats)], sem="ve")
        self.op("vector", lambda e: e.tensor_scalar(var, sums2, 1.0 / D, EPS,
                                                    ALU.mult, ALU.add), sem="ve")
        self.op("vector", lambda e: e.tensor_mul(mm, m, m), sem="ve")
        self.op("vector", lambda e: e.tensor_sub(var2, var, mm), sem="ve")
        sv = self.op("scalar", lambda e: e.activation(sd, var2, AF.Sqrt,
                                                      bias=0.0, scale=1.0),
                     waits=[("ve", self.now("ve"))], sem="se")
        self.op("vector", lambda e: e.reciprocal(a1, sd),
                waits=[("se", sv)], sem="ve")
        self.op("vector", lambda e: e.tensor_mul(b1, m, a1), sem="ve")
        self.op("vector", lambda e: e.tensor_scalar_mul(b1, b1, -1.0), sem="ve")
        self.ve_ab1 = self.now("ve")

    def p1_affine(self):
        g = self.g
        a1 = self.fv("ab1", 0, 128, 0, 17)
        b1 = self.fv("ab1", 0, 128, 18, 17)
        ps7b = self.ps[7][:, :].bitcast(BF16)
        xaffb = self.bvv("xaff", 0, 128, 0, 2048)
        h1 = self.bvv("h1T", 0, 128, 0, 8 * H1S)
        dre = {}
        self.se_aff, self.pe_tp = {}, {}
        for i in range(TT + 1):
            if i < TT:
                xt = self.fv("xrot", 0, 128, (i % 2) * 1032, 1024)
                w = ([("ve", self.ve_ab1), ("se", self.now("se"))] if i < 2
                     else [("se", self.se_aff[i - 2])])
                dre[i] = self.dma("sync", xt,
                                  g["x_tm"][3 + i * 128: 3 + (i + 1) * 128, :],
                                  waits=w, slot="xr%d" % (i % 2))
                pn, colw = 128, 128
            else:
                xt = self.fv("xhalo", 0, 3, 0, 1024)
                pn, colw = 3, 3
            xafft = xaffb[0:pn, (i % 2) * 1024:(i % 2) * 1024 + 1024]
            sa_ = self.op("scalar", lambda e, x=xt, i=i, xa=xafft, pn=pn: e.activation(
                xa, x, AF.Identity,
                bias=b1[0:pn, i:i + 1], scale=a1[0:pn, i:i + 1]),
                waits=[("ve", self.ve_ab1)] + ([dre[i]] if i < TT else []) +
                      ([("pe", self.pe_tp[i - 2])] if i >= 2 else []),
                sem="se")
            self.se_aff[i] = sa_
            for dt in range(8):
                self.op("tensor", lambda e, dt=dt, pn=pn, colw=colw, xa=xafft: e.transpose(
                    ps7b[:, dt * 128: dt * 128 + colw],
                    xa[0:pn, dt * 128: dt * 128 + 128],
                    self.sid[0:pn, 0:pn]),
                    waits=(([("se", sa_), ("se2", self.now("se2"))] +
                            ([self.ld["sid"]] if i == 0 else [])) if dt == 0 else []),
                    sem="pe" if dt == 7 else None)
            pe = self.now("pe")
            self.pe_tp[i] = pe
            for dt in range(8):
                col0 = (3 + i * 128) if i < TT else 0
                self.op("scalar", lambda e, dt=dt, col0=col0, colw=colw: e.activation(
                    h1[:, dt * H1S + col0: dt * H1S + col0 + colw],
                    ps7b[:, dt * 128: dt * 128 + colw], AF.Identity,
                    bias=self.fv("mod_cols", 0, 128, 0 + dt, 1),
                    scale=self.fv("modx", 0, 128, 0 + dt, 1)),
                    waits=([("pe", pe), ("ve", self.ve_mod)] if dt == 0 else []),
                    sem="se2")
        hl = self.fv("sc", 0, 128, 4, 1)
        vh = self.op("vector", lambda e: e.tensor_scalar(
            h1.rearrange("p (dt t) -> p dt t", t=H1S)[:, :, 0:3],
            h1.rearrange("p (dt t) -> p dt t", t=H1S)[:, :, 0:3],
            hl, None, ALU.mult),
            waits=[("se2", self.now("se2")), self.ld["halo"]], sem="ve")
        self.ve_h1T = self.now("ve")
        self.se2_h1T = self.now("se2")
        if "h1T_bf" in self.dbgo:
            self.dma("sync", self.dbgo["h1T_bf"][:, :]
                     .rearrange("(dt p) t -> p dt t", p=128),
                     h1.rearrange("p (dt t) -> p dt t", t=H1S)[:, :, 0:TH],
                     waits=[("ve", self.ve_h1T), ("se2", self.now("se2"))],
                     slot="dbg")

    def p4_mixer_in(self):
        g = self.g
        h1 = self.bvv("h1T", 0, 128, 0, 8 * H1S)
        xcins = [self.fv("xcin", 0, 128, 0, 2051),
                 self.fv("xcin2", 0, 128, 0, 2051)]
        xc0 = self.fv("xc0", 0, 128, 0, 2048)
        xcsl = self.bvv("xcsl", 0, 128, 0, 2048)
        xpwb = self.bvv("xpwb", 0, 128, 0, 16 * 34)
        ps6b = self.ps[6][:, :].bitcast(BF16)
        chunks = [(0, 512), (512, 512), (1024, 512), (1536, 512), (2048, 3)]
        wx_used = {}
        evac_ve = {}
        se_tap0 = None
        se_silu = {}
        ve_conv = {}
        pe_cons = {}
        se2_tp = None
        for es in range(16):
            half = es % 2
            xcin = xcins[half]
            wr = self.bvv("wrot", 0, 128, half * 1088, 1024)
            w = [("pe", wx_used[es - 2])] if es >= 2 else []
            dwx = self.dma("gpsimd", wr.rearrange("p (K j) -> p K j", j=128),
                           g["wxc_t"][es].transpose([1, 0, 2]),
                           waits=w, slot="wx%d" % half)
            for ci, (c0, n) in enumerate(chunks):
                bank = ci % 2
                for K in range(8):
                    wts = []
                    if K == 0:
                        wts = [dwx]
                        if (es, ci) != (0, 0) and bank in evac_ve:
                            wts.append(("se", evac_ve[bank]))
                    self.op("tensor", lambda e, K=K, c0=c0, n=n, bank=bank, wr=wr:
                            e.matmul(self.ps[bank][:, 0:n],
                                     wr[:, K * 128:(K + 1) * 128],
                                     h1[:, K * H1S + c0: K * H1S + c0 + n],
                                     start=(K == 0), stop=(K == 7)),
                            waits=wts, sem="pe" if K == 7 else None)
                pe = self.now("pe")
                evw = [("pe", pe)]
                if ci == 0 and es >= 2:
                    evw.append(("ve", ve_conv[es - 2]))
                self.op("scalar", lambda e, c0=c0, n=n, bank=bank, xcin=xcin:
                        e.activation(xcin[:, c0:c0 + n], self.ps[bank][:, 0:n],
                                     AF.Identity, bias=0.0, scale=1.0),
                        waits=evw, sem="se")
                evac_ve[bank] = self.now("se")
            wx_used[es] = self.now("pe")
            cw = self.fv("convw", 0, 128, 0, 64)
            cb = self.fv("convw", 0, 128, 64, 16)
            t0w = [("se", self.now("se"))]
            if es == 0:
                t0w += [self.ld["convw"], self.ld["convb"]]
            se_tap0 = self.op("scalar", lambda e, es=es, xcin=xcin: e.activation(
                xc0, xcin[:, 0:2048], AF.Identity,
                bias=cb[:, es:es + 1], scale=cw[:, es * 4: es * 4 + 1]),
                waits=t0w, sem="se")
            for k in (1, 2, 3):
                tw = [("se", se_tap0)] if k == 1 else []
                self.op("vector", lambda e, k=k, es=es, xcin=xcin:
                        e.scalar_tensor_tensor(
                    xc0, xcin[:, k:2048 + k], cw[:, es * 4 + k: es * 4 + k + 1],
                    xc0, ALU.mult, ALU.add), waits=tw, sem="ve")
            ve_conv[es] = self.now("ve")
            vconv = self.now("ve")
            sw = [("ve", vconv)]
            if es > 0:
                sw.append(("pe", pe_cons[es - 1]))
            se_silu[es] = self.op("scalar", lambda e: e.activation(
                xcsl, xc0, AF.Silu, bias=0.0, scale=1.0), waits=sw, sem="se")
            for n4 in range(4):
                self.op("tensor", lambda e, n4=n4, es=es: e.matmul(
                    self.ps[2 + n4][0:33, 0:512],
                    xpwb[:, es * 34: es * 34 + 33],
                    xcsl[:, n4 * 512:(n4 + 1) * 512],
                    start=(es == 0), stop=(es == 15)),
                    waits=([("se", se_silu[es])] +
                           ([self.ld["xpwb"]] if es == 0 else []))
                    if n4 == 0 else [], sem=None)
            xcs = self.bvv("xcsTM", 0, 128, 0, 16 * 2048)
            for b2 in range(2):
                for ti in range(8):
                    tt = b2 * 8 + ti
                    wts = []
                    if ti == 0:
                        wts = [("se", se_silu[es])]
                        if se2_tp is not None:
                            wts.append(("se2", se2_tp))
                    self.op("tensor", lambda e, tt=tt, ti=ti: e.transpose(
                        ps6b[:, ti * 128:(ti + 1) * 128],
                        xcsl[:, tt * 128:(tt + 1) * 128],
                        self.sid[:, :]),
                        waits=wts, sem="pe" if ti == 7 else None)
                pe = self.now("pe")
                self.op("scalar", lambda e, b2=b2, es=es: e.activation(
                    xcs.rearrange("p (ch t) -> p ch t", t=2048)
                    [:, b2 * 8:(b2 + 1) * 8, es * 128:(es + 1) * 128],
                    ps6b[:, 0:1024].rearrange("p (ti t) -> p ti t", t=128),
                    AF.Identity, bias=0.0, scale=1.0),
                    waits=[("pe", pe)], sem="se2")
                se2_tp = self.now("se2")
            pe_cons[es] = self.now("pe")
        pe = self.now("pe")
        for n4 in range(4):
            self.op("vector", lambda e, n4=n4: e.tensor_copy(
                self.fv("xp", 0, 33, n4 * 512, 512), self.ps[2 + n4][0:33, 0:512]),
                waits=[("pe", pe)] if n4 == 0 else [], sem="ve")
        self.ve_xp = self.now("ve")
        self.se2_xcs = self.now("se2")
        if "xp" in self.dbgo:
            self.dma("sync", self.dbgo["xp"][:, :], self.fv("xp", 0, 33, 0, 2048),
                     waits=[("ve", self.ve_xp)], slot="dbg")
        if "xcs_bf" in self.dbgo:
            self.dma("sync", self.dbgo["xcs_bf"][:, :],
                     self.bvv("xcsTM", 0, 128, 0, 16 * 2048),
                     waits=[("se2", self.se2_xcs)], slot="dbg2")

    def p5a_smalls(self):
        g = self.g
        comb = self.fv("comb", 0, 34, 0, 2048)
        d1 = self.cdma("sync", self.fv("comb", 0, 16, 0, 2048),
                      self.fv("xp", 17, 16, 0, 2048),
                      waits=[("ve", self.ve_xp)], slot="c2")
        d2 = self.cdma("sync", self.fv("comb", 16, 16, 0, 2048),
                      self.fv("xp", 1, 16, 0, 2048),
                      waits=[("ve", self.ve_xp)], slot="c2")
        x0 = self.fv("e3t", 64, 1, 0, 2048)
        dx0 = self.cdma("sync", x0, self.fv("xp", 0, 1, 0, 2048),
                       waits=[("ve", self.ve_xp)], slot="c2")
        sc1 = self.fv("e3u", 64, 1, 0, 2048)
        sc2 = self.fv("repA", 64, 1, 0, 2048)
        self.op("vector", lambda e: e.tensor_scalar(sc1, x0, -1.0, None,
                                                    ALU.mult),
                waits=[dx0], sem="ve")
        self.op("vector", lambda e: e.tensor_tensor(sc1, x0, sc1, ALU.min),
                sem="ve")
        sp = self.op("scalar", lambda e: e.activation(sc1, sc1, AF.Exp,
                                                      bias=0.0, scale=1.0),
                     waits=[("ve", self.now("ve"))], sem="se")
        sp = self.op("scalar", lambda e: e.activation(sc1, sc1, AF.Ln,
                                                      bias=1.0, scale=1.0), sem="se")
        self.op("vector", lambda e: e.tensor_scalar_max(sc2, x0, 0.0),
                waits=[("se", sp)], sem="ve")
        vdl = self.op("vector", lambda e: e.tensor_add(sc1, sc1, sc2), sem="ve")
        ddl = self.cdma("sync", self.fv("comb", 32, 1, 0, 2048), sc1,
                       waits=[("ve", vdl)], slot="c2")
        ones = self.fv("e3t", 64, 1, 0, 2048)
        self.op("vector", lambda e: e.memset(ones, 1.0), sem="ve")
        rgc = self.fv("repA", 64, 1, 0, 2048)
        vrg = self.op("vector", lambda e: e.tensor_tensor_scan(
            rgc, ones, sc1, 0.0, ALU.mult, ALU.add),
            waits=[("ve", self.now("ve"))], sem="ve")
        drg = self.cdma("sync", g["rows_hbm"][12:13, 0:2048], rgc,
                       waits=[("ve", vrg)], slot="c2")
        drc2 = self.cdma("sync", self.fv("comb", 33, 1, 0, 2048), rgc,
                        waits=[("ve", vrg)], slot="c2")
        vcb = self.now("ve")
        xpTc = self.fv("xpTc", 0, 128, 0, 16 * 34)
        self.se2_xpT = {}
        for i in range(NCH):
            bank = 6 + (i % 2)
            wts = ([("ve", vcb), d1, d2, ddl, drc2, self.ld["sidf"]]
                   if i < 2 else [("se2", self.se2_xpT[i - 2])])
            self.op("tensor", lambda e, i=i, bank=bank: e.transpose(
                self.ps[bank][:, 0:34],
                comb[:, i * 128:(i + 1) * 128], self.sidf[0:34, 0:34]),
                waits=wts, sem="pe")
            pe = self.now("pe")
            self.op("scalar", lambda e, i=i, bank=bank: e.activation(
                xpTc[:, i * 34:(i + 1) * 34], self.ps[bank][:, 0:34],
                AF.Identity, bias=0.0, scale=1.0),
                waits=[("pe", pe)], sem="se2")
            self.se2_xpT[i] = self.now("se2")
        st = self.fv("cmb", 0, 1, 0, 48)
        self.op("vector", lambda e: e.memset(st[:, 0:1], 0.0), sem="ve")
        dbe = self.cdma("sync", st[:, 16:32],
                       g["rows_hbm"][12, 127:2048:128].rearrange("(a c) -> a c", a=1),
                       waits=[drg], slow=True, slot="c3")
        dbs = self.cdma("sync", st[:, 1:16],
                       g["rows_hbm"][12, 127:1920:128].rearrange("(a c) -> a c", a=1),
                       waits=[drg], slow=True, slot="c3")
        vb = self.op("vector", lambda e: e.tensor_sub(
            st[:, 32:48], st[:, 16:32], st[:, 0:16]),
            waits=[dbe, dbs, ("ve", self.now("ve"))], sem="ve")
        db = self.cdma("sync", g["rows_hbm"][11:12, 0:48], st,
                      waits=[("ve", vb)], slot="c3")
        dbst = self.cdma("sync", self.fv("cmb", 0, 128, 48, 16),
                        g["rows_hbm"][11:12, 0:16].partition_broadcast(128),
                        waits=[db], slot="c3")
        dbnd = self.cdma("sync", self.fv("cmb", 0, 128, 64, 16),
                        g["rows_hbm"][11, 16:32].rearrange("(a c) -> a c", a=1)
                        .partition_broadcast(128),
                        waits=[db], slot="c3")
        drr = self.cdma("sync", self.fv("rows16", 0, 16, 0, 16),
                       g["rows_hbm"][11, 32:48].rearrange("(a c) -> a c", a=1)
                       .partition_broadcast(16),
                       waits=[db], slot="c3")
        dec = self.fv("rows16", 0, 16, 16, 16)
        self.op("scalar", lambda e: e.activation(
            dec, self.fv("rows16", 0, 16, 0, 16), AF.Exp, bias=0.0,
            scale=self.fv("acol", 0, 16, 0, 1)),
            waits=[drr, self.ld["acol"]], sem="se")
        self.se_dec = self.now("se")
        self.d_bst128 = dbst
        self.d_bnd128 = dbnd
        vprev = None
        pka = None
        for i in range(NCH):
            sl = self.fv("rch", 0, 16, 0, 128)
            w = [drg, dbst, self.ld["acol"]]
            if vprev is not None:
                w.append(("ve", vprev))
            if pka is not None:
                w.append(pka)
            dsl = self.dma("sync", sl,
                           g["rows_hbm"][12, i * 128:(i + 1) * 128]
                           .rearrange("(a c) -> a c", a=1)
                           .partition_broadcast(16),
                           waits=w, slot="rch")
            self.op("vector", lambda e, sl=sl, i=i: e.tensor_scalar(
                sl, sl, self.fv("cmb", 0, 16, 48 + i, 1),
                None, ALU.subtract), waits=[dsl], sem="ve")
            vprev = self.op("vector", lambda e, sl=sl: e.tensor_scalar(
                sl, sl, self.fv("acol", 0, 16, 0, 1),
                None, ALU.mult), sem="ve")
            pka = self.dma("sync", g["krow_hbm"][i, 0], sl,
                           waits=[("ve", vprev)], slot="karow")
            self.dma("sync", g["krow_hbm"][i, 1],
                     self.fv("comb", 0, 16, i * 128, 128),
                     waits=[d1], slot="kcrow")
        self.ve_p5a = self.now("ve")

    def p5b_z(self):
        g = self.g
        h1 = self.bvv("h1T", 0, 128, 0, 8 * H1S)
        evac_se = {}
        dst_done = {}
        wz_used = {}
        for j4 in range(4):
            half = j4 % 2
            wz = self.bvv("wzrot", 0, 128, half * 4104, 4096)
            w = [("pe", wz_used[j4 - 2])] if j4 >= 2 else []
            dwz = self.dma("gpsimd", wz.rearrange("p (K j) -> p K j", j=512),
                           g["wz_t"][:, :, j4 * 512:(j4 + 1) * 512]
                           .transpose([1, 0, 2]),
                           waits=w, slot="wz%d" % half)
            for i in range(NCH):
                bank = i % 2
                for K in range(8):
                    wts = []
                    if K == 0:
                        wts = [dwz]
                        key = (bank,)
                        if key in evac_se:
                            wts.append(("se", evac_se[key]))
                    self.op("tensor", lambda e, K=K, i=i, bank=bank, wz=wz:
                            e.matmul(self.ps[bank][:, 0:512],
                                     h1[:, K * H1S + 3 + i * 128:
                                        K * H1S + 3 + (i + 1) * 128],
                                     wz[:, K * 512:(K + 1) * 512],
                                     start=(K == 0), stop=(K == 7)),
                            waits=wts, sem="pe" if K == 7 else None)
                pe = self.now("pe")
                stg = self.fv("xrl", 0, 128, bank * 256, 256).bitcast(BF16)
                sw = [("pe", pe)]
                key2 = (bank, "d")
                if key2 in dst_done:
                    sw.append(dst_done[key2])
                self.op("scalar", lambda e, stg=stg, bank=bank: e.activation(
                    stg, self.ps[bank][:, 0:512], AF.Silu, bias=0.0, scale=1.0),
                    waits=sw, sem="se")
                evac_se[(bank,)] = self.now("se")
                dst_done[(bank, "d")] = self.dma(
                    "sync", g["sz_hbm"][i, :, j4 * 512:(j4 + 1) * 512], stg,
                    waits=[("se", evac_se[(bank,)])], slot="szst%d" % bank)
            wz_used[j4] = self.now("pe")
        self.pe_z_done = self.now("pe")
        self.d_szst_full = [("D_szst0", self.now("D_szst0")),
                            ("D_szst1", self.now("D_szst1"))]

    def p5c_loopA(self):
        g = self.g
        xcs = self.bvv("xcsTM", 0, 128, 0, 16 * 2048)
        xpTc = self.fv("xpTc", 0, 128, 0, 16 * 34)
        hloc = self.fv("hstate", 0, 16, 0, 2048)
        dec = self.fv("rows16", 0, 16, 16, 16)
        vmz = self.op("vector", lambda e: e.memset(hloc, 0.0), sem="ve")
        rec_ve = {}
        for i in range(NCH):
            dend = self.fv("cmb", 0, 128, 80, 1)
            wts = [self.d_bnd128, self.ld["arep"]] if i == 0 else []
            wts = wts + [("se2", self.se2_xpT[i])]
            self.op("vector", lambda e, i=i: e.tensor_scalar(
                dend, self.fv("cmb", 0, 128, 64 + i, 1),
                xpTc[:, i * 34 + 33: i * 34 + 34], None, ALU.subtract),
                waits=wts, sem="ve")
            wet = self.fv("cmb", 0, 128, 82, 16)
            sw = self.op("scalar", lambda e: e.activation(
                wet, self.fv("arep"), AF.Exp, bias=0.0, scale=dend),
                waits=[("ve", self.now("ve"))], sem="se")
            dbc = self.fv("cmb", 0, 128, 116, 8).bitcast(BF16)
            self.op("vector", lambda e, i=i: e.tensor_scalar(
                dbc, xpTc[:, i * 34 + 16: i * 34 + 32],
                xpTc[:, i * 34 + 32: i * 34 + 33], None, ALU.mult), sem="ve")
            ub = self.bvv("ubufs", 0, 128, i * 16, 16)
            vu = self.op("vector", lambda e, ub=ub: e.tensor_mul(ub, dbc, wet),
                         waits=[("se", sw)], sem="ve")
            for n4 in range(4):
                bank = 6 + (n4 % 2)
                wts = [("ve", vu)] if n4 == 0 else []
                if (i, n4) == (0, 0):
                    wts.append(("se2", self.se2_xpT[NCH - 1]))
                if (i, n4) >= (0, 2) and (bank, "rec") in rec_ve:
                    wts.append(("ve", rec_ve[(bank, "rec")]))
                self.op("tensor", lambda e, n4=n4, i=i, bank=bank, ub=ub: e.matmul(
                    self.ps[bank][0:16, 0:512], ub,
                    xcs[:, i * 2048 + n4 * 512: i * 2048 + (n4 + 1) * 512],
                    start=True, stop=True),
                    waits=wts, sem="pe")
                pe = self.now("pe")
                self.op("vector", lambda e, n4=n4, i=i, bank=bank:
                        e.scalar_tensor_tensor(
                            hloc[:, n4 * 512:(n4 + 1) * 512],
                            hloc[:, n4 * 512:(n4 + 1) * 512],
                            dec[:, i:i + 1],
                            self.ps[bank][0:16, 0:512], ALU.mult, ALU.add),
                        waits=[("pe", pe)] + ([("ve", vmz), ("se", self.se_dec)]
                                              if (i, n4) == (0, 0) else []),
                        sem="ve")
                rec_ve[(bank, "rec")] = self.now("ve")
        vrec = self.now("ve")
        d1 = self.cdma("sync", g["ag1_in"][0, 0:S * E]
                      .rearrange("(p c) -> p c", p=16), hloc,
                      waits=[("ve", vrec)], slot="c7")
        st2 = self.fv("cmb", 0, 1, 0, 16)
        self.op("vector", lambda e: e.memset(st2, 0.0), sem="ve")
        vr = self.op("vector", lambda e: e.tensor_copy(
            st2[:, 0:1], self.fv("cmb", 0, 1, 31, 1)),
            waits=[("D_c3", self.now("D_c3"))], sem="ve")
        d2 = self.cdma("sync", g["ag1_in"][0:1, S * E:S * E + 16], st2,
                      waits=[("ve", vr)], slot="c7")
        self.op("gpsimd", lambda e: e.collective_compute(
            "AllGather", ALU.bypass, replica_groups=[list(range(NCORE))],
            ins=[g["ag1_in"][:, :]], outs=[g["ag1_out"][:, :]]),
            waits=[d1, d2], sem="cc")
        self.cc_ag1 = self.now("cc")
        self.d_ag1send = d1

    def p5d_hin(self):
        g = self.g
        r16 = lambda p0, pn, o, n: self.fv("rows16", p0, pn, o, n)
        dom = self.cdma("sync", r16(0, 8, 32, 8), g["ordmask_t"][:, :], slot="c5")
        drc = self.cdma("sync", r16(0, 8, 40, 1),
                       g["ag1_out"][:, S * E:S * E + 1],
                       waits=[("cc", self.cc_ag1)], slow=True, slot="c5")
        self.op("tensor", lambda e: e.matmul(
            self.ps[5][0:8, 0:1], r16(0, 8, 32, 8), r16(0, 8, 40, 1),
            start=True, stop=True), waits=[dom, drc], sem="pe")
        pe = self.now("pe")
        vs = self.op("vector", lambda e: e.tensor_copy(
            r16(0, 8, 42, 1), self.ps[5][0:8, 0:1]), waits=[("pe", pe)], sem="ve")
        ds = self.cdma("sync", g["rows_hbm"][13, 0:8].rearrange("(c a) -> c a", a=1),
                      r16(0, 8, 42, 1), waits=[("ve", vs)], slow=True, slot="c5")
        dsr = self.cdma("sync", r16(0, 16, 44, 8),
                       g["rows_hbm"][13:14, 0:8].partition_broadcast(16),
                       waits=[ds], slot="c5")
        dpm = self.cdma("sync", r16(0, 16, 52, 8),
                       g["pmask"][0:1, :].partition_broadcast(16), slot="c5")
        sc = self.op("scalar", lambda e: e.activation(
            r16(0, 16, 60, 8), r16(0, 16, 44, 8), AF.Exp, bias=0.0,
            scale=self.fv("acol", 0, 16, 0, 1)),
            waits=[dsr, self.ld["acol"]], sem="se")
        vc = self.op("vector", lambda e: e.tensor_mul(
            r16(0, 16, 60, 8), r16(0, 16, 60, 8), r16(0, 16, 52, 8)),
            waits=[("se", sc), dpm], sem="ve")
        hin = self.fv("hstate", 0, 16, 0, 2048)
        vmz = self.op("vector", lambda e: e.memset(hin, 0.0),
                      waits=[self.d_ag1send], sem="ve")
        vprev = None
        for j in range(NCORE):
            half = j % 2
            sl = self.fv("xp" if half == 0 else "xcin2", 0, 16, 0, 2048)
            w = [("cc", self.cc_ag1)]
            if j == 1:
                w += self.d_szst_full
            if j >= 2:
                w.append(("ve", vprev))
            dj = self.dma("sync", sl, g["ag1_out"][j, 0:S * E]
                          .rearrange("(p c) -> p c", p=16),
                          waits=w, slot="hj%d" % half)
            vprev = self.op("vector", lambda e, sl=sl, j=j: e.scalar_tensor_tensor(
                hin, sl, r16(0, 16, 60 + j, 1), hin, ALU.mult, ALU.add),
                waits=[dj] + ([("ve", vc), ("ve", vmz)] if j == 0 else []),
                sem="ve")
        hb = self.bvv("hinbf", 0, 16, 0, 2048)
        self.op("vector", lambda e: e.tensor_copy(hb, hin), sem="ve")
        self.ve_hin = self.now("ve")

    def p5e_pre(self):
        g = self.g
        wot = self.bvv("wot", 0, 128, 0, 16 * 1024)
        dws = []
        for K in range(16):
            dws.append(self.dma("gpsimd", wot[:, K * 1024:(K + 1) * 1024],
                                g["wout_t"][K], waits=[("pe", self.pe_z_done)]
                                if K == 0 else [], slot="wotl"))
        mx = self.fv("modx")
        dga = self.cdma("sync", g["rows_hbm"][14, 0:1024]
                       .rearrange("(dt p) -> p dt", p=128), mx[:, 16:24],
                       waits=[("ve", self.ve_mod)], slow=True, slot="c6")
        dgr = self.cdma("sync", self.fv("garepf", 0, 128, 0, 1024),
                       g["rows_hbm"][14:15, 0:1024].partition_broadcast(128),
                       waits=[dga], slot="c6")
        gab = self.bvv("garep", 0, 128, 0, 1024)
        vg = self.op("vector", lambda e: e.tensor_copy(
            gab, self.fv("garepf", 0, 128, 0, 1024)), waits=[dgr], sem="ve")
        wfull = ("D_wotl", self.now("D_wotl"))
        for K in range(16):
            self.op("vector", lambda e, K=K: e.tensor_mul(
                wot[:, K * 1024:(K + 1) * 1024],
                wot[:, K * 1024:(K + 1) * 1024], gab),
                waits=[wfull, ("ve", vg)] if K == 0 else [], sem="ve")
        self.ve_wot = self.now("ve")
        self.d_drep = self.dma("gpsimd", self.bvv("drep", 0, 128, 0, 2048),
                               g["dparam"][0:1, :].partition_broadcast(128),
                               waits=[dgr, ("ve", vg)], slot="drep")

    def p5e_loopB(self):
        g = self.g
        xcs = self.bvv("xcsTM", 0, 128, 0, 16 * 2048)
        xpTc = self.fv("xpTc", 0, 128, 0, 16 * 34)
        hloc = self.fv("hstate", 0, 16, 0, 2048)
        dec = self.fv("rows16", 0, 16, 16, 16)
        repA = self.fv("repA", 0, 128, 0, 2048)
        repC = self.bvv("repC", 0, 128, 0, 2048)
        e3t = self.fv("e3t", 0, 128, 0, 2048)
        e3u = self.fv("e3u", 0, 128, 0, 2048)
        wot = self.bvv("wot", 0, 128, 0, 16 * 1024)
        hb = self.bvv("hinbf", 0, 16, 0, 2048)
        drep = self.bvv("drep", 0, 128, 0, 2048)
        ps6b = self.ps[6][:, :].bitcast(BF16)
        ps7b = self.ps[7][:, :].bitcast(BF16)
        vmz = None
        st = {}
        ctx = {}
        hsns = {}

        def emit_hsn(i):
            half = i % 2
            hsn = self.bvv("hsn0" if half == 0 else "hsn1", 0, 16, 0, 2048)
            w = [("ve", vmz)] if i == 0 else [("ve", st[("recb", i - 1, 3)])]
            if ("ymm", i - 2) in st:
                w.append(("pe", st[("ymm", i - 2)]))
            vsn = self.op("scalar", lambda e, hsn=hsn: e.activation(
                hsn, hloc, AF.Identity, bias=0.0, scale=1.0),
                waits=w, sem="se")
            hsns[i] = (hsn, vsn)

        def stageA(i):
            half = i % 2
            if i >= 2:
                emit_hsn(i - 1)
            dsz = self.dma("sync", self.bvv("szrot", 0, 128, half * 2052, 2048),
                           g["sz_hbm"][i], waits=[st[("y2c", i - 2)]]
                           if ("y2c", i - 2) in st else
                           ([("pe", self.pe_z_done)] if i < 2 else []),
                           slot="szl%d" % half)
            dra = self.dma("sync", repA,
                           g["krow_hbm"][i:i + 1, 0, :].partition_broadcast(128),
                           waits=([("ve", st[("xsub", i - 1)])]
                                  if ("xsub", i - 1) in st else
                                  [("D_karow", self.now("D_karow")),
                                   ("D_c2", self.now("D_c2"))]),
                           slot="repA")
            drc = self.dma("gpsimd", repC,
                           g["krow_hbm"][i:i + 1, 1, :].partition_broadcast(128),
                           waits=([("ve", st[("crep", i - 1)])]
                                  if ("crep", i - 1) in st else
                                  [("D_kcrow", self.now("D_kcrow"))]),
                           slot="repC")
            rlc = self.fv("cmb", 0, 128, 114, 1)
            self.op("vector", lambda e, i=i: e.tensor_scalar(
                rlc, xpTc[:, i * 34 + 33: i * 34 + 34],
                self.fv("cmb", 0, 128, 48 + i, 1), None, ALU.subtract),
                waits=([self.d_bst128] if i == 0 else []) +
                [("se2", self.se2_xpT[i])], sem="ve")
            rA = self.fv("cmb", 0, 128, 82, 16)
            self.op("vector", lambda e: e.tensor_tensor(
                rA, self.fv("arep"), rlc.broadcast_to([128, 16]), ALU.mult),
                sem="ve")
            vx = self.op("vector", lambda e: e.tensor_tensor(
                e3t.rearrange("p (s t) -> p s t", t=128),
                repA.rearrange("p (s t) -> p s t", t=128),
                rA.unsqueeze(2).broadcast_to([128, 16, 128]), ALU.subtract),
                waits=[dra], sem="ve")
            st[("xsub", i)] = self.now("ve")
            self.op("vector", lambda e: e.tensor_scalar_min(e3t, e3t, 0.0),
                    sem="ve")
            sx = self.op("scalar", lambda e: e.activation(
                e3u, e3t, AF.Exp, bias=0.0, scale=1.0),
                waits=[("ve", self.now("ve"))], sem="se")
            dbc = self.fv("cmb", 0, 128, 116, 8).bitcast(BF16)
            self.op("vector", lambda e, i=i: e.tensor_scalar(
                dbc, xpTc[:, i * 34 + 16: i * 34 + 32],
                xpTc[:, i * 34 + 32: i * 34 + 33], None, ALU.mult),
                waits=[("se", sx)], sem="ve")
            self.op("vector", lambda e: e.tensor_tensor(
                e3u.rearrange("p (s t) -> p s t", t=128),
                e3u.rearrange("p (s t) -> p s t", t=128),
                dbc.unsqueeze(2).broadcast_to([128, 16, 128]), ALU.mult),
                sem="ve")
            self.op("vector", lambda e: e.tensor_mul(e3u, e3u, repC),
                    waits=[drc], sem="ve")
            st[("crep", i)] = self.now("ve")
            kbf = self.fv("tmp0", 0, 128, 648, 128)
            self.op("vector", lambda e: e.tensor_reduce(
                kbf, e3u.rearrange("p (s t) -> p t s", t=128), AX.X, ALU.add),
                sem="ve")
            kb = self.bvv("kbslot", 0, 128, half * 128, 128)
            vkb = self.op("vector", lambda e, kb=kb: e.tensor_mul(
                kb, kbf, self.stril[:, :]), waits=[self.ld["tril"]]
                if i == 0 else [], sem="ve")
            st[("kb", i)] = self.now("ve")
            rch = self.fv("rch", 0, 16, 0, 128)
            drch = self.dma("sync", rch,
                            g["rows_hbm"][12, i * 128:(i + 1) * 128]
                            .rearrange("(a c) -> a c", a=1)
                            .partition_broadcast(16),
                            waits=[("se", st[("gexp", i - 1)])]
                            if ("gexp", i - 1) in st else
                            [("ve", self.ve_p5a)], slot="rch")
            eg = self.fv("e3t", 0, 16, 0, 128)
            el = self.fv("e3t", 0, 16, 128, 128)
            sg1 = self.op("scalar", lambda e: e.activation(
                eg, rch, AF.Exp, bias=0.0, scale=self.fv("acol", 0, 16, 0, 1)),
                waits=[drch, ("ve", st[("xsub", i)])], sem="se")
            gsl = self.bvv("gslot", 0, 16, half * 128, 128)
            gglsl = self.bvv("gslot", 0, 16, 256 + half * 128, 128)
            vggl = self.op("vector", lambda e, gglsl=gglsl, i=i: e.tensor_mul(
                gglsl, self.fv("comb", 0, 16, i * 128, 128), eg),
                waits=[("se", sg1)], sem="ve")
            self.op("vector", lambda e, i=i: e.tensor_scalar(
                rch, rch, self.fv("cmb", 0, 16, 48 + i, 1), None, ALU.subtract),
                sem="ve")
            sg2 = self.op("scalar", lambda e: e.activation(
                el, rch, AF.Exp, bias=0.0, scale=self.fv("acol", 0, 16, 0, 1)),
                waits=[("ve", self.now("ve"))], sem="se")
            st[("gexp", i)] = self.now("se")
            vgl = self.op("vector", lambda e, gsl=gsl, i=i: e.tensor_mul(
                gsl, self.fv("comb", 0, 16, i * 128, 128), el),
                waits=[("se", sg2)], sem="ve")
            st[("g", i)] = self.now("ve")
            ctx[i] = (dsz, kb, gsl, gglsl)

        def stageB(i):
            half = i % 2
            dsz, kb, gsl, gglsl = ctx[i]
            if i not in hsns:
                emit_hsn(i)
            hsn, vsn = hsns[i]
            for n4 in range(4):
                bank = 2 + n4
                w1 = [("ve", st[("kb", i)])]
                if ("y2v", i - 1, n4) in st:
                    w1.append(("ve", st[("y2v", i - 1, n4)]))
                self.op("tensor", lambda e, n4=n4, i=i, bank=bank, kb=kb: e.matmul(
                    self.ps[bank][:, 0:512], kb,
                    xcs[:, i * 2048 + n4 * 512: i * 2048 + (n4 + 1) * 512],
                    start=True, stop=False), waits=w1, sem=None)
                self.op("tensor", lambda e, n4=n4, bank=bank, gsl=gsl, hsn=hsn:
                        e.matmul(self.ps[bank][:, 0:512], gsl,
                                 hsn[:, n4 * 512:(n4 + 1) * 512],
                                 start=False, stop=False),
                        waits=[("ve", st[("g", i)]), ("se", vsn)]
                        if n4 == 0 else [], sem=None)
                self.op("tensor", lambda e, n4=n4, bank=bank, gglsl=gglsl:
                        e.matmul(self.ps[bank][:, 0:512], gglsl,
                                 hb[:, n4 * 512:(n4 + 1) * 512],
                                 start=False, stop=True),
                        waits=[("ve", self.ve_hin)] if (i, n4) == (0, 0) else [],
                        sem="pe")
            st[("ymm", i)] = self.now("pe")
            ub = self.bvv("ubufs", 0, 128, i * 16, 16)
            for n4 in range(4):
                bank = 6 + (n4 % 2)
                wts = []
                if ("tp", i - 1) in st and n4 < 2:
                    wts.append(("se2", st[("tp", i - 1)]))
                if ("recb", i, n4 - 2) in st:
                    wts.append(("ve", st[("recb", i, n4 - 2)]))
                self.op("tensor", lambda e, n4=n4, i=i, bank=bank, ub=ub: e.matmul(
                    self.ps[bank][0:16, 0:512], ub,
                    xcs[:, i * 2048 + n4 * 512: i * 2048 + (n4 + 1) * 512],
                    start=True, stop=True), waits=wts, sem="pe")
                pe = self.now("pe")
                self.op("vector", lambda e, n4=n4, i=i, bank=bank:
                        e.scalar_tensor_tensor(
                            hloc[:, n4 * 512:(n4 + 1) * 512],
                            hloc[:, n4 * 512:(n4 + 1) * 512],
                            dec[:, i:i + 1],
                            self.ps[bank][0:16, 0:512], ALU.mult, ALU.add),
                        waits=[("pe", pe), ("se", vsn)], sem="ve")
                st[("recb", i, n4)] = self.now("ve")
            y2sl = self.bvv("y2r", 0, 128, 0, 2048)
            for n4 in range(4):
                bank = 2 + n4
                t1 = self.fv("tmp0", 0, 128, 520, 256).bitcast(BF16)
                t2 = self.fv("tmp1", 0, 128, 520, 512)
                self.op("vector", lambda e, n4=n4, i=i, t1=t1: e.tensor_mul(
                    t1, xcs[:, i * 2048 + n4 * 512: i * 2048 + (n4 + 1) * 512],
                    drep[:, n4 * 512:(n4 + 1) * 512]),
                    waits=[self.d_drep] if (i, n4) == (0, 0) else [], sem="ve")
                self.op("vector", lambda e, bank=bank, t1=t1, t2=t2: e.tensor_add(
                    t2, self.ps[bank][:, 0:512], t1),
                    waits=[("pe", st[("ymm", i)])] if n4 == 0 else [], sem="ve")
                self.op("vector", lambda e, n4=n4, t2=t2, y2sl=y2sl, half=half:
                        e.tensor_mul(y2sl[:, n4 * 512:(n4 + 1) * 512], t2,
                                     self.bvv("szrot", 0, 128, half * 2052, 2048)
                                     [:, n4 * 512:(n4 + 1) * 512]),
                        waits=([dsz] if n4 == 0 else []) +
                        ([self._dbgy] if n4 == 0 and getattr(self, "_dbgy", None)
                         else []), sem="ve")
                st[("y2v", i, n4)] = self.now("ve")
            st[("y2c", i)] = ("ve", self.now("ve"))
            if "y2_bf" in self.dbgo:
                self._dbgy = self.dma("sync", self.dbgo["y2_bf"][i], y2sl,
                                      waits=[st[("y2c", i)]], slot="dbgy")
            for b2 in range(2):
                psb = ps6b if b2 == 0 else ps7b
                for ti in range(8):
                    es = b2 * 8 + ti
                    wts = []
                    if ti == 0:
                        wts = [("ve", st[("y2v", i, 3)]),
                               ("ve", st[("recb", i, 3 if b2 == 0 else 2)])]
                        if ("opj", i - 1) in st:
                            wts.append(("pe", st[("opj", i - 1)]))
                    self.op("tensor", lambda e, es=es, ti=ti, psb=psb, y2sl=y2sl:
                            e.transpose(psb[:, ti * 128:(ti + 1) * 128],
                                        y2sl[:, es * 128:(es + 1) * 128],
                                        self.sid[:, :]),
                            waits=wts, sem="pe" if ti == 7 else None)
                pe = self.now("pe")
                self.op("scalar", lambda e, b2=b2, psb=psb: e.activation(
                    self.bvv("y2f", 0, 128, b2 * 1024, 1024), psb[:, 0:1024],
                    AF.Identity, bias=0.0, scale=1.0),
                    waits=[("pe", pe)], sem="se2")
            st[("tp", i)] = self.now("se2")
            y2f = self.bvv("y2f", 0, 128, 0, 2048)
            dxr = self.dma("sync", self.fv("xrl", 0, 128, 0, 1024),
                           g["x_tm"][3 + i * 128: 3 + (i + 1) * 128, :],
                           waits=[("ve", st[("x2v", i - 1)])]
                           if ("x2v", i - 1) in st else
                           ([("pe", self.pe_z_done), ("ve", self.ve_hin)]
                            + self.d_szst_full if i == 0 else []),
                           slot="xrl")
            for dN in range(2):
                for Ke in range(16):
                    wts = []
                    if Ke == 0:
                        wts = [("se2", st[("tp", i)])]
                        if (i, dN) == (0, 0):
                            wts.append(("ve", self.ve_wot))
                        if ("x2v", i - 1) in st:
                            wts.append(("ve", st[("x2v", i - 1)]))
                    self.op("tensor", lambda e, dN=dN, Ke=Ke, y2f=y2f: e.matmul(
                        self.ps[dN][:, 0:512], y2f[:, Ke * 128:(Ke + 1) * 128],
                        wot[:, Ke * 1024 + dN * 512: Ke * 1024 + dN * 512 + 512],
                        start=(Ke == 0), stop=(Ke == 15)),
                        waits=wts, sem="pe" if Ke == 15 else None)
            st[("opj", i)] = self.now("pe")
            pe = self.now("pe")
            for dN in range(2):
                xo = (self.fv("tmp0", 0, 128, 0, 512) if dN == 0
                      else self.fv("tmp1", 0, 128, 0, 512))
                self.op("vector", lambda e, dN=dN, xo=xo: e.tensor_add(
                    xo, self.fv("xrl", 0, 128, dN * 512, 512),
                    self.ps[dN][:, 0:512]),
                    waits=[("pe", pe), dxr] + ([st[("x2st", i - 1, dN)]]
                                               if ("x2st", i - 1, dN) in st
                                               else []), sem="ve")
                st[("x2v", i)] = self.now("ve")
                st[("x2st", i, dN)] = self.dma(
                    "sync", g["x2_hbm"][i * 128:(i + 1) * 128,
                                        dN * 512:dN * 512 + 512], xo,
                    waits=[("ve", self.now("ve"))], slot="x2st%d" % dN)

        stageA(0)
        stageA(1)
        self.p5d_hin()
        vmz = self.op("vector", lambda e: e.memset(hloc, 0.0),
                      waits=[("ve", self.ve_hin)], sem="ve")
        emit_hsn(0)
        for i in range(NCH):
            stageB(i)
            if i + 2 < NCH:
                stageA(i + 2)
        self.ve_p5 = self.now("ve")
        self.st_p5 = st

    def mband(self, it):
        if it < 16:
            return self.bvv("xcsTM", 0, 128, it * 2048, 2048)
        return self.bvv("R1", 0, 128, (it - 16) * 2048, 2048)

    def p7_ln2(self):
        g = self.g
        h2 = self.fv("h2TM", 0, 128, 0, 16 * 1024)
        mx = self.fv("modx")
        mc = self.fv("mod_cols")
        d1 = self.cdma("sync", g["rows_hbm"][18, 0:1024]
                      .rearrange("(dt p) -> p dt", p=128), mx[:, 8:16],
                      slow=True, slot="c8")
        d2 = self.cdma("sync", g["rows_hbm"][17, 0:1024]
                      .rearrange("(dt p) -> p dt", p=128), mc[:, 24:32],
                      slow=True, slot="c8")
        dcb = self.cdma("sync", self.fv("cbrep", 0, 128, 0, 1024),
                       g["rows_hbm"][18:19, 0:1024].partition_broadcast(128),
                       waits=[d1], slot="c8")
        dsb = self.cdma("sync", self.fv("sbrep", 0, 128, 0, 1024),
                       g["rows_hbm"][17:18, 0:1024].partition_broadcast(128),
                       waits=[d2], slot="c8")
        x2w = [("D_x2st0", self.now("D_x2st0")),
               ("D_x2st1", self.now("D_x2st1"))]
        amc = self.fv("cmb", 0, 128, 0, 1)
        vam = self.op("vector", lambda e: e.memset(amc, 0.0), sem="ve")
        sums = self.fv("cmb", 0, 128, 48, 16)
        sums2 = self.fv("cmb", 0, 128, 64, 16)
        a2c = self.fv("cmb", 0, 128, 80, 16)
        b2c = self.fv("cmb", 0, 128, 96, 16)
        st = {}
        dl = {}
        for ch in range(NCH):
            h2c = h2[:, ch * 1024:(ch + 1) * 1024]
            w = list(x2w) if ch < 2 else [dl[ch - 2]]
            dl[ch] = self.dma("sync", h2c,
                              g["x2_hbm"][ch * 128:(ch + 1) * 128, :],
                              waits=w, slot="xr%d" % (ch % 2))
            sq = self.fv("tmp0", 0, 128, 0, 1024) if ch % 2 == 0 else \
                self.fv("tmp1", 0, 128, 0, 1024)
            sqs = self.op("scalar", lambda e, x=h2c, q=sq: e.activation(
                q, x, AF.Square, bias=0.0, scale=1.0),
                waits=[dl[ch]] + ([("ve", st[("s2", ch - 2)])]
                                  if ch >= 2 else []),
                sem="se")
            self.op("vector", lambda e, x=h2c, ch=ch: e.reduce_sum(
                sums[:, ch:ch + 1], x, AX.X), waits=[dl[ch]], sem="ve")
            self.op("vector", lambda e, q=sq, ch=ch: e.reduce_sum(
                sums2[:, ch:ch + 1], q, AX.X), waits=[("se", sqs)], sem="ve")
            st[("s2", ch)] = self.now("ve")
        self.op("vector", lambda e: e.tensor_scalar_mul(sums, sums, 1.0 / D),
                sem="ve")
        self.op("vector", lambda e: e.tensor_scalar(
            sums2, sums2, 1.0 / D, EPS, ALU.mult, ALU.add), sem="ve")
        self.op("vector", lambda e: e.tensor_mul(b2c, sums, sums), sem="ve")
        self.op("vector", lambda e: e.tensor_sub(sums2, sums2, b2c), sem="ve")
        sv = self.op("scalar", lambda e: e.activation(
            sums2, sums2, AF.Sqrt, bias=0.0, scale=1.0),
            waits=[("ve", self.now("ve"))], sem="se")
        self.op("vector", lambda e: e.reciprocal(a2c, sums2),
                waits=[("se", sv)], sem="ve")
        self.op("vector", lambda e: e.tensor_mul(b2c, sums, a2c), sem="ve")
        vb2 = self.op("vector", lambda e: e.tensor_scalar_mul(b2c, b2c, -1.0),
                      sem="ve")
        for ch in range(NCH):
            h2c = h2[:, ch * 1024:(ch + 1) * 1024]
            sa_ = self.op("scalar", lambda e, o=h2c, ch=ch: e.activation(
                o, o, AF.Identity, bias=b2c[:, ch:ch + 1],
                scale=a2c[:, ch:ch + 1]),
                waits=[("ve", vb2)] if ch == 0 else [], sem="se")
            self.op("vector", lambda e, o=h2c: e.tensor_mul(
                o, o, self.fv("cbrep", 0, 128, 0, 1024)),
                waits=[("se", sa_), dcb] if ch == 0 else [("se", sa_)],
                sem="ve")
            self.op("vector", lambda e, o=h2c: e.tensor_add(
                o, o, self.fv("sbrep", 0, 128, 0, 1024)),
                waits=[dsb] if ch == 0 else [], sem="ve")
            tc = self.fv("cmb", 0, 128, 14, 1)
            self.op("vector", lambda e, o=h2c: e.reduce_max(
                tc, o, AX.X, apply_absolute_value=True), sem="ve")
            self.op("vector", lambda e: e.tensor_tensor(amc, amc, tc, ALU.max),
                    waits=[("ve", vam)] if ch == 0 else [], sem="ve")
        vamf = self.now("ve")
        dsr = self.cdma("sync", g["rows_hbm"][16, 0:128]
                       .rearrange("(c a) -> c a", a=1), amc,
                       waits=[("ve", vamf)], slow=True, slot="c9")
        arow = self.fv("xp", 0, 1, 0, 128)
        dld = self.cdma("sync", arow, g["rows_hbm"][16:17, 0:128],
                       waits=[dsr], slot="c9")
        gsc = self.fv("xp", 0, 1, 130, 1)
        vg = self.op("vector", lambda e: e.reduce_max(gsc, arow, AX.X),
                     waits=[dld], sem="ve")
        dag = self.cdma("sync", g["ag2_in"][:, :], gsc, waits=[("ve", vg)],
                       slot="c9")
        self.op("gpsimd", lambda e: e.collective_compute(
            "AllReduce", ALU.max, replica_groups=[list(range(NCORE))],
            ins=[g["ag2_in"][:, :]], outs=[g["ag2_out"][:, :]]),
            waits=[dag], sem="cc")
        cc2 = self.now("cc")
        gx = self.fv("cmb", 0, 128, 18, 1)
        dgx = self.cdma("sync", gx, g["ag2_out"][0:1, :].partition_broadcast(128),
                       waits=[("cc", cc2)], slot="c9")
        cxc = self.fv("cmb", 0, 128, 20, 1)
        dcx = self.cdma("sync", cxc, g["clip_x"][0:1, :].partition_broadcast(128),
                       slot="c9")
        ssx = self.op("scalar", lambda e: e.activation(
            cxc, cxc, AF.Sigmoid, bias=0.0, scale=1.0), waits=[dcx], sem="se")
        tsg = self.fv("cmb", 0, 128, 22, 1)
        self.op("vector", lambda e: e.tensor_mul(tsg, cxc, gx),
                waits=[("se", ssx), dgx], sem="ve")
        sxi = self.fv("cmb", 0, 128, 24, 1)
        self.op("vector", lambda e: e.reciprocal(sxi, tsg), sem="ve")
        vsx = self.op("vector", lambda e: e.tensor_scalar_mul(sxi, sxi, 127.0),
                      sem="ve")
        self.tsg_col = tsg
        self.mgc = self.fv("cmb", 0, 128, 42, 1)
        self.nmgc = self.fv("cmb", 0, 128, 44, 1)
        self.op("vector", lambda e: e.memset(self.mgc, MAGIC), sem="ve")
        vmg = self.op("vector", lambda e: e.memset(self.nmgc, -MAGIC), sem="ve")
        self.vmg = vmg
        qx = self.bvv("qx", 0, 128, 0, 16384)
        ps7b = self.ps[7][:, :].bitcast(BF16)
        for ch in range(NCH):
            h2c = h2[:, ch * 1024:(ch + 1) * 1024]
            tq = self.fv("tmp0", 0, 128, 0, 1024) if ch % 2 == 0 else \
                self.fv("tmp1", 0, 128, 0, 1024)
            qs = self.bvv("y2r", 0, 128, (ch % 2) * 1024, 1024)
            w = ([("ve", vsx), ("ve", vmg)] if ch < 2
                 else [("se2", st[("qev", ch - 2)])])
            s1 = self.op("scalar", lambda e, i=h2c, o=tq: e.activation(
                o, i, AF.Identity, bias=self.mgc, scale=sxi),
                waits=w, sem="se")
            s2 = self.op("scalar", lambda e, o=tq, q=qs: e.activation(
                q, o, AF.Identity, bias=self.nmgc, scale=1.0), sem="se")
            vq = self.op("vector", lambda e, q=qs: e.tensor_scalar(
                q, q, 127.0, -128.0, ALU.min, ALU.max),
                waits=[("se", s2)], sem="ve")
            for dt in range(8):
                self.op("tensor", lambda e, dt=dt, q=qs: e.transpose(
                    ps7b[:, dt * 128:(dt + 1) * 128],
                    q[:, dt * 128:(dt + 1) * 128], self.sid[:, :]),
                    waits=[("ve", vq), ("se2", self.now("se2"))]
                    if dt == 0 else [],
                    sem="pe" if dt == 7 else None)
            pe = self.now("pe")
            for dt in range(8):
                self.op("scalar", lambda e, dt=dt, ch=ch: e.activation(
                    qx[:, dt * 2048 + ch * 128: dt * 2048 + (ch + 1) * 128],
                    ps7b[:, dt * 128:(dt + 1) * 128], AF.Identity,
                    bias=0.0, scale=1.0),
                    waits=[("pe", pe)] if dt == 0 else [], sem="se2")
            st[("qev", ch)] = self.now("se2")
        self.se2_qx = self.now("se2")

    def p8a_gateup(self):
        g = self.g
        qx = self.bvv("qx", 0, 128, 0, 16384)
        fcol = self.fv("cmb", 0, 128, 26, 1)
        vf = self.op("vector", lambda e: e.tensor_scalar_mul(
            fcol, self.tsg_col, 1.0 / 16129.0), sem="ve")
        sgc = self.fv("scl_g", 0, 128, 0, 22)
        suc = self.fv("scl_u", 0, 128, 0, 22)
        vsc = []
        for nm, row, dst in (("sg", 4, sgc), ("su", 5, suc)):
            d = self.cdma("sync", dst, g["rows_hbm"][row, 0:I]
                         .rearrange("(it p) -> p it", p=128),
                         slow=True, slot="c10")
            vsc.append(self.op("vector", lambda e, a=dst: e.tensor_scalar(
                a, a, fcol, None, ALU.mult), waits=[d, ("ve", vf)], sem="ve"))
        amc = self.fv("cmb", 0, 128, 28, 1)
        vam = self.op("vector", lambda e: e.memset(amc, 0.0), sem="ve")
        st = {}
        for it in range(IT):
            half = it % 2
            wg = self.bvv("qwrot", 0, 128, half * 4096, 1024)
            wu = self.bvv("qwrot", 0, 128, half * 4096 + 1024, 1024)
            lw = [("cc", self.cc_agq)] if it == 0 else \
                ([("pe", st[("mml", it - 2)])] if it >= 2 else [])
            dg = self.dma("sync", wg.rearrange("p (c i) -> p c i", i=128),
                          g["agq_out"][:, it * 16384:(it + 1) * 16384]
                          .rearrange("c (p i) -> p c i", p=128),
                          waits=lw, slot="qwg%d" % half)
            du = self.dma("sync", wu.rearrange("p (c i) -> p c i", i=128),
                          g["agq_out"][:, 128 * I + it * 16384:
                                       128 * I + (it + 1) * 16384]
                          .rearrange("c (p i) -> p c i", p=128),
                          waits=lw, slot="qwu%d" % half)
            for tb in range(4):
                bg, bu = tb % 2, 2 + tb % 2
                for dt in range(8):
                    w = []
                    if dt == 0:
                        w = [dg] + ([("se", st[("sil", it, tb - 2)])]
                                    if ("sil", it, tb - 2) in st else []) + \
                            ([("se", st[("sil", it - 1, tb + 2)])]
                             if ("sil", it - 1, tb + 2) in st else []) + \
                            ([("se2", self.se2_qx)] if (it, tb) == (0, 0) else [])
                    self.op("tensor", lambda e, dt=dt, tb=tb, bg=bg, wg=wg:
                            e.matmul(self.ps[bg][:, 0:512],
                                     wg[:, dt * 128:(dt + 1) * 128],
                                     qx[:, dt * 2048 + tb * 512:
                                        dt * 2048 + (tb + 1) * 512],
                                     start=(dt == 0), stop=(dt == 7)),
                            waits=w, sem="pe" if dt == 7 else None)
                for dt in range(8):
                    w = []
                    if dt == 0:
                        w = [du] + ([("ve", st[("mst", it, tb - 2)])]
                                    if ("mst", it, tb - 2) in st else []) + \
                            ([("ve", st[("mst", it - 1, tb + 2)])]
                             if ("mst", it - 1, tb + 2) in st else [])
                    self.op("tensor", lambda e, dt=dt, tb=tb, bu=bu, wu=wu:
                            e.matmul(self.ps[bu][:, 0:512],
                                     wu[:, dt * 128:(dt + 1) * 128],
                                     qx[:, dt * 2048 + tb * 512:
                                        dt * 2048 + (tb + 1) * 512],
                                     start=(dt == 0), stop=(dt == 7)),
                            waits=w, sem="pe" if dt == 7 else None)
                pe = self.now("pe")
                if tb == 3:
                    st[("mml", it)] = pe
                sgb = self.fv("tmp0", 0, 128, 0, 512) if tb % 2 == 0 else \
                    self.fv("tmp1", 0, 128, 0, 512)
                ssil = self.op("scalar", lambda e, bg=bg, it=it, o=sgb:
                               e.activation(o, self.ps[bg][:, 0:512], AF.Silu,
                                            bias=0.0,
                                            scale=sgc[:, it:it + 1]),
                               waits=[("pe", pe)] +
                               ([("ve", vsc[0])] if it == 0 and tb == 0 else []),
                               sem="se")
                st[("sil", it, tb)] = ssil
                msl = self.mband(it)[:, tb * 512:(tb + 1) * 512]
                self.op("vector", lambda e, bu=bu, it=it, o=msl, sgb=sgb:
                        e.scalar_tensor_tensor(o, self.ps[bu][:, 0:512],
                                               suc[:, it:it + 1], sgb,
                                               ALU.mult, ALU.mult),
                        waits=[("se", ssil)] +
                        ([("ve", vsc[1])] if it == 0 and tb == 0 else []),
                        sem="ve")
                tc = self.fv("cmb", 0, 128, 30, 1)
                self.op("vector", lambda e, o=msl: e.reduce_max(
                    tc, o, AX.X, apply_absolute_value=True), sem="ve")
                self.op("vector", lambda e: e.tensor_tensor(
                    amc, amc, tc, ALU.max),
                    waits=[("ve", vam)] if (it, tb) == (0, 0) else [], sem="ve")
                st[("mst", it, tb)] = self.now("ve")
        vamf = self.now("ve")
        dsr = self.cdma("sync", g["rows_hbm"][19, 0:128]
                       .rearrange("(c a) -> c a", a=1), amc,
                       waits=[("ve", vamf)], slow=True, slot="c11")
        arow = self.fv("xp", 0, 1, 132, 128)
        dld = self.cdma("sync", arow, g["rows_hbm"][19:20, 0:128],
                       waits=[dsr], slot="c11")
        gsc = self.fv("xp", 0, 1, 262, 1)
        vg = self.op("vector", lambda e: e.reduce_max(gsc, arow, AX.X),
                     waits=[dld], sem="ve")
        dag = self.cdma("sync", g["ag3_in"][:, :], gsc, waits=[("ve", vg)],
                       slot="c11")
        self.op("gpsimd", lambda e: e.collective_compute(
            "AllReduce", ALU.max, replica_groups=[list(range(NCORE))],
            ins=[g["ag3_in"][:, :]], outs=[g["ag3_out"][:, :]]),
            waits=[dag], sem="cc")
        self.cc_ag3 = self.now("cc")

    def p8b_down(self):
        g = self.g
        g2 = self.fv("cmb", 0, 128, 34, 1)
        dg2 = self.cdma("sync", g2, g["ag3_out"][0:1, :].partition_broadcast(128),
                       waits=[("cc", self.cc_ag3)], slot="c11")
        cdc = self.fv("cmb", 0, 128, 36, 1)
        dcd = self.cdma("sync", cdc, g["clip_dn"][0:1, :].partition_broadcast(128),
                       slot="c11")
        ssd = self.op("scalar", lambda e: e.activation(
            cdc, cdc, AF.Sigmoid, bias=0.0, scale=1.0), waits=[dcd], sem="se")
        s2f = self.fv("cmb", 0, 128, 38, 1)
        self.op("vector", lambda e: e.tensor_mul(s2f, cdc, g2),
                waits=[("se", ssd), dg2], sem="ve")
        s2i = self.fv("cmb", 0, 128, 40, 1)
        self.op("vector", lambda e: e.reciprocal(s2i, s2f), sem="ve")
        vs2 = self.op("vector", lambda e: e.tensor_scalar_mul(s2i, s2i, 127.0),
                      sem="ve")
        cd8 = self.fv("coef_dn", 0, 128, 0, 8)
        ddn = self.cdma("sync", cd8, g["rows_hbm"][6, 0:1024]
                       .rearrange("(dt p) -> p dt", p=128), slow=True,
                       slot="c11")
        self.op("vector", lambda e: e.tensor_mul(
            cd8, cd8, self.fv("modx", 0, 128, 24, 8)), waits=[ddn], sem="ve")
        self.op("vector", lambda e: e.tensor_scalar(
            cd8, cd8, s2f, None, ALU.mult), sem="ve")
        vcf = self.op("vector", lambda e: e.tensor_scalar_mul(
            cd8, cd8, 1.0 / 16129.0), sem="ve")
        dcr = self.cdma("sync", g["rows_hbm"][20, 0:1024]
                       .rearrange("(dt p) -> p dt", p=128), cd8,
                       waits=[("ve", vcf)], slow=True, slot="c11")
        dcf = self.cdma("sync", self.fv("coefrep", 0, 128, 0, 1024),
                       g["rows_hbm"][20:21, 0:1024].partition_broadcast(128),
                       waits=[dcr], slot="c11")
        def qdt(hb):
            if hb < 16:
                return self.bvv("R4", 0, 128, hb * 1024, 1024)
            return self.bvv("R6", 0, 128, (hb - 16) * 1024, 1024)
        for hb in range(IT):
            hv, itl = divmod(hb, 11)
            off = 2 * 128 * I + hv * 11 * 16384 + itl * 16384
            self.dma("sync", qdt(hb).rearrange("p (c h) -> p c h", h=128),
                     g["agq_out"][:, off:off + 16384]
                     .rearrange("c (p h) -> p c h", p=128),
                     slot="qdl")
        dqd = [("D_qdl", self.now("D_qdl"))]
        for hb in range(IT):
            band = self.mband(hb)
            for hf in range(2):
                tq = self.fv("tmp0", 0, 128, 0, 1024) if hf == 0 else \
                    self.fv("tmp1", 0, 128, 0, 1024)
                sl = band[:, hf * 1024:(hf + 1) * 1024]
                s1 = self.op("scalar", lambda e, i=sl, o=tq: e.activation(
                    o, i, AF.Identity, bias=self.mgc, scale=s2i),
                    waits=[("ve", vs2)] if hb == 0 and hf == 0 else [],
                    sem="se")
                s2 = self.op("scalar", lambda e, o=tq, q=sl: e.activation(
                    q, o, AF.Identity, bias=self.nmgc, scale=1.0), sem="se")
                self.op("vector", lambda e, q=sl: e.tensor_scalar(
                    q, q, 127.0, -128.0, ALU.min, ALU.max),
                    waits=[("se", s2)], sem="ve")
        vqmf = self.now("ve")
        st = {}
        for ch in range(NCH):
            dxr = self.dma("sync", self.fv("xrl", 0, 128, 0, 1024),
                           g["x2_hbm"][ch * 128:(ch + 1) * 128, :],
                           waits=[("ve", st[("x3v", ch - 1)])]
                           if ("x3v", ch - 1) in st else [], slot="xrl")
            for dN in range(2):
                for hb in range(IT):
                    w = []
                    if hb == 0:
                        w = [("ve", vqmf)] if ch == 0 and dN == 0 else []
                        if ch == 0:
                            w += dqd
                        if ("x3st", ch - 1, dN) in st:
                            w.append(st[("x3st", ch - 1, dN)])
                        if ("ev", ch - 1, dN) in st:
                            w.append(("ve", st[("ev", ch - 1, dN)]))
                    self.op("tensor", lambda e, hb=hb, ch=ch, dN=dN: e.matmul(
                        self.ps[dN][:, 0:512],
                        self.mband(hb)[:, ch * 128:(ch + 1) * 128],
                        qdt(hb)[:, dN * 512:(dN + 1) * 512],
                        start=(hb == 0), stop=(hb == IT - 1)),
                        waits=w, sem="pe" if hb == IT - 1 else None)
                pe = self.now("pe")
                xo = (self.fv("tmp0", 0, 128, 0, 512) if dN == 0
                      else self.fv("tmp1", 0, 128, 0, 512))
                self.op("vector", lambda e, dN=dN, xo=xo: e.tensor_tensor(
                    xo, self.ps[dN][:, 0:512],
                    self.fv("coefrep", 0, 128, dN * 512, 512), ALU.mult),
                    waits=[("pe", pe), dcf] + ([st[("x3st", ch - 1, dN)]]
                                               if ("x3st", ch - 1, dN) in st
                                               else []), sem="ve")
                self.op("vector", lambda e, dN=dN, xo=xo: e.tensor_add(
                    xo, xo, self.fv("xrl", 0, 128, dN * 512, 512)),
                    waits=[dxr], sem="ve")
                st[("ev", ch, dN)] = self.now("ve")
                st[("x3v", ch)] = self.now("ve")
                st[("x3st", ch, dN)] = self.dma(
                    "sync", self.out_x3[ch * 128:(ch + 1) * 128,
                                        dN * 512:(dN + 1) * 512], xo,
                    waits=[("ve", self.now("ve"))], slot="out%d" % dN)

    def emit_all(self):
        phases = [("p0", self.p0_consts, 0), ("p2a", self.p2a_mod_send, 0),
                  ("p1s", self.p1_ln1, 1), ("p2b", self.p2b_mod_recv, 1),
                  ("p2c", self.p2c_quant_weights, 0),
                  ("p1", self.p1_affine, 1), ("p4", self.p4_mixer_in, 1),
                  ("p5a", self.p5a_smalls, 0), ("p5b", self.p5b_z, 0),
                  ("p5c", self.p5c_loopA, 1), ("p5pre", self.p5e_pre, 0),
                  ("p5", self.p5e_loopB, 1),
                  ("p7", self.p7_ln2, 1), ("p8a", self.p8a_gateup, 1),
                  ("p8b", self.p8b_down, 1)]
        for name, fn, bar in phases:
            fn()
            if bar:
                self.barrier()
            if self.upto == name:
                self.final_wait()
                return
        if self.upto in ("all",):
            self.final_wait()
            return
        raise NotImplementedError("later phases pending")

    def final_wait(self):
        waits = [(sname, self.count[sname]) for sname in self.semnames
                 if sname.startswith("D_") and self.count[sname] > 0]
        waits += [("cc", self.now("cc"))]
        self.emit("sync", lambda e: e.nop(), waits, ())


def make_program(dbg=(), upto="all"):
    b = Build(dbg, upto)
    b.emit_all()
    b.materialize()
    b.stack.close()
    return b


def kernel(**inputs):
    maps = _stage(inputs)
    b = make_program()
    res = run_bass_kernel_spmd(b.nc, maps, core_ids=list(range(NCORE)))
    out = np.empty((B, L, D), np.float32)
    for k in range(NCORE):
        bb, q = divmod(k, 4)
        out[bb, q * T:(q + 1) * T] = np.asarray(res.results[k]["out_x3"])
    return out


# version 09 (final)
# speedup vs baseline: 1.0185x; 1.0032x over previous
import contextlib
import numpy as np
import ml_dtypes
import concourse.bass as bass
import concourse.mybir as mybir
from concourse.bass_utils import run_bass_kernel_spmd

F32 = mybir.dt.float32
BF16 = mybir.dt.bfloat16
AF = mybir.ActivationFunctionType
ALU = mybir.AluOpType
AX = mybir.AxisListType

NCORE = 8
B, L, D = 2, 8192, 1024
E, S, CONV = 2048, 16, 4
I = 2816
T = 2048
TH = T + 3
C = 128
NCH = T // C
DT = D // 128
ET = E // 128
TT = T // 128
IT = I // 128
MAGIC = 12582912.0
IQ = I // 8
EPS = 1e-5
H1S = 2052


def _stage(inp):
    x = np.asarray(inp["L0_attn_x"])
    w_in = np.asarray(inp["L0_attn_in_proj_w"])
    conv_w = np.asarray(inp["L0_attn_conv_w"]).reshape(E, CONV)
    conv_b = np.asarray(inp["L0_attn_conv_b"]).reshape(1, E)
    xpw = np.asarray(inp["L0_attn_x_proj_w"])
    alog = np.asarray(inp["L0_attn_A_log"])
    dpar = np.asarray(inp["L0_attn_D_param"]).reshape(1, E)
    w_out = np.asarray(inp["L0_attn_out_proj_w"])
    clip_x = np.asarray(inp["L0_ffn_clip_x"]).reshape(1, 1)
    clip_dn = np.asarray(inp["L0_ffn_clip_down"]).reshape(1, 1)
    w_g = np.asarray(inp["L0_ffn_w_gate"])
    w_u = np.asarray(inp["L0_ffn_w_up"])
    w_d = np.asarray(inp["L0_ffn_w_down"])
    cc = np.asarray(inp["L0_c"])
    wmod = np.asarray(inp["L0_wmod"])

    co = np.ascontiguousarray
    wxc_tiled = co(w_in[:E].T.reshape(8, 128, 16, 128).transpose(2, 0, 1, 3))
    wz_t = co(w_in[E:].T.reshape(8, 128, E))
    xpw_t = co(xpw.T.reshape(16, 128, 33))
    wout_t = co(w_out.T.reshape(16, 128, D))
    wgT = w_g.T
    wuT = w_u.T
    wdT = w_d.T

    maps = []
    for k in range(NCORE):
        b, q = divmod(k, 4)
        lo = q * T
        xh = x[b, 0:3] if q == 0 else x[b, lo - 3:lo]
        x_tm = co(np.concatenate([xh, x[b, lo:lo + T]], axis=0))
        pm = np.zeros((1, 8), np.float32)
        om = np.zeros((8, 8), np.float32)
        for j in range(4 * b, k):
            pm[0, j] = 1.0
            for u in range(j + 1, k):
                om[u, j] = 1.0
        maps.append(dict(
            x_tm=x_tm.astype(np.float32),
            wxc_t=wxc_tiled.astype(np.float32),
            wz_t=wz_t.astype(np.float32),
            xpw_t=xpw_t.astype(np.float32),
            wout_t=wout_t.astype(np.float32),
            conv_w=co(conv_w).astype(np.float32),
            conv_b=co(conv_b).astype(np.float32),
            dparam=co(dpar).astype(np.float32),
            alog_row=co(alog[0:1, :]).astype(np.float32),
            clip_x=clip_x.astype(np.float32),
            clip_dn=clip_dn.astype(np.float32),
            wmod_sh=co(wmod[:, k * 768:(k + 1) * 768]).astype(np.float32),
            cT=co(cc.T).astype(np.float32),
            wg_nat=co(w_g[k * IQ:(k + 1) * IQ]).astype(np.float32),
            wu_nat=co(w_u[k * IQ:(k + 1) * IQ]).astype(np.float32),
            wd_nat=co(w_d[k * 128:(k + 1) * 128]).astype(np.float32),
            wgT_sh=co(wgT[k * 128:(k + 1) * 128]).astype(np.float32),
            wuT_sh=co(wuT[k * 128:(k + 1) * 128]).astype(np.float32),
            wdT_sh=co(wdT[:, k * 128:(k + 1) * 128]).astype(np.float32),
            halo=np.array([[0.0 if q == 0 else 1.0]], np.float32),
            bsel=np.array([[float(b)]], np.float32),
            pmask=pm,
            ordmask_t=om,
        ))
    return maps


class Build:
    def __init__(self, dbg=(), upto="all", dbg_scr=False):
        self.upto = upto
        self.dbg_scr = dbg_scr
        self.nc = nc = bass.Bass("TRN2", target_bir_lowering=False)
        self.events = []
        self.count = {}
        self.dbg = dict(dbg)
        self.stack = contextlib.ExitStack()

        g = self.g = {}
        def din(name, shape):
            g[name] = nc.dram_tensor(name, list(shape), F32, kind="ExternalInput").ap()
        for name, shape in [
            ("x_tm", (TH, D)), ("wxc_t", (16, 8, 128, 128)), ("wz_t", (8, 128, E)),
            ("xpw_t", (16, 128, 33)), ("wout_t", (16, 128, D)), ("conv_w", (E, CONV)),
            ("conv_b", (1, E)), ("dparam", (1, E)), ("alog_row", (1, S)),
            ("clip_x", (1, 1)), ("clip_dn", (1, 1)), ("wmod_sh", (D, 768)),
            ("cT", (D, 2)), ("wg_nat", (IQ, D)), ("wu_nat", (IQ, D)),
            ("wd_nat", (128, I)), ("wgT_sh", (128, I)), ("wuT_sh", (128, I)),
            ("wdT_sh", (I, 128)), ("halo", (1, 1)), ("bsel", (1, 1)),
            ("pmask", (1, 8)), ("ordmask_t", (8, 8)),
        ]:
            din(name, shape)
        self.out_x3 = nc.dram_tensor("out_x3", [T, D], F32, kind="ExternalOutput").ap()
        self.dbgo = {}
        for name, shape in self.dbg.items():
            dt = BF16 if name.endswith("_bf") else F32
            self.dbgo[name] = nc.dram_tensor("dbg_" + name, list(shape), dt,
                                             kind="ExternalOutput").ap()

        def scr(name, shape, dt=F32):
            kw = ({"kind": "ExternalOutput"}
                  if self.dbg_scr and not name.startswith("ag") else {})
            g[name] = nc.dram_tensor(name, list(shape), dt, **kw).ap()
        scr("x2_hbm", (T, D))
        scr("sz_hbm", (NCH, 128, 2048), BF16)
        scr("h2_hbm", (TT, 8, 128, 128))
        scr("xq_hbm", (8, 128, T))
        scr("rows_hbm", (24, 4096))
        scr("krow_hbm", (NCH, 2, 2048))
        scr("ag0_in", (1, 2368))
        scr("ag0_out", (NCORE, 2368))
        NQ = 128 * I * 2 + I * 128
        scr("agq_in", (1, NQ), BF16)
        scr("agq_out", (NCORE, NQ), BF16)
        scr("ag1_in", (1, S * E + 16))
        scr("ag1_out", (NCORE, S * E + 16))
        scr("ag2_in", (1, 1))
        scr("ag2_out", (1, 1))
        scr("ag3_in", (1, 1))
        scr("ag3_out", (1, 1))

        g["ident"] = nc.inline_tensor(np.eye(128, dtype=ml_dtypes.bfloat16),
                                      name="identbf").ap()
        g["identf"] = nc.inline_tensor(np.eye(128, dtype=np.float32),
                                       name="identf32").ap()
        g["tril"] = nc.inline_tensor(
            np.ascontiguousarray(np.triu(np.ones((128, 128), np.float32))),
            name="trilf").ap()

        self.semnames = ["dls", "dlg", "dlc", "cc", "pe", "ve", "se", "bar",
                         "pe2", "ve2", "se2"]
        for s in self.semnames:
            self.count[s] = 0
        self.sems = {}

        self.NW = 52736
        self.sb = self.stack.enter_context(nc.sbuf_tensor("arena", [128, self.NW], F32))
        self.sid = self.stack.enter_context(nc.sbuf_tensor("sident", [128, 128], BF16))
        self.sidf = self.stack.enter_context(nc.sbuf_tensor("sidentf", [128, 128], F32))
        self.stril = self.stack.enter_context(nc.sbuf_tensor("stril", [128, 128], F32))
        self.ps = [self.stack.enter_context(nc.psum_tensor(f"ps{i}", [128, 512], F32))
                   for i in range(8)]

        self.off = {}
        self._size = {}
        self.cur = 0
        self._layout()

    def _alloc(self, name, nwords):
        self._size[name] = nwords
        self.off[name] = self.cur
        self.cur += (nwords + 1) & ~1
        assert self.cur <= self.NW, f"arena overflow at {name}: {self.cur}"

    def _alias(self, name, base, word_off, nwords):
        self._size[name] = nwords
        assert word_off + nwords <= self._size[base], (name, base)
        self.off[name] = self.off[base] + word_off

    def _layout(self):
        A, AL = self._alloc, self._alias
        A("mod_cols", 96)
        A("modx", 32)
        A("arow", 16); A("acol", 2); A("arep", 16)
        A("scl_g", 22); A("scl_u", 22); A("coef_dn", 8)
        A("sc", 64)
        A("convw", 80)
        A("tmp0", 1032); A("tmp1", 1032); A("cmb", 128)

        A("R1", 7168)
        AL("modw", "R1", 0, 6144)
        AL("wqstage", "R1", 3074, 1408)
        AL("wqrep", "R1", 4484, 1408)
        AL("wqout", "R1", 5894, 704)
        AL("amaxst", "R1", 3074, 2816)
        AL("xc0", "R1", 0, 2052)
        AL("xcin", "R1", 2054, 2052)
        AL("xcsl", "R1", 4108, 1028)
        AL("wrot", "R1", 5138, 1088)
        AL("repA", "R1", 0, 2048)
        AL("repC", "R1", 2048, 1024)
        AL("e3t", "R1", 3072, 2048)
        AL("e3u", "R1", 5120, 2048)

        A("R2", 4192)
        AL("xp", "R2", 0, 2048)
        AL("r2row", "R2", 0, 2816)
        AL("comb", "R2", 2050, 2080)
        AL("convw2", "R2", 2050, 80)

        A("R3", 9600)
        AL("xrot", "R3", 0, 2064)
        AL("xhalo", "R3", 2066, 1024)
        AL("sums", "R3", 3092, 40)
        AL("ab1", "R3", 3134, 40)
        AL("xaff", "R3", 3176, 1028)
        AL("xpTc", "R3", 0, 548)
        AL("hsn0", "R3", 550, 1024)
        AL("hsn1", "R3", 1576, 1024)
        AL("kbslot", "R3", 2602, 128)
        AL("gslot", "R3", 2732, 260)
        AL("rch", "R3", 2994, 128)
        AL("rows16", "R3", 3124, 96)
        AL("ubufs", "R3", 3222, 128)
        AL("hstate", "R3", 3352, 2048)
        AL("y2r", "R3", 5402, 1024)
        AL("xcin2", "R3", 5402, 2052)
        AL("xrl", "R3", 6428, 1024)
        AL("y2f", "R3", 7454, 1024)
        AL("hinbf", "R3", 8480, 1024)
        AL("ab2", "R3", 9506, 40)
        AL("x2rot", "R3", 0, 2064)
        AL("x2aff", "R3", 3176, 1028)

        A("R4", 8212)
        AL("h1T", "R4", 0, 8212)
        AL("wot", "R4", 0, 8192)
        AL("qx", "R4", 0, 8192)

        A("R6", 4104)
        AL("wzrot", "R6", 0, 4100)
        AL("szrot", "R6", 0, 2052)
        AL("garepf", "R6", 2054, 1024)
        AL("garep", "R6", 3080, 512)
        AL("drep", "R6", 2054, 1024)
        AL("qwrot", "R6", 0, 4100)
        AL("dnbuf", "R6", 516, 2048)
        AL("xpwb", "R6", 2054, 288)

        A("xcsTM", 16384)
        AL("h2TM", "xcsTM", 0, 16384)
        AL("cbrep", "R6", 0, 1024)
        AL("sbrep", "R6", 1026, 1024)
        AL("coefrep", "R6", 3072, 1024)

    def fv(self, name, p0=0, pn=128, o=0, n=None):
        off, sz = self.off[name], self._size[name]
        n = (sz - o) if n is None else n
        assert o + n <= sz, ("fv overflow", name, o, n, sz)
        return self.sb[p0:p0 + pn, off + o:off + o + n]

    def bvv(self, name, p0=0, pn=128, o=0, n=None):
        off, sz = self.off[name], self._size[name]
        nb = 2 * sz
        n = (nb - o) if n is None else n
        base = self.sb[p0:p0 + pn, off:off + sz].bitcast(BF16)
        return base[:, o:o + n]

    def now(self, s):
        return self.count[s]

    def emit(self, eng, fn, waits=(), incs=()):
        for s, n in incs:
            self.count[s] += n
        self.events.append((eng, fn, tuple(waits), tuple(incs)))

    def dma(self, eng, out, in_, waits=(), slow=False, slot="misc"):
        sem = "D_" + slot
        if sem not in self.count:
            self.count[sem] = 0
            self.semnames.append(sem)
        def fn(e, o=out, i=in_, sl=slow):
            return e.dma_start(out=o, in_=i, allow_slow_non_contiguous=sl)
        self.emit(eng, fn, waits, [(sem, 16)])
        return (sem, self.now(sem))

    def cdma(self, eng, out, in_, waits=(), slow=False, slot="chain"):
        if not hasattr(self, "_chain"):
            self._chain = {}
        prev = self._chain.get(slot)
        w = list(waits) + ([prev] if prev else [])
        t = self.dma(eng, out, in_, w, slow, slot)
        self._chain[slot] = t
        return t

    def op(self, eng, fn, waits=(), sem=None, drain=True):
        incs = [(sem, 1)] if sem else []
        self.emit(eng, fn, waits, incs)
        if drain and eng in ("vector", "scalar"):
            self.emit(eng, lambda e: e.drain(), (), ())
        return self.now(sem) if sem else None

    def barrier(self, engines=("tensor", "vector", "scalar", "gpsimd", "sync")):
        for e in engines:
            self.emit(e, lambda eng: eng.nop(), (), [("bar", 1)])
        tgt = self.now("bar")
        for e in engines:
            self.emit(e, lambda eng: eng.nop(), [("bar", tgt)], ())
        return tgt

    def materialize(self):
        nc = self.nc
        for s in self.semnames:
            self.sems[s] = self.stack.enter_context(nc.semaphore(s))
        sems = self.sems
        with nc.Block() as block:
            def run(engname, eng):
                for e, fn, waits, incs in self.events:
                    if e != engname:
                        continue
                    for s, v in waits:
                        eng.wait_ge(sems[s], v)
                    ins = fn(eng)
                    for s, n in incs:
                        ins = ins.then_inc(sems[s], n)

            @block.tensor
            def _(eng):
                run("tensor", eng)

            @block.vector
            def _(eng):
                run("vector", eng)

            @block.scalar
            def _(eng):
                run("scalar", eng)

            @block.gpsimd
            def _(eng):
                run("gpsimd", eng)

            @block.sync
            def _(eng):
                run("sync", eng)

    def p0_consts(self):
        g = self.g
        L = self.ld = {}
        L["sid"] = self.cdma("sync", self.sid[:, :], g["ident"][:, :], slot="cs")
        L["tril"] = self.cdma("sync", self.stril[:, :], g["tril"][:, :], slot="cs")
        L["sidf"] = self.cdma("sync", self.sidf[:, :], g["identf"][:, :], slot="cs")
        arow = self.fv("arow", 0, 1, 0, 16)
        d0 = self.cdma("sync", arow, g["alog_row"][:, :], slot="cs")
        s0 = self.op("scalar", lambda e: e.activation(arow, arow, AF.Exp,
                                                      bias=0.0, scale=1.0),
                     waits=[d0], sem="se")
        v0 = self.op("vector", lambda e: e.tensor_scalar_mul(arow, arow, -1.0),
                     waits=[("se", s0)], sem="ve")
        d1 = self.cdma("sync", g["rows_hbm"][0:1, 0:16], arow, waits=[("ve", v0)],
                      slot="cs")
        L["acol"] = self.cdma("sync", self.fv("acol", 0, 16, 0, 1),
                             g["rows_hbm"][0, 0:16].rearrange("(a b) -> a b", b=1),
                             waits=[d1], slow=True, slot="cs")
        L["acol2"] = self.cdma("sync", self.fv("acol", 16, 16, 0, 1),
                              g["rows_hbm"][0, 0:16].rearrange("(a b) -> a b", b=1),
                              waits=[d1], slow=True, slot="cs")
        L["arep"] = self.cdma("sync", self.fv("arep"),
                             g["rows_hbm"][0:1, 0:16].partition_broadcast(128),
                             waits=[d1], slot="cs")
        L["convw"] = self.cdma("sync", self.fv("convw", 0, 128, 0, 64)
                              .rearrange("p (et c) -> p et c", c=4),
                              g["conv_w"].rearrange("(et p) c -> p et c", p=128),
                              slot="cs")
        L["convb"] = self.cdma("sync", self.fv("convw", 0, 128, 64, 16),
                              g["conv_b"].rearrange("a (et p) -> p (a et)", p=128),
                              slow=True, slot="cs")
        L["xpwb"] = self.dma("gpsimd", self.bvv("xpwb", 0, 128, 0, 16 * 34)
                             .rearrange("p (es m) -> p es m", m=34)[:, :, 0:33],
                             g["xpw_t"].transpose([1, 0, 2]), slot="xpwb")
        L["clips"] = self.cdma("sync", self.fv("sc", 0, 1, 0, 1), g["clip_x"][:, :],
                              slot="cs")
        self.cdma("sync", self.fv("sc", 0, 1, 2, 1), g["clip_dn"][:, :], slot="cs")
        L["halo"] = self.cdma("sync", self.fv("sc", 0, 128, 4, 1),
                             g["halo"][0:1, :].partition_broadcast(128), slot="cs")
        L["bsel"] = self.cdma("sync", self.fv("sc", 0, 128, 6, 1),
                             g["bsel"][0:1, :].partition_broadcast(128), slot="cs")

    def p2a_mod_send(self):
        g = self.g
        ct = self.fv("sc", 0, 128, 8, 16)
        dl = self.cdma("sync", ct.rearrange("p (dt b) -> p dt b", b=2),
                      g["cT"].rearrange("(dt p) b -> p dt b", p=128), slot="cs")
        se0 = self.op("scalar", lambda e: e.activation(ct, ct, AF.Silu,
                                                       bias=0.0, scale=1.0),
                      waits=[dl], sem="se")
        dmw = self.dma("sync", self.fv("modw", 0, 128, 0, 8 * 768)
                       .rearrange("p (dt j) -> p dt j", j=768),
                       g["wmod_sh"].rearrange("(dt p) j -> p dt j", p=128),
                       slot="modw")
        modp = self.fv("sc", 0, 128, 40, 12)
        modw = self.fv("modw", 0, 128, 0, 8 * 768)
        for ccj in range(6):
            for dt in range(8):
                w = []
                if dt == 0:
                    w = [dmw, ("se", se0), ("ve", self.now("ve"))]
                self.op("tensor", lambda e, ccj=ccj, dt=dt: e.matmul(
                    self.ps[0][:, 0:2],
                    modw[:, dt * 768 + ccj * 128: dt * 768 + (ccj + 1) * 128],
                    ct[:, dt * 2: dt * 2 + 2],
                    start=(dt == 0), stop=(dt == 7)),
                    waits=w, sem="pe" if dt == 7 else None)
            pe = self.now("pe")
            self.op("vector", lambda e, ccj=ccj: e.tensor_copy(
                modp.rearrange("p (b c) -> p b c", c=6)[:, :, ccj],
                self.ps[0][:, 0:2]),
                waits=[("pe", pe)], sem="ve")
        vmod = self.now("ve")
        self.pe_mod = self.now("pe")
        dsent = [self.dma("sync", g["ag0_in"][0, 0:1536]
                          .rearrange("(p bc) -> p bc", bc=12), modp,
                          waits=[("ve", vmod)], slot="ag0m")]
        vprev = None
        prevstore = None
        first_amax = True
        for name, nrows, ncols, aoff in (("wg_nat", IQ, D, 1536),
                                         ("wu_nat", IQ, D, 1888),
                                         ("wd_nat", 128, I, 2240)):
            r0 = 0
            while r0 < nrows:
                rn = min(128, nrows - r0)
                stage = self.fv("amaxst", 0, rn, 0, ncols)
                d = self.dma("sync", stage, g[name][r0:r0 + rn, :],
                             waits=([("ve", vprev)] if vprev else
                                    [("pe", self.pe_mod)] if first_amax else []),
                             slot="amaxst")
                first_amax = False
                col = self.fv("sc", 0, rn, 24, 1)
                vprev = self.op("vector", lambda e, st=stage, c=col, rn=rn:
                                e.reduce_max(c, st, AX.X, apply_absolute_value=True),
                                waits=[d] + ([prevstore] if prevstore else []),
                                sem="ve")
                prevstore = self.dma("sync",
                                     g["ag0_in"][0, aoff + r0: aoff + r0 + rn]
                                     .rearrange("(p a) -> p a", a=1), col,
                                     waits=[("ve", vprev)], slot="ag0c")
                dsent.append(prevstore)
                r0 += rn
        agw = [("D_ag0m", self.now("D_ag0m")), ("D_ag0c", self.now("D_ag0c"))]
        self.op("gpsimd", lambda e: e.collective_compute(
            "AllGather", ALU.bypass, replica_groups=[list(range(NCORE))],
            ins=[self.g["ag0_in"][:, :]], outs=[self.g["ag0_out"][:, :]]),
            waits=agw, sem="cc")
        self.cc_ag0 = self.now("cc")

    def p2b_mod_recv(self):
        g = self.g
        d = self.dma("sync",
                     self.fv("mod_cols").rearrange("p (b co cc) -> p b co cc",
                                                   b=2, co=8),
                     g["ag0_out"][:, 0:1536]
                     .rearrange("co (p b cc) -> p b co cc", b=2, cc=6),
                     waits=[("cc", self.cc_ag0)], slow=True, slot="modc")
        mc = self.fv("mod_cols")
        bs = self.fv("sc", 0, 128, 6, 1)
        self.op("vector", lambda e: e.tensor_sub(self.fv("tmp0", 0, 128, 0, 48),
                                                 mc[:, 48:96], mc[:, 0:48]),
                waits=[d, self.ld["bsel"]], sem="ve")
        self.op("vector", lambda e: e.scalar_tensor_tensor(
            mc[:, 0:48], self.fv("tmp0", 0, 128, 0, 48), bs, mc[:, 0:48],
            ALU.mult, ALU.add), sem="ve")
        mx = self.fv("modx")
        self.op("vector", lambda e: e.tensor_scalar_add(mx[:, 0:8], mc[:, 8:16], 1.0),
                sem="ve")
        self.op("vector", lambda e: e.tensor_scalar_add(mx[:, 8:16], mc[:, 32:40], 1.0),
                sem="ve")
        for src, dsto in ((16, 16), (40, 24)):
            t0 = self.fv("tmp0", 0, 128, 0, 8)
            t1 = self.fv("tmp0", 0, 128, 8, 8)
            t2 = self.fv("tmp0", 0, 128, 16, 8)
            self.op("vector", lambda e, s=src, a=t0: e.tensor_single_scalar(
                a, mc[:, s:s + 8], 0.0, ALU.is_ge), sem="ve")
            self.op("vector", lambda e, s=src, a=t1: e.tensor_scalar_max(
                a, mc[:, s:s + 8], 0.25), sem="ve")
            self.op("vector", lambda e, s=src, a=t2: e.tensor_scalar_min(
                a, mc[:, s:s + 8], -0.25), sem="ve")
            self.op("vector", lambda e, a=t1, bq=t2: e.tensor_sub(a, a, bq), sem="ve")
            self.op("vector", lambda e, a=t0, bq=t1: e.tensor_mul(a, a, bq), sem="ve")
            self.op("vector", lambda e, a=t0, bq=t2, do=dsto: e.tensor_add(
                mx[:, do:do + 8], a, bq), sem="ve")
        self.ve_mod = self.now("ve")

    def p2c_quant_weights(self):
        g = self.g
        dr = []
        dr.append(self.cdma("sync", g["rows_hbm"][4, 0:I]
                           .rearrange("(c j) -> c j", j=IQ),
                           g["ag0_out"][:, 1536:1536 + IQ],
                           waits=[("cc", self.cc_ag0)], slot="c4"))
        dr.append(self.cdma("sync", g["rows_hbm"][5, 0:I]
                           .rearrange("(c j) -> c j", j=IQ),
                           g["ag0_out"][:, 1888:1888 + IQ],
                           waits=[("cc", self.cc_ag0)], slot="c4"))
        dr.append(self.cdma("sync", g["rows_hbm"][6, 0:1024]
                           .rearrange("(c j) -> c j", j=128),
                           g["ag0_out"][:, 2240:2240 + 128],
                           waits=[("cc", self.cc_ag0)], slot="c4"))
        vprev = None
        prevrec = None
        rec_t = []
        for i, (src, dst, n) in enumerate(((4, 7, I), (5, 8, I), (6, 9, 1024))):
            row = self.fv("r2row", 0, 1, 0, n)
            d = self.dma("sync", row, g["rows_hbm"][src:src + 1, 0:n],
                         waits=[dr[i]] + ([("ve", vprev)] if vprev else []) +
                               ([prevrec] if prevrec else []),
                         slot="xprow")
            self.op("vector", lambda e, r=row: e.reciprocal(r, r),
                    waits=[d], sem="ve")
            vprev = self.op("vector", lambda e, r=row: e.tensor_scalar_mul(
                r, r, 127.0), sem="ve")
            prevrec = self.cdma("sync", g["rows_hbm"][dst:dst + 1, 0:n], row,
                               waits=[("ve", vprev)], slot="c4")
            rec_t.append(prevrec)
        HF = I // 2
        vq = None
        dq_out = []
        for wname, rrow, qoff in (("wgT_sh", 7, 0), ("wuT_sh", 8, 128 * I)):
            for hv in range(2):
                st = self.fv("wqstage", 0, 128, 0, HF)
                rp = self.fv("wqrep", 0, 128, 0, HF)
                qo = self.bvv("wqout", 0, 128, 0, HF)
                w = [("ve", vq)] if vq else []
                d1 = self.dma("sync", st, g[wname][:, hv * HF:(hv + 1) * HF],
                              waits=w, slot="wqstage")
                d2 = self.dma("sync", rp,
                              g["rows_hbm"][rrow:rrow + 1, hv * HF:(hv + 1) * HF]
                              .partition_broadcast(128),
                              waits=w + [rec_t[0 if rrow == 7 else 1]],
                              slot="wqrep")
                self.op("vector", lambda e, a=st, bq=rp: e.tensor_mul(a, a, bq),
                        waits=[d1, d2, ("D_agq", self.now("D_agq"))
                               if "D_agq" in self.count else d1], sem="ve")
                vq = self.op("vector", lambda e, a=st, q=qo: e.tensor_scalar(
                    q, a, MAGIC, MAGIC, ALU.add, ALU.subtract), sem="ve")
                dq_out.append(self.dma("sync",
                              g["agq_in"][0, qoff + hv * 11 * 16384:
                                          qoff + (hv + 1) * 11 * 16384]
                              .rearrange("(it p i) -> p it i", it=11, i=128),
                              qo.rearrange("p (it i) -> p it i", i=128),
                              waits=[("ve", vq)], slot="agq"))
        row = self.fv("xp", 0, 1, 0, 128)
        dd = self.dma("sync", row, g["ag0_in"][0:1, 2240:2368],
                      waits=[("ve", vq), prevrec], slot="xprow")
        self.op("vector", lambda e, r=row: e.reciprocal(r, r), waits=[dd], sem="ve")
        v = self.op("vector", lambda e, r=row: e.tensor_scalar_mul(r, r, 127.0),
                    sem="ve")
        d9 = self.cdma("sync", g["rows_hbm"][10:11, 0:128], row,
                      waits=[("ve", v)], slot="c4")
        rp = self.fv("wqrep", 0, 128, 0, 128)
        d2 = self.dma("sync", rp, g["rows_hbm"][10:11, 0:128].partition_broadcast(128),
                      waits=[d9, ("ve", vq)], slot="wqrep")
        for hv in range(2):
            st = self.fv("wqstage", 0, 128, 0, HF)
            d1 = self.dma("sync", st.rearrange("p (it h) -> p it h", h=128),
                          g["wdT_sh"][hv * HF:(hv + 1) * HF, :]
                          .rearrange("(it p) h -> p it h", p=128),
                          waits=[("ve", vq)], slot="wqstage")
            qo = self.bvv("wqout", 0, 128, 0, HF)
            self.op("vector", lambda e, a=st, bq=rp: e.tensor_tensor(
                a.rearrange("p (it h) -> p it h", h=128),
                a.rearrange("p (it h) -> p it h", h=128),
                bq.unsqueeze(1).broadcast_to([128, 11, 128]), ALU.mult),
                waits=[d1, d2, ("D_agq", self.now("D_agq"))], sem="ve")
            vq = self.op("vector", lambda e, a=st, q=qo: e.tensor_scalar(
                q, a, MAGIC, MAGIC, ALU.add, ALU.subtract), sem="ve")
            dq_out.append(self.dma("sync",
                          g["agq_in"][0, 2 * 128 * I + hv * 11 * 16384:
                                      2 * 128 * I + (hv + 1) * 11 * 16384]
                          .rearrange("(it p h) -> p it h", it=11, h=128),
                          qo.rearrange("p (it h) -> p it h", h=128),
                          waits=[("ve", vq)], slot="agq"))
        self.op("gpsimd", lambda e: e.collective_compute(
            "AllGather", ALU.bypass, replica_groups=[list(range(NCORE))],
            ins=[g["agq_in"][:, :]], outs=[g["agq_out"][:, :]]),
            waits=[("D_agq", self.now("D_agq"))], sem="cc")
        self.cc_agq = self.now("cc")

    def p1_ln1(self):
        g = self.g
        sums = self.fv("sums", 0, 128, 0, 17)
        sums2 = self.fv("sums", 0, 128, 18, 17)
        for i in range(TT):
            xt = self.fv("xrot", 0, 128, (i % 2) * 1032, 1024)
            w = []
            if i >= 2:
                w = [("ve", self.ve_x_used[i - 2]), ("se", self.se_x_used[i - 2])]
            d = self.dma("sync", xt, g["x_tm"][3 + i * 128: 3 + (i + 1) * 128, :],
                         waits=w, slot="xr%d" % (i % 2))
            buf = self.fv("tmp1", 0, 128, 0, 1024)
            if i == 0:
                self.se_x_used, self.ve_x_used, self.ve_sq = {}, {}, {}
            sqw = [d] + ([("ve", self.ve_sq[i - 1])] if i >= 1 else [])
            s = self.op("scalar", lambda e, x=xt, q=buf: e.activation(
                q, x, AF.Square, bias=0.0, scale=1.0), waits=sqw, sem="se")
            self.se_x_used[i] = s
            self.op("vector", lambda e, x=xt, i=i: e.reduce_sum(
                sums[:, i:i + 1], x, AX.X), waits=[d], sem="ve")
            self.ve_x_used[i] = self.now("ve")
            self.op("vector", lambda e, q=buf, i=i: e.reduce_sum(
                sums2[:, i:i + 1], q, AX.X), waits=[("se", s)], sem="ve")
            self.ve_sq[i] = self.now("ve")
        vz = self.op("vector", lambda e: e.memset(sums[:, 16:17], 0.0), sem="ve")
        vz = self.op("vector", lambda e: e.memset(sums2[:, 16:17], 0.0), sem="ve")
        xh = self.fv("xhalo", 0, 3, 0, 1024)
        dh = self.dma("sync", xh, g["x_tm"][0:3, :], slot="xhalo")
        sqh = self.fv("tmp1", 0, 3, 0, 1024)
        s = self.op("scalar", lambda e: e.activation(sqh, xh, AF.Square,
                                                     bias=0.0, scale=1.0),
                    waits=[dh, ("ve", self.now("ve"))], sem="se")
        self.op("vector", lambda e: e.reduce_sum(sums[0:3, 16:17], xh, AX.X),
                waits=[dh], sem="ve")
        self.op("vector", lambda e: e.reduce_sum(sums2[0:3, 16:17], sqh, AX.X),
                waits=[("se", s)], sem="ve")
        vstats = self.now("ve")
        m = self.fv("cmb", 0, 128, 0, 17)
        var = self.fv("cmb", 0, 128, 18, 17)
        mm = self.fv("cmb", 0, 128, 36, 17)
        var2 = self.fv("cmb", 0, 128, 54, 17)
        sd = self.fv("cmb", 0, 128, 72, 17)
        a1 = self.fv("ab1", 0, 128, 0, 17)
        b1 = self.fv("ab1", 0, 128, 18, 17)
        self.op("vector", lambda e: e.tensor_scalar_mul(m, sums, 1.0 / D),
                waits=[("ve", vstats)], sem="ve")
        self.op("vector", lambda e: e.tensor_scalar(var, sums2, 1.0 / D, EPS,
                                                    ALU.mult, ALU.add), sem="ve")
        self.op("vector", lambda e: e.tensor_mul(mm, m, m), sem="ve")
        self.op("vector", lambda e: e.tensor_sub(var2, var, mm), sem="ve")
        sv = self.op("scalar", lambda e: e.activation(sd, var2, AF.Sqrt,
                                                      bias=0.0, scale=1.0),
                     waits=[("ve", self.now("ve"))], sem="se")
        self.op("vector", lambda e: e.reciprocal(a1, sd),
                waits=[("se", sv)], sem="ve")
        self.op("vector", lambda e: e.tensor_mul(b1, m, a1), sem="ve")
        self.op("vector", lambda e: e.tensor_scalar_mul(b1, b1, -1.0), sem="ve")
        self.ve_ab1 = self.now("ve")

    def p1_affine(self):
        g = self.g
        a1 = self.fv("ab1", 0, 128, 0, 17)
        b1 = self.fv("ab1", 0, 128, 18, 17)
        ps7b = self.ps[7][:, :].bitcast(BF16)
        xaffb = self.bvv("xaff", 0, 128, 0, 2048)
        h1 = self.bvv("h1T", 0, 128, 0, 8 * H1S)
        dre = {}
        self.se_aff, self.pe_tp = {}, {}
        for i in range(TT + 1):
            if i < TT:
                xt = self.fv("xrot", 0, 128, (i % 2) * 1032, 1024)
                w = ([("ve", self.ve_ab1), ("se", self.now("se"))] if i < 2
                     else [("se", self.se_aff[i - 2])])
                dre[i] = self.dma("sync", xt,
                                  g["x_tm"][3 + i * 128: 3 + (i + 1) * 128, :],
                                  waits=w, slot="xr%d" % (i % 2))
                pn, colw = 128, 128
            else:
                xt = self.fv("xhalo", 0, 3, 0, 1024)
                pn, colw = 3, 3
            xafft = xaffb[0:pn, (i % 2) * 1024:(i % 2) * 1024 + 1024]
            sa_ = self.op("scalar", lambda e, x=xt, i=i, xa=xafft, pn=pn: e.activation(
                xa, x, AF.Identity,
                bias=b1[0:pn, i:i + 1], scale=a1[0:pn, i:i + 1]),
                waits=[("ve", self.ve_ab1)] + ([dre[i]] if i < TT else []) +
                      ([("pe", self.pe_tp[i - 2])] if i >= 2 else []),
                sem="se")
            self.se_aff[i] = sa_
            for dt in range(8):
                self.op("tensor", lambda e, dt=dt, pn=pn, colw=colw, xa=xafft: e.transpose(
                    ps7b[:, dt * 128: dt * 128 + colw],
                    xa[0:pn, dt * 128: dt * 128 + 128],
                    self.sid[0:pn, 0:pn]),
                    waits=(([("se", sa_), ("se2", self.now("se2"))] +
                            ([self.ld["sid"]] if i == 0 else [])) if dt == 0 else []),
                    sem="pe" if dt == 7 else None)
            pe = self.now("pe")
            self.pe_tp[i] = pe
            for dt in range(8):
                col0 = (3 + i * 128) if i < TT else 0
                self.op("scalar", lambda e, dt=dt, col0=col0, colw=colw: e.activation(
                    h1[:, dt * H1S + col0: dt * H1S + col0 + colw],
                    ps7b[:, dt * 128: dt * 128 + colw], AF.Identity,
                    bias=self.fv("mod_cols", 0, 128, 0 + dt, 1),
                    scale=self.fv("modx", 0, 128, 0 + dt, 1)),
                    waits=([("pe", pe), ("ve", self.ve_mod)] if dt == 0 else []),
                    sem="se2")
        hl = self.fv("sc", 0, 128, 4, 1)
        vh = self.op("vector", lambda e: e.tensor_scalar(
            h1.rearrange("p (dt t) -> p dt t", t=H1S)[:, :, 0:3],
            h1.rearrange("p (dt t) -> p dt t", t=H1S)[:, :, 0:3],
            hl, None, ALU.mult),
            waits=[("se2", self.now("se2")), self.ld["halo"]], sem="ve")
        self.ve_h1T = self.now("ve")
        self.se2_h1T = self.now("se2")
        if "h1T_bf" in self.dbgo:
            self.dma("sync", self.dbgo["h1T_bf"][:, :]
                     .rearrange("(dt p) t -> p dt t", p=128),
                     h1.rearrange("p (dt t) -> p dt t", t=H1S)[:, :, 0:TH],
                     waits=[("ve", self.ve_h1T), ("se2", self.now("se2"))],
                     slot="dbg")

    def p4_mixer_in(self):
        g = self.g
        h1 = self.bvv("h1T", 0, 128, 0, 8 * H1S)
        xcins = [self.fv("xcin", 0, 128, 0, 2051),
                 self.fv("xcin2", 0, 128, 0, 2051)]
        xc0 = self.fv("xc0", 0, 128, 0, 2048)
        xcsl = self.bvv("xcsl", 0, 128, 0, 2048)
        xpwb = self.bvv("xpwb", 0, 128, 0, 16 * 34)
        ps6b = self.ps[6][:, :].bitcast(BF16)
        chunks = [(0, 512), (512, 512), (1024, 512), (1536, 512), (2048, 3)]
        wx_used = {}
        evac_ve = {}
        se_tap0 = None
        se_silu = {}
        ve_conv = {}
        pe_cons = {}
        se2_tp = None
        for es in range(16):
            half = es % 2
            xcin = xcins[half]
            wr = self.bvv("wrot", 0, 128, half * 1088, 1024)
            w = [("pe", wx_used[es - 2])] if es >= 2 else []
            dwx = self.dma("gpsimd", wr.rearrange("p (K j) -> p K j", j=128),
                           g["wxc_t"][es].transpose([1, 0, 2]),
                           waits=w, slot="wx%d" % half)
            for ci, (c0, n) in enumerate(chunks):
                bank = ci % 2
                for K in range(8):
                    wts = []
                    if K == 0:
                        wts = [dwx]
                        if (es, ci) != (0, 0) and bank in evac_ve:
                            wts.append(("se", evac_ve[bank]))
                    self.op("tensor", lambda e, K=K, c0=c0, n=n, bank=bank, wr=wr:
                            e.matmul(self.ps[bank][:, 0:n],
                                     wr[:, K * 128:(K + 1) * 128],
                                     h1[:, K * H1S + c0: K * H1S + c0 + n],
                                     start=(K == 0), stop=(K == 7)),
                            waits=wts, sem="pe" if K == 7 else None)
                pe = self.now("pe")
                evw = [("pe", pe)]
                if ci == 0 and es >= 2:
                    evw.append(("ve", ve_conv[es - 2]))
                self.op("scalar", lambda e, c0=c0, n=n, bank=bank, xcin=xcin:
                        e.activation(xcin[:, c0:c0 + n], self.ps[bank][:, 0:n],
                                     AF.Identity, bias=0.0, scale=1.0),
                        waits=evw, sem="se")
                evac_ve[bank] = self.now("se")
            wx_used[es] = self.now("pe")
            cw = self.fv("convw", 0, 128, 0, 64)
            cb = self.fv("convw", 0, 128, 64, 16)
            t0w = [("se", self.now("se"))]
            if es == 0:
                t0w += [self.ld["convw"], self.ld["convb"]]
            se_tap0 = self.op("scalar", lambda e, es=es, xcin=xcin: e.activation(
                xc0, xcin[:, 0:2048], AF.Identity,
                bias=cb[:, es:es + 1], scale=cw[:, es * 4: es * 4 + 1]),
                waits=t0w, sem="se")
            for k in (1, 2, 3):
                tw = [("se", se_tap0)] if k == 1 else []
                self.op("vector", lambda e, k=k, es=es, xcin=xcin:
                        e.scalar_tensor_tensor(
                    xc0, xcin[:, k:2048 + k], cw[:, es * 4 + k: es * 4 + k + 1],
                    xc0, ALU.mult, ALU.add), waits=tw, sem="ve")
            ve_conv[es] = self.now("ve")
            vconv = self.now("ve")
            sw = [("ve", vconv)]
            if es > 0:
                sw.append(("pe", pe_cons[es - 1]))
            se_silu[es] = self.op("scalar", lambda e: e.activation(
                xcsl, xc0, AF.Silu, bias=0.0, scale=1.0), waits=sw, sem="se")
            for n4 in range(4):
                self.op("tensor", lambda e, n4=n4, es=es: e.matmul(
                    self.ps[2 + n4][0:33, 0:512],
                    xpwb[:, es * 34: es * 34 + 33],
                    xcsl[:, n4 * 512:(n4 + 1) * 512],
                    start=(es == 0), stop=(es == 15)),
                    waits=([("se", se_silu[es])] +
                           ([self.ld["xpwb"]] if es == 0 else []))
                    if n4 == 0 else [], sem=None)
            xcs = self.bvv("xcsTM", 0, 128, 0, 16 * 2048)
            for b2 in range(2):
                for ti in range(8):
                    tt = b2 * 8 + ti
                    wts = []
                    if ti == 0:
                        wts = [("se", se_silu[es])]
                        if se2_tp is not None:
                            wts.append(("se2", se2_tp))
                    self.op("tensor", lambda e, tt=tt, ti=ti: e.transpose(
                        ps6b[:, ti * 128:(ti + 1) * 128],
                        xcsl[:, tt * 128:(tt + 1) * 128],
                        self.sid[:, :]),
                        waits=wts, sem="pe" if ti == 7 else None)
                pe = self.now("pe")
                self.op("scalar", lambda e, b2=b2, es=es: e.activation(
                    xcs.rearrange("p (ch t) -> p ch t", t=2048)
                    [:, b2 * 8:(b2 + 1) * 8, es * 128:(es + 1) * 128],
                    ps6b[:, 0:1024].rearrange("p (ti t) -> p ti t", t=128),
                    AF.Identity, bias=0.0, scale=1.0),
                    waits=[("pe", pe)], sem="se2")
                se2_tp = self.now("se2")
            pe_cons[es] = self.now("pe")
        pe = self.now("pe")
        for n4 in range(4):
            self.op("vector", lambda e, n4=n4: e.tensor_copy(
                self.fv("xp", 0, 33, n4 * 512, 512), self.ps[2 + n4][0:33, 0:512]),
                waits=[("pe", pe)] if n4 == 0 else [], sem="ve")
        self.ve_xp = self.now("ve")
        self.se2_xcs = self.now("se2")
        if "xp" in self.dbgo:
            self.dma("sync", self.dbgo["xp"][:, :], self.fv("xp", 0, 33, 0, 2048),
                     waits=[("ve", self.ve_xp)], slot="dbg")
        if "xcs_bf" in self.dbgo:
            self.dma("sync", self.dbgo["xcs_bf"][:, :],
                     self.bvv("xcsTM", 0, 128, 0, 16 * 2048),
                     waits=[("se2", self.se2_xcs)], slot="dbg2")

    def p5a_smalls(self):
        g = self.g
        comb = self.fv("comb", 0, 34, 0, 2048)
        d1 = self.cdma("sync", self.fv("comb", 0, 16, 0, 2048),
                      self.fv("xp", 17, 16, 0, 2048),
                      waits=[("ve", self.ve_xp)], slot="c2")
        d2 = self.cdma("sync", self.fv("comb", 16, 16, 0, 2048),
                      self.fv("xp", 1, 16, 0, 2048),
                      waits=[("ve", self.ve_xp)], slot="c2")
        x0 = self.fv("e3t", 64, 1, 0, 2048)
        dx0 = self.cdma("sync", x0, self.fv("xp", 0, 1, 0, 2048),
                       waits=[("ve", self.ve_xp)], slot="c2")
        sc1 = self.fv("e3u", 64, 1, 0, 2048)
        sc2 = self.fv("repA", 64, 1, 0, 2048)
        self.op("vector", lambda e: e.tensor_scalar(sc1, x0, -1.0, None,
                                                    ALU.mult),
                waits=[dx0], sem="ve")
        self.op("vector", lambda e: e.tensor_tensor(sc1, x0, sc1, ALU.min),
                sem="ve")
        sp = self.op("scalar", lambda e: e.activation(sc1, sc1, AF.Exp,
                                                      bias=0.0, scale=1.0),
                     waits=[("ve", self.now("ve"))], sem="se")
        sp = self.op("scalar", lambda e: e.activation(sc1, sc1, AF.Ln,
                                                      bias=1.0, scale=1.0), sem="se")
        self.op("vector", lambda e: e.tensor_scalar_max(sc2, x0, 0.0),
                waits=[("se", sp)], sem="ve")
        vdl = self.op("vector", lambda e: e.tensor_add(sc1, sc1, sc2), sem="ve")
        ddl = self.cdma("sync", self.fv("comb", 32, 1, 0, 2048), sc1,
                       waits=[("ve", vdl)], slot="c2")
        ones = self.fv("e3t", 64, 1, 0, 2048)
        self.op("vector", lambda e: e.memset(ones, 1.0), sem="ve")
        rgc = self.fv("repA", 64, 1, 0, 2048)
        vrg = self.op("vector", lambda e: e.tensor_tensor_scan(
            rgc, ones, sc1, 0.0, ALU.mult, ALU.add),
            waits=[("ve", self.now("ve"))], sem="ve")
        drg = self.cdma("sync", g["rows_hbm"][12:13, 0:2048], rgc,
                       waits=[("ve", vrg)], slot="c2")
        drc2 = self.cdma("sync", self.fv("comb", 33, 1, 0, 2048), rgc,
                        waits=[("ve", vrg)], slot="c2")
        vcb = self.now("ve")
        xpTc = self.fv("xpTc", 0, 128, 0, 16 * 34)
        self.se2_xpT = {}
        for i in range(NCH):
            bank = 6 + (i % 2)
            wts = ([("ve", vcb), d1, d2, ddl, drc2, self.ld["sidf"]]
                   if i < 2 else [("se2", self.se2_xpT[i - 2])])
            self.op("tensor", lambda e, i=i, bank=bank: e.transpose(
                self.ps[bank][:, 0:34],
                comb[:, i * 128:(i + 1) * 128], self.sidf[0:34, 0:34]),
                waits=wts, sem="pe")
            pe = self.now("pe")
            self.op("scalar", lambda e, i=i, bank=bank: e.activation(
                xpTc[:, i * 34:(i + 1) * 34], self.ps[bank][:, 0:34],
                AF.Identity, bias=0.0, scale=1.0),
                waits=[("pe", pe)], sem="se2")
            self.se2_xpT[i] = self.now("se2")
        st = self.fv("cmb", 0, 1, 0, 48)
        self.op("vector", lambda e: e.memset(st[:, 0:1], 0.0), sem="ve")
        dbe = self.cdma("sync", st[:, 16:32],
                       g["rows_hbm"][12, 127:2048:128].rearrange("(a c) -> a c", a=1),
                       waits=[drg], slow=True, slot="c3")
        dbs = self.cdma("sync", st[:, 1:16],
                       g["rows_hbm"][12, 127:1920:128].rearrange("(a c) -> a c", a=1),
                       waits=[drg], slow=True, slot="c3")
        vb = self.op("vector", lambda e: e.tensor_sub(
            st[:, 32:48], st[:, 16:32], st[:, 0:16]),
            waits=[dbe, dbs, ("ve", self.now("ve"))], sem="ve")
        db = self.cdma("sync", g["rows_hbm"][11:12, 0:48], st,
                      waits=[("ve", vb)], slot="c3")
        dbst = self.cdma("sync", self.fv("cmb", 0, 128, 48, 16),
                        g["rows_hbm"][11:12, 0:16].partition_broadcast(128),
                        waits=[db], slot="c3")
        dbnd = self.cdma("sync", self.fv("cmb", 0, 128, 64, 16),
                        g["rows_hbm"][11, 16:32].rearrange("(a c) -> a c", a=1)
                        .partition_broadcast(128),
                        waits=[db], slot="c3")
        drr = self.cdma("sync", self.fv("rows16", 0, 16, 0, 16),
                       g["rows_hbm"][11, 32:48].rearrange("(a c) -> a c", a=1)
                       .partition_broadcast(16),
                       waits=[db], slot="c3")
        dec = self.fv("rows16", 0, 16, 16, 16)
        self.op("scalar", lambda e: e.activation(
            dec, self.fv("rows16", 0, 16, 0, 16), AF.Exp, bias=0.0,
            scale=self.fv("acol", 0, 16, 0, 1)),
            waits=[drr, self.ld["acol"]], sem="se")
        self.se_dec = self.now("se")
        self.d_bst128 = dbst
        self.d_bnd128 = dbnd
        vprev = None
        pka = None
        for i in range(NCH):
            sl = self.fv("rch", 0, 16, 0, 128)
            w = [drg, dbst, self.ld["acol"]]
            if vprev is not None:
                w.append(("ve", vprev))
            if pka is not None:
                w.append(pka)
            dsl = self.dma("sync", sl,
                           g["rows_hbm"][12, i * 128:(i + 1) * 128]
                           .rearrange("(a c) -> a c", a=1)
                           .partition_broadcast(16),
                           waits=w, slot="rch")
            self.op("vector", lambda e, sl=sl, i=i: e.tensor_scalar(
                sl, sl, self.fv("cmb", 0, 16, 48 + i, 1),
                None, ALU.subtract), waits=[dsl], sem="ve")
            vprev = self.op("vector", lambda e, sl=sl: e.tensor_scalar(
                sl, sl, self.fv("acol", 0, 16, 0, 1),
                None, ALU.mult), sem="ve")
            pka = self.dma("sync", g["krow_hbm"][i, 0], sl,
                           waits=[("ve", vprev)], slot="karow")
            self.dma("sync", g["krow_hbm"][i, 1],
                     self.fv("comb", 0, 16, i * 128, 128),
                     waits=[d1], slot="kcrow")
        self.ve_p5a = self.now("ve")

    def p5b_z(self):
        g = self.g
        h1 = self.bvv("h1T", 0, 128, 0, 8 * H1S)
        evac_se = {}
        dst_done = {}
        wz_used = {}
        for j4 in range(4):
            half = j4 % 2
            wz = self.bvv("wzrot", 0, 128, half * 4104, 4096)
            w = [("pe", wz_used[j4 - 2])] if j4 >= 2 else []
            dwz = self.dma("gpsimd", wz.rearrange("p (K j) -> p K j", j=512),
                           g["wz_t"][:, :, j4 * 512:(j4 + 1) * 512]
                           .transpose([1, 0, 2]),
                           waits=w, slot="wz%d" % half)
            for i in range(NCH):
                bank = i % 2
                for K in range(8):
                    wts = []
                    if K == 0:
                        wts = [dwz]
                        key = (bank,)
                        if key in evac_se:
                            wts.append(("se", evac_se[key]))
                    self.op("tensor", lambda e, K=K, i=i, bank=bank, wz=wz:
                            e.matmul(self.ps[bank][:, 0:512],
                                     h1[:, K * H1S + 3 + i * 128:
                                        K * H1S + 3 + (i + 1) * 128],
                                     wz[:, K * 512:(K + 1) * 512],
                                     start=(K == 0), stop=(K == 7)),
                            waits=wts, sem="pe" if K == 7 else None)
                pe = self.now("pe")
                stg = self.fv("xrl", 0, 128, bank * 256, 256).bitcast(BF16)
                sw = [("pe", pe)]
                key2 = (bank, "d")
                if key2 in dst_done:
                    sw.append(dst_done[key2])
                self.op("scalar", lambda e, stg=stg, bank=bank: e.activation(
                    stg, self.ps[bank][:, 0:512], AF.Silu, bias=0.0, scale=1.0),
                    waits=sw, sem="se")
                evac_se[(bank,)] = self.now("se")
                dst_done[(bank, "d")] = self.dma(
                    "sync", g["sz_hbm"][i, :, j4 * 512:(j4 + 1) * 512], stg,
                    waits=[("se", evac_se[(bank,)])], slot="szst%d" % bank)
            wz_used[j4] = self.now("pe")
        self.pe_z_done = self.now("pe")
        self.d_szst_full = [("D_szst0", self.now("D_szst0")),
                            ("D_szst1", self.now("D_szst1"))]

    def p5c_loopA(self):
        g = self.g
        xcs = self.bvv("xcsTM", 0, 128, 0, 16 * 2048)
        xpTc = self.fv("xpTc", 0, 128, 0, 16 * 34)
        hloc = self.fv("hstate", 0, 16, 0, 2048)
        dec = self.fv("rows16", 0, 16, 16, 16)
        vmz = self.op("vector", lambda e: e.memset(hloc, 0.0), sem="ve")
        rec_ve = {}
        for i in range(NCH):
            dend = self.fv("cmb", 0, 128, 80, 1)
            wts = [self.d_bnd128, self.ld["arep"]] if i == 0 else []
            wts = wts + [("se2", self.se2_xpT[i])]
            self.op("vector", lambda e, i=i: e.tensor_scalar(
                dend, self.fv("cmb", 0, 128, 64 + i, 1),
                xpTc[:, i * 34 + 33: i * 34 + 34], None, ALU.subtract),
                waits=wts, sem="ve")
            wet = self.fv("cmb", 0, 128, 82, 16)
            sw = self.op("scalar", lambda e: e.activation(
                wet, self.fv("arep"), AF.Exp, bias=0.0, scale=dend),
                waits=[("ve", self.now("ve"))], sem="se")
            dbc = self.fv("cmb", 0, 128, 116, 8).bitcast(BF16)
            self.op("vector", lambda e, i=i: e.tensor_scalar(
                dbc, xpTc[:, i * 34 + 16: i * 34 + 32],
                xpTc[:, i * 34 + 32: i * 34 + 33], None, ALU.mult), sem="ve")
            ub = self.bvv("ubufs", 0, 128, i * 16, 16)
            vu = self.op("vector", lambda e, ub=ub: e.tensor_mul(ub, dbc, wet),
                         waits=[("se", sw)], sem="ve")
            for n4 in range(4):
                bank = 6 + (n4 % 2)
                wts = [("ve", vu)] if n4 == 0 else []
                if (i, n4) == (0, 0):
                    wts.append(("se2", self.se2_xpT[NCH - 1]))
                if (i, n4) >= (0, 2) and (bank, "rec") in rec_ve:
                    wts.append(("ve", rec_ve[(bank, "rec")]))
                self.op("tensor", lambda e, n4=n4, i=i, bank=bank, ub=ub: e.matmul(
                    self.ps[bank][0:16, 0:512], ub,
                    xcs[:, i * 2048 + n4 * 512: i * 2048 + (n4 + 1) * 512],
                    start=True, stop=True),
                    waits=wts, sem="pe")
                pe = self.now("pe")
                self.op("vector", lambda e, n4=n4, i=i, bank=bank:
                        e.scalar_tensor_tensor(
                            hloc[:, n4 * 512:(n4 + 1) * 512],
                            hloc[:, n4 * 512:(n4 + 1) * 512],
                            dec[:, i:i + 1],
                            self.ps[bank][0:16, 0:512], ALU.mult, ALU.add),
                        waits=[("pe", pe)] + ([("ve", vmz), ("se", self.se_dec)]
                                              if (i, n4) == (0, 0) else []),
                        sem="ve")
                rec_ve[(bank, "rec")] = self.now("ve")
        vrec = self.now("ve")
        d1 = self.cdma("sync", g["ag1_in"][0, 0:S * E]
                      .rearrange("(p c) -> p c", p=16), hloc,
                      waits=[("ve", vrec)], slot="c7")
        st2 = self.fv("cmb", 0, 1, 0, 16)
        self.op("vector", lambda e: e.memset(st2, 0.0), sem="ve")
        vr = self.op("vector", lambda e: e.tensor_copy(
            st2[:, 0:1], self.fv("cmb", 0, 1, 31, 1)),
            waits=[("D_c3", self.now("D_c3"))], sem="ve")
        d2 = self.cdma("sync", g["ag1_in"][0:1, S * E:S * E + 16], st2,
                      waits=[("ve", vr)], slot="c7")
        self.op("gpsimd", lambda e: e.collective_compute(
            "AllGather", ALU.bypass, replica_groups=[list(range(NCORE))],
            ins=[g["ag1_in"][:, :]], outs=[g["ag1_out"][:, :]]),
            waits=[d1, d2], sem="cc")
        self.cc_ag1 = self.now("cc")
        self.d_ag1send = d1

    def p5d_hin(self):
        g = self.g
        r16 = lambda p0, pn, o, n: self.fv("rows16", p0, pn, o, n)
        dom = self.cdma("sync", r16(0, 8, 32, 8), g["ordmask_t"][:, :], slot="c5")
        drc = self.cdma("sync", r16(0, 8, 40, 1),
                       g["ag1_out"][:, S * E:S * E + 1],
                       waits=[("cc", self.cc_ag1)], slow=True, slot="c5")
        self.op("tensor", lambda e: e.matmul(
            self.ps[5][0:8, 0:1], r16(0, 8, 32, 8), r16(0, 8, 40, 1),
            start=True, stop=True), waits=[dom, drc], sem="pe")
        pe = self.now("pe")
        vs = self.op("vector", lambda e: e.tensor_copy(
            r16(0, 8, 42, 1), self.ps[5][0:8, 0:1]), waits=[("pe", pe)], sem="ve")
        ds = self.cdma("sync", g["rows_hbm"][13, 0:8].rearrange("(c a) -> c a", a=1),
                      r16(0, 8, 42, 1), waits=[("ve", vs)], slow=True, slot="c5")
        dsr = self.cdma("sync", r16(0, 16, 44, 8),
                       g["rows_hbm"][13:14, 0:8].partition_broadcast(16),
                       waits=[ds], slot="c5")
        dpm = self.cdma("sync", r16(0, 16, 52, 8),
                       g["pmask"][0:1, :].partition_broadcast(16), slot="c5")
        sc = self.op("scalar", lambda e: e.activation(
            r16(0, 16, 60, 8), r16(0, 16, 44, 8), AF.Exp, bias=0.0,
            scale=self.fv("acol", 0, 16, 0, 1)),
            waits=[dsr, self.ld["acol"]], sem="se")
        vc = self.op("vector", lambda e: e.tensor_mul(
            r16(0, 16, 60, 8), r16(0, 16, 60, 8), r16(0, 16, 52, 8)),
            waits=[("se", sc), dpm], sem="ve")
        hin = self.fv("hstate", 0, 16, 0, 2048)
        vmz = self.op("vector", lambda e: e.memset(hin, 0.0),
                      waits=[self.d_ag1send], sem="ve")
        vprev = None
        for j in range(NCORE):
            half = j % 2
            sl = self.fv("xp" if half == 0 else "xcin2", 0, 16, 0, 2048)
            w = [("cc", self.cc_ag1)]
            if j == 1:
                w += self.d_szst_full
            if j >= 2:
                w.append(("ve", vprev))
            dj = self.dma("sync", sl, g["ag1_out"][j, 0:S * E]
                          .rearrange("(p c) -> p c", p=16),
                          waits=w, slot="hj%d" % half)
            vprev = self.op("vector", lambda e, sl=sl, j=j: e.scalar_tensor_tensor(
                hin, sl, r16(0, 16, 60 + j, 1), hin, ALU.mult, ALU.add),
                waits=[dj] + ([("ve", vc), ("ve", vmz)] if j == 0 else []),
                sem="ve")
        hb = self.bvv("hinbf", 0, 16, 0, 2048)
        self.op("vector", lambda e: e.tensor_copy(hb, hin), sem="ve")
        self.ve_hin = self.now("ve")

    def p5e_pre(self):
        g = self.g
        wot = self.bvv("wot", 0, 128, 0, 16 * 1024)
        dws = []
        for K in range(16):
            dws.append(self.dma("gpsimd", wot[:, K * 1024:(K + 1) * 1024],
                                g["wout_t"][K], waits=[("pe", self.pe_z_done)]
                                if K == 0 else [], slot="wotl"))
        mx = self.fv("modx")
        dga = self.cdma("sync", g["rows_hbm"][14, 0:1024]
                       .rearrange("(dt p) -> p dt", p=128), mx[:, 16:24],
                       waits=[("ve", self.ve_mod)], slow=True, slot="c6")
        dgr = self.cdma("sync", self.fv("garepf", 0, 128, 0, 1024),
                       g["rows_hbm"][14:15, 0:1024].partition_broadcast(128),
                       waits=[dga], slot="c6")
        gab = self.bvv("garep", 0, 128, 0, 1024)
        vg = self.op("vector", lambda e: e.tensor_copy(
            gab, self.fv("garepf", 0, 128, 0, 1024)), waits=[dgr], sem="ve")
        wfull = ("D_wotl", self.now("D_wotl"))
        for K in range(16):
            self.op("vector", lambda e, K=K: e.tensor_mul(
                wot[:, K * 1024:(K + 1) * 1024],
                wot[:, K * 1024:(K + 1) * 1024], gab),
                waits=[wfull, ("ve", vg)] if K == 0 else [], sem="ve")
        self.ve_wot = self.now("ve")
        self.d_drep = self.dma("gpsimd", self.bvv("drep", 0, 128, 0, 2048),
                               g["dparam"][0:1, :].partition_broadcast(128),
                               waits=[dgr, ("ve", vg)], slot="drep")

    def p5e_loopB(self):
        g = self.g
        xcs = self.bvv("xcsTM", 0, 128, 0, 16 * 2048)
        xpTc = self.fv("xpTc", 0, 128, 0, 16 * 34)
        hloc = self.fv("hstate", 0, 16, 0, 2048)
        dec = self.fv("rows16", 0, 16, 16, 16)
        repA = self.fv("repA", 0, 128, 0, 2048)
        repC = self.bvv("repC", 0, 128, 0, 2048)
        e3t = self.fv("e3t", 0, 128, 0, 2048)
        e3u = self.fv("e3u", 0, 128, 0, 2048)
        wot = self.bvv("wot", 0, 128, 0, 16 * 1024)
        hb = self.bvv("hinbf", 0, 16, 0, 2048)
        drep = self.bvv("drep", 0, 128, 0, 2048)
        ps6b = self.ps[6][:, :].bitcast(BF16)
        ps7b = self.ps[7][:, :].bitcast(BF16)
        vmz = None
        st = {}
        ctx = {}
        hsns = {}

        def emit_hsn(i):
            half = i % 2
            hsn = self.bvv("hsn0" if half == 0 else "hsn1", 0, 16, 0, 2048)
            w = [("ve", vmz)] if i == 0 else [("ve", st[("recb", i - 1, 3)])]
            if ("ymm", i - 2) in st:
                w.append(("pe", st[("ymm", i - 2)]))
            vsn = self.op("scalar", lambda e, hsn=hsn: e.activation(
                hsn, hloc, AF.Identity, bias=0.0, scale=1.0),
                waits=w, sem="se")
            hsns[i] = (hsn, vsn)

        def stageA(i):
            half = i % 2
            if i >= 2:
                emit_hsn(i - 1)
            dsz = self.dma("sync", self.bvv("szrot", 0, 128, half * 2052, 2048),
                           g["sz_hbm"][i], waits=[st[("y2c", i - 2)]]
                           if ("y2c", i - 2) in st else
                           ([("pe", self.pe_z_done)] if i < 2 else []),
                           slot="szl%d" % half)
            dra = self.dma("sync", repA,
                           g["krow_hbm"][i:i + 1, 0, :].partition_broadcast(128),
                           waits=([("ve", st[("xsub", i - 1)])]
                                  if ("xsub", i - 1) in st else
                                  [("D_karow", self.now("D_karow")),
                                   ("D_c2", self.now("D_c2"))]),
                           slot="repA")
            drc = self.dma("gpsimd", repC,
                           g["krow_hbm"][i:i + 1, 1, :].partition_broadcast(128),
                           waits=([("ve", st[("crep", i - 1)])]
                                  if ("crep", i - 1) in st else
                                  [("D_kcrow", self.now("D_kcrow"))]),
                           slot="repC")
            rlc = self.fv("cmb", 0, 128, 114, 1)
            self.op("vector", lambda e, i=i: e.tensor_scalar(
                rlc, xpTc[:, i * 34 + 33: i * 34 + 34],
                self.fv("cmb", 0, 128, 48 + i, 1), None, ALU.subtract),
                waits=([self.d_bst128] if i == 0 else []) +
                [("se2", self.se2_xpT[i])], sem="ve")
            rA = self.fv("cmb", 0, 128, 82, 16)
            self.op("vector", lambda e: e.tensor_tensor(
                rA, self.fv("arep"), rlc.broadcast_to([128, 16]), ALU.mult),
                sem="ve")
            vx = self.op("vector", lambda e: e.tensor_tensor(
                e3t.rearrange("p (s t) -> p s t", t=128),
                repA.rearrange("p (s t) -> p s t", t=128),
                rA.unsqueeze(2).broadcast_to([128, 16, 128]), ALU.subtract),
                waits=[dra], sem="ve")
            st[("xsub", i)] = self.now("ve")
            self.op("vector", lambda e: e.tensor_scalar_min(e3t, e3t, 0.0),
                    sem="ve")
            sx = self.op("scalar", lambda e: e.activation(
                e3u, e3t, AF.Exp, bias=0.0, scale=1.0),
                waits=[("ve", self.now("ve"))], sem="se")
            dbc = self.fv("cmb", 0, 128, 116, 8).bitcast(BF16)
            self.op("vector", lambda e, i=i: e.tensor_scalar(
                dbc, xpTc[:, i * 34 + 16: i * 34 + 32],
                xpTc[:, i * 34 + 32: i * 34 + 33], None, ALU.mult),
                waits=[("se", sx)], sem="ve")
            self.op("vector", lambda e: e.tensor_tensor(
                e3u.rearrange("p (s t) -> p s t", t=128),
                e3u.rearrange("p (s t) -> p s t", t=128),
                dbc.unsqueeze(2).broadcast_to([128, 16, 128]), ALU.mult),
                sem="ve")
            self.op("vector", lambda e: e.tensor_mul(e3u, e3u, repC),
                    waits=[drc], sem="ve")
            st[("crep", i)] = self.now("ve")
            kbf = self.fv("tmp0", 0, 128, 648, 128)
            self.op("vector", lambda e: e.tensor_reduce(
                kbf, e3u.rearrange("p (s t) -> p t s", t=128), AX.X, ALU.add),
                sem="ve")
            kb = self.bvv("kbslot", 0, 128, half * 128, 128)
            vkb = self.op("vector", lambda e, kb=kb: e.tensor_mul(
                kb, kbf, self.stril[:, :]), waits=[self.ld["tril"]]
                if i == 0 else [], sem="ve")
            st[("kb", i)] = self.now("ve")
            rch = self.fv("rch", 0, 16, 0, 128)
            drch = self.dma("sync", rch,
                            g["rows_hbm"][12, i * 128:(i + 1) * 128]
                            .rearrange("(a c) -> a c", a=1)
                            .partition_broadcast(16),
                            waits=[("se", st[("gexp", i - 1)])]
                            if ("gexp", i - 1) in st else
                            [("ve", self.ve_p5a)], slot="rch")
            eg = self.fv("e3t", 0, 16, 0, 128)
            el = self.fv("e3t", 0, 16, 128, 128)
            sg1 = self.op("scalar", lambda e: e.activation(
                eg, rch, AF.Exp, bias=0.0, scale=self.fv("acol", 0, 16, 0, 1)),
                waits=[drch, ("ve", st[("xsub", i)])], sem="se")
            gsl = self.bvv("gslot", 0, 16, half * 128, 128)
            gglsl = self.bvv("gslot", 0, 16, 256 + half * 128, 128)
            vggl = self.op("vector", lambda e, gglsl=gglsl, i=i: e.tensor_mul(
                gglsl, self.fv("comb", 0, 16, i * 128, 128), eg),
                waits=[("se", sg1)], sem="ve")
            self.op("vector", lambda e, i=i: e.tensor_scalar(
                rch, rch, self.fv("cmb", 0, 16, 48 + i, 1), None, ALU.subtract),
                sem="ve")
            sg2 = self.op("scalar", lambda e: e.activation(
                el, rch, AF.Exp, bias=0.0, scale=self.fv("acol", 0, 16, 0, 1)),
                waits=[("ve", self.now("ve"))], sem="se")
            st[("gexp", i)] = self.now("se")
            vgl = self.op("vector", lambda e, gsl=gsl, i=i: e.tensor_mul(
                gsl, self.fv("comb", 0, 16, i * 128, 128), el),
                waits=[("se", sg2)], sem="ve")
            st[("g", i)] = self.now("ve")
            ctx[i] = (dsz, kb, gsl, gglsl)

        def stageB(i):
            half = i % 2
            dsz, kb, gsl, gglsl = ctx[i]
            if i not in hsns:
                emit_hsn(i)
            hsn, vsn = hsns[i]
            for n4 in range(4):
                bank = 2 + n4
                w1 = [("ve", st[("kb", i)])]
                if ("y2v", i - 1, n4) in st:
                    w1.append(("ve", st[("y2v", i - 1, n4)]))
                self.op("tensor", lambda e, n4=n4, i=i, bank=bank, kb=kb: e.matmul(
                    self.ps[bank][:, 0:512], kb,
                    xcs[:, i * 2048 + n4 * 512: i * 2048 + (n4 + 1) * 512],
                    start=True, stop=False), waits=w1, sem=None)
                self.op("tensor", lambda e, n4=n4, bank=bank, gsl=gsl, hsn=hsn:
                        e.matmul(self.ps[bank][:, 0:512], gsl,
                                 hsn[:, n4 * 512:(n4 + 1) * 512],
                                 start=False, stop=False),
                        waits=[("ve", st[("g", i)]), ("se", vsn)]
                        if n4 == 0 else [], sem=None)
                self.op("tensor", lambda e, n4=n4, bank=bank, gglsl=gglsl:
                        e.matmul(self.ps[bank][:, 0:512], gglsl,
                                 hb[:, n4 * 512:(n4 + 1) * 512],
                                 start=False, stop=True),
                        waits=[("ve", self.ve_hin)] if (i, n4) == (0, 0) else [],
                        sem="pe")
            st[("ymm", i)] = self.now("pe")
            ub = self.bvv("ubufs", 0, 128, i * 16, 16)
            for n4 in range(4):
                bank = 6 + (n4 % 2)
                wts = []
                if ("tp", i - 1) in st and n4 < 2:
                    wts.append(("se2", st[("tp", i - 1)]))
                if ("recb", i, n4 - 2) in st:
                    wts.append(("ve", st[("recb", i, n4 - 2)]))
                self.op("tensor", lambda e, n4=n4, i=i, bank=bank, ub=ub: e.matmul(
                    self.ps[bank][0:16, 0:512], ub,
                    xcs[:, i * 2048 + n4 * 512: i * 2048 + (n4 + 1) * 512],
                    start=True, stop=True), waits=wts, sem="pe")
                pe = self.now("pe")
                self.op("vector", lambda e, n4=n4, i=i, bank=bank:
                        e.scalar_tensor_tensor(
                            hloc[:, n4 * 512:(n4 + 1) * 512],
                            hloc[:, n4 * 512:(n4 + 1) * 512],
                            dec[:, i:i + 1],
                            self.ps[bank][0:16, 0:512], ALU.mult, ALU.add),
                        waits=[("pe", pe), ("se", vsn)], sem="ve")
                st[("recb", i, n4)] = self.now("ve")
            y2sl = self.bvv("y2r", 0, 128, 0, 2048)
            for n4 in range(4):
                bank = 2 + n4
                t1 = self.fv("tmp0", 0, 128, 520, 256).bitcast(BF16)
                t2 = self.fv("tmp1", 0, 128, 520, 512)
                self.op("vector", lambda e, n4=n4, i=i, t1=t1: e.tensor_mul(
                    t1, xcs[:, i * 2048 + n4 * 512: i * 2048 + (n4 + 1) * 512],
                    drep[:, n4 * 512:(n4 + 1) * 512]),
                    waits=[self.d_drep] if (i, n4) == (0, 0) else [], sem="ve")
                self.op("vector", lambda e, bank=bank, t1=t1, t2=t2: e.tensor_add(
                    t2, self.ps[bank][:, 0:512], t1),
                    waits=[("pe", st[("ymm", i)])] if n4 == 0 else [], sem="ve")
                self.op("vector", lambda e, n4=n4, t2=t2, y2sl=y2sl, half=half:
                        e.tensor_mul(y2sl[:, n4 * 512:(n4 + 1) * 512], t2,
                                     self.bvv("szrot", 0, 128, half * 2052, 2048)
                                     [:, n4 * 512:(n4 + 1) * 512]),
                        waits=([dsz] if n4 == 0 else []) +
                        ([self._dbgy] if n4 == 0 and getattr(self, "_dbgy", None)
                         else []), sem="ve")
                st[("y2v", i, n4)] = self.now("ve")
            st[("y2c", i)] = ("ve", self.now("ve"))
            if "y2_bf" in self.dbgo:
                self._dbgy = self.dma("sync", self.dbgo["y2_bf"][i], y2sl,
                                      waits=[st[("y2c", i)]], slot="dbgy")
            for b2 in range(2):
                psb = ps6b if b2 == 0 else ps7b
                for ti in range(8):
                    es = b2 * 8 + ti
                    wts = []
                    if ti == 0:
                        wts = [("ve", st[("y2v", i, 3)]),
                               ("ve", st[("recb", i, 3 if b2 == 0 else 2)])]
                        if ("opj", i - 1) in st:
                            wts.append(("pe", st[("opj", i - 1)]))
                    self.op("tensor", lambda e, es=es, ti=ti, psb=psb, y2sl=y2sl:
                            e.transpose(psb[:, ti * 128:(ti + 1) * 128],
                                        y2sl[:, es * 128:(es + 1) * 128],
                                        self.sid[:, :]),
                            waits=wts, sem="pe" if ti == 7 else None)
                pe = self.now("pe")
                self.op("scalar", lambda e, b2=b2, psb=psb: e.activation(
                    self.bvv("y2f", 0, 128, b2 * 1024, 1024), psb[:, 0:1024],
                    AF.Identity, bias=0.0, scale=1.0),
                    waits=[("pe", pe)], sem="se2")
            st[("tp", i)] = self.now("se2")
            y2f = self.bvv("y2f", 0, 128, 0, 2048)
            dxr = self.dma("sync", self.fv("xrl", 0, 128, 0, 1024),
                           g["x_tm"][3 + i * 128: 3 + (i + 1) * 128, :],
                           waits=[("ve", st[("x2v", i - 1)])]
                           if ("x2v", i - 1) in st else
                           ([("pe", self.pe_z_done), ("ve", self.ve_hin)]
                            + self.d_szst_full if i == 0 else []),
                           slot="xrl")
            for dN in range(2):
                for Ke in range(16):
                    wts = []
                    if Ke == 0:
                        wts = [("se2", st[("tp", i)])]
                        if (i, dN) == (0, 0):
                            wts.append(("ve", self.ve_wot))
                        if ("x2v", i - 1) in st:
                            wts.append(("ve", st[("x2v", i - 1)]))
                    self.op("tensor", lambda e, dN=dN, Ke=Ke, y2f=y2f: e.matmul(
                        self.ps[dN][:, 0:512], y2f[:, Ke * 128:(Ke + 1) * 128],
                        wot[:, Ke * 1024 + dN * 512: Ke * 1024 + dN * 512 + 512],
                        start=(Ke == 0), stop=(Ke == 15)),
                        waits=wts, sem="pe" if Ke == 15 else None)
            st[("opj", i)] = self.now("pe")
            pe = self.now("pe")
            for dN in range(2):
                xo = (self.fv("tmp0", 0, 128, 0, 512) if dN == 0
                      else self.fv("tmp1", 0, 128, 0, 512))
                self.op("vector", lambda e, dN=dN, xo=xo: e.tensor_add(
                    xo, self.fv("xrl", 0, 128, dN * 512, 512),
                    self.ps[dN][:, 0:512]),
                    waits=[("pe", pe), dxr] + ([st[("x2st", i - 1, dN)]]
                                               if ("x2st", i - 1, dN) in st
                                               else []), sem="ve")
                st[("x2v", i)] = self.now("ve")
                st[("x2st", i, dN)] = self.dma(
                    "sync", g["x2_hbm"][i * 128:(i + 1) * 128,
                                        dN * 512:dN * 512 + 512], xo,
                    waits=[("ve", self.now("ve"))], slot="x2st%d" % dN)

        stageA(0)
        stageA(1)
        self.p5d_hin()
        vmz = self.op("vector", lambda e: e.memset(hloc, 0.0),
                      waits=[("ve", self.ve_hin)], sem="ve")
        emit_hsn(0)
        for i in range(NCH):
            stageB(i)
            if i + 2 < NCH:
                stageA(i + 2)
        self.ve_p5 = self.now("ve")
        self.st_p5 = st

    def mband(self, it):
        if it < 16:
            return self.bvv("xcsTM", 0, 128, it * 2048, 2048)
        return self.bvv("R1", 0, 128, (it - 16) * 2048, 2048)

    def p7_ln2(self):
        g = self.g
        h2 = self.fv("h2TM", 0, 128, 0, 16 * 1024)
        mx = self.fv("modx")
        mc = self.fv("mod_cols")
        d1 = self.cdma("sync", g["rows_hbm"][18, 0:1024]
                      .rearrange("(dt p) -> p dt", p=128), mx[:, 8:16],
                      slow=True, slot="c8")
        d2 = self.cdma("sync", g["rows_hbm"][17, 0:1024]
                      .rearrange("(dt p) -> p dt", p=128), mc[:, 24:32],
                      slow=True, slot="c8")
        dcb = self.cdma("sync", self.fv("cbrep", 0, 128, 0, 1024),
                       g["rows_hbm"][18:19, 0:1024].partition_broadcast(128),
                       waits=[d1], slot="c8")
        dsb = self.cdma("sync", self.fv("sbrep", 0, 128, 0, 1024),
                       g["rows_hbm"][17:18, 0:1024].partition_broadcast(128),
                       waits=[d2], slot="c8")
        x2w = [("D_x2st0", self.now("D_x2st0")),
               ("D_x2st1", self.now("D_x2st1"))]
        amc = self.fv("cmb", 0, 128, 0, 1)
        vam = self.op("vector", lambda e: e.memset(amc, 0.0), sem="ve")
        sums = self.fv("cmb", 0, 128, 48, 16)
        sums2 = self.fv("cmb", 0, 128, 64, 16)
        a2c = self.fv("cmb", 0, 128, 80, 16)
        b2c = self.fv("cmb", 0, 128, 96, 16)
        st = {}
        dl = {}
        for ch in range(NCH):
            h2c = h2[:, ch * 1024:(ch + 1) * 1024]
            w = list(x2w) if ch < 2 else [dl[ch - 2]]
            dl[ch] = self.dma("sync", h2c,
                              g["x2_hbm"][ch * 128:(ch + 1) * 128, :],
                              waits=w, slot="xr%d" % (ch % 2))
            sq = self.fv("tmp0", 0, 128, 0, 1024) if ch % 2 == 0 else \
                self.fv("tmp1", 0, 128, 0, 1024)
            sqs = self.op("scalar", lambda e, x=h2c, q=sq: e.activation(
                q, x, AF.Square, bias=0.0, scale=1.0),
                waits=[dl[ch]] + ([("ve", st[("s2", ch - 2)])]
                                  if ch >= 2 else []),
                sem="se")
            self.op("vector", lambda e, x=h2c, ch=ch: e.reduce_sum(
                sums[:, ch:ch + 1], x, AX.X), waits=[dl[ch]], sem="ve")
            self.op("vector", lambda e, q=sq, ch=ch: e.reduce_sum(
                sums2[:, ch:ch + 1], q, AX.X), waits=[("se", sqs)], sem="ve")
            st[("s2", ch)] = self.now("ve")
        self.op("vector", lambda e: e.tensor_scalar_mul(sums, sums, 1.0 / D),
                sem="ve")
        self.op("vector", lambda e: e.tensor_scalar(
            sums2, sums2, 1.0 / D, EPS, ALU.mult, ALU.add), sem="ve")
        self.op("vector", lambda e: e.tensor_mul(b2c, sums, sums), sem="ve")
        self.op("vector", lambda e: e.tensor_sub(sums2, sums2, b2c), sem="ve")
        sv = self.op("scalar", lambda e: e.activation(
            sums2, sums2, AF.Sqrt, bias=0.0, scale=1.0),
            waits=[("ve", self.now("ve"))], sem="se")
        self.op("vector", lambda e: e.reciprocal(a2c, sums2),
                waits=[("se", sv)], sem="ve")
        self.op("vector", lambda e: e.tensor_mul(b2c, sums, a2c), sem="ve")
        vb2 = self.op("vector", lambda e: e.tensor_scalar_mul(b2c, b2c, -1.0),
                      sem="ve")
        for ch in range(NCH):
            h2c = h2[:, ch * 1024:(ch + 1) * 1024]
            sa_ = self.op("scalar", lambda e, o=h2c, ch=ch: e.activation(
                o, o, AF.Identity, bias=b2c[:, ch:ch + 1],
                scale=a2c[:, ch:ch + 1]),
                waits=[("ve", vb2)] if ch == 0 else [], sem="se")
            self.op("vector", lambda e, o=h2c: e.tensor_mul(
                o, o, self.fv("cbrep", 0, 128, 0, 1024)),
                waits=[("se", sa_), dcb] if ch == 0 else [("se", sa_)],
                sem="ve")
            self.op("vector", lambda e, o=h2c: e.tensor_add(
                o, o, self.fv("sbrep", 0, 128, 0, 1024)),
                waits=[dsb] if ch == 0 else [], sem="ve")
            tc = self.fv("cmb", 0, 128, 14, 1)
            self.op("vector", lambda e, o=h2c: e.reduce_max(
                tc, o, AX.X, apply_absolute_value=True), sem="ve")
            self.op("vector", lambda e: e.tensor_tensor(amc, amc, tc, ALU.max),
                    waits=[("ve", vam)] if ch == 0 else [], sem="ve")
        vamf = self.now("ve")
        dsr = self.cdma("sync", g["rows_hbm"][16, 0:128]
                       .rearrange("(c a) -> c a", a=1), amc,
                       waits=[("ve", vamf)], slow=True, slot="c9")
        arow = self.fv("xp", 0, 1, 0, 128)
        dld = self.cdma("sync", arow, g["rows_hbm"][16:17, 0:128],
                       waits=[dsr], slot="c9")
        gsc = self.fv("xp", 0, 1, 130, 1)
        vg = self.op("vector", lambda e: e.reduce_max(gsc, arow, AX.X),
                     waits=[dld], sem="ve")
        dag = self.cdma("sync", g["ag2_in"][:, :], gsc, waits=[("ve", vg)],
                       slot="c9")
        self.op("gpsimd", lambda e: e.collective_compute(
            "AllReduce", ALU.max, replica_groups=[list(range(NCORE))],
            ins=[g["ag2_in"][:, :]], outs=[g["ag2_out"][:, :]]),
            waits=[dag], sem="cc")
        cc2 = self.now("cc")
        gx = self.fv("cmb", 0, 128, 18, 1)
        dgx = self.cdma("sync", gx, g["ag2_out"][0:1, :].partition_broadcast(128),
                       waits=[("cc", cc2)], slot="c9")
        cxc = self.fv("cmb", 0, 128, 20, 1)
        dcx = self.cdma("sync", cxc, g["clip_x"][0:1, :].partition_broadcast(128),
                       slot="c9")
        ssx = self.op("scalar", lambda e: e.activation(
            cxc, cxc, AF.Sigmoid, bias=0.0, scale=1.0), waits=[dcx], sem="se")
        tsg = self.fv("cmb", 0, 128, 22, 1)
        self.op("vector", lambda e: e.tensor_mul(tsg, cxc, gx),
                waits=[("se", ssx), dgx], sem="ve")
        sxi = self.fv("cmb", 0, 128, 24, 1)
        self.op("vector", lambda e: e.reciprocal(sxi, tsg), sem="ve")
        vsx = self.op("vector", lambda e: e.tensor_scalar_mul(sxi, sxi, 127.0),
                      sem="ve")
        self.tsg_col = tsg
        self.mgc = self.fv("cmb", 0, 128, 42, 1)
        self.nmgc = self.fv("cmb", 0, 128, 44, 1)
        self.op("vector", lambda e: e.memset(self.mgc, MAGIC), sem="ve")
        vmg = self.op("vector", lambda e: e.memset(self.nmgc, -MAGIC), sem="ve")
        self.vmg = vmg
        qx = self.bvv("qx", 0, 128, 0, 16384)
        ps7b = self.ps[7][:, :].bitcast(BF16)
        for ch in range(NCH):
            h2c = h2[:, ch * 1024:(ch + 1) * 1024]
            tq = self.fv("tmp0", 0, 128, 0, 1024) if ch % 2 == 0 else \
                self.fv("tmp1", 0, 128, 0, 1024)
            qs = self.bvv("y2r", 0, 128, (ch % 2) * 1024, 1024)
            w = ([("ve", vsx), ("ve", vmg)] if ch < 2
                 else [("se2", st[("qev", ch - 2)])])
            s1 = self.op("scalar", lambda e, i=h2c, o=tq: e.activation(
                o, i, AF.Identity, bias=self.mgc, scale=sxi),
                waits=w, sem="se")
            s2 = self.op("scalar", lambda e, o=tq, q=qs: e.activation(
                q, o, AF.Identity, bias=self.nmgc, scale=1.0), sem="se")
            vq = self.op("vector", lambda e, q=qs: e.tensor_scalar(
                q, q, 127.0, -128.0, ALU.min, ALU.max),
                waits=[("se", s2)], sem="ve")
            for dt in range(8):
                self.op("tensor", lambda e, dt=dt, q=qs: e.transpose(
                    ps7b[:, dt * 128:(dt + 1) * 128],
                    q[:, dt * 128:(dt + 1) * 128], self.sid[:, :]),
                    waits=[("ve", vq), ("se2", self.now("se2"))]
                    if dt == 0 else [],
                    sem="pe" if dt == 7 else None)
            pe = self.now("pe")
            for dt in range(8):
                self.op("scalar", lambda e, dt=dt, ch=ch: e.activation(
                    qx[:, dt * 2048 + ch * 128: dt * 2048 + (ch + 1) * 128],
                    ps7b[:, dt * 128:(dt + 1) * 128], AF.Identity,
                    bias=0.0, scale=1.0),
                    waits=[("pe", pe)] if dt == 0 else [], sem="se2")
            st[("qev", ch)] = self.now("se2")
        self.se2_qx = self.now("se2")

    def p8a_gateup(self):
        g = self.g
        qx = self.bvv("qx", 0, 128, 0, 16384)
        fcol = self.fv("cmb", 0, 128, 26, 1)
        vf = self.op("vector", lambda e: e.tensor_scalar_mul(
            fcol, self.tsg_col, 1.0 / 16129.0), sem="ve")
        sgc = self.fv("scl_g", 0, 128, 0, 22)
        suc = self.fv("scl_u", 0, 128, 0, 22)
        vsc = []
        for nm, row, dst in (("sg", 4, sgc), ("su", 5, suc)):
            d = self.cdma("sync", dst, g["rows_hbm"][row, 0:I]
                         .rearrange("(it p) -> p it", p=128),
                         slow=True, slot="c10")
            vsc.append(self.op("vector", lambda e, a=dst: e.tensor_scalar(
                a, a, fcol, None, ALU.mult), waits=[d, ("ve", vf)], sem="ve"))
        amc = self.fv("cmb", 0, 128, 28, 1)
        vam = self.op("vector", lambda e: e.memset(amc, 0.0), sem="ve")
        st = {}
        for it in range(IT):
            half = it % 2
            wg = self.bvv("qwrot", 0, 128, half * 4096, 1024)
            wu = self.bvv("qwrot", 0, 128, half * 4096 + 1024, 1024)
            lw = [("cc", self.cc_agq)] if it == 0 else \
                ([("pe", st[("mml", it - 2)])] if it >= 2 else [])
            dg = self.dma("sync", wg.rearrange("p (c i) -> p c i", i=128),
                          g["agq_out"][:, it * 16384:(it + 1) * 16384]
                          .rearrange("c (p i) -> p c i", p=128),
                          waits=lw, slot="qwg%d" % half)
            du = self.dma("sync", wu.rearrange("p (c i) -> p c i", i=128),
                          g["agq_out"][:, 128 * I + it * 16384:
                                       128 * I + (it + 1) * 16384]
                          .rearrange("c (p i) -> p c i", p=128),
                          waits=lw, slot="qwu%d" % half)
            for tb in range(4):
                bg, bu = tb % 2, 2 + tb % 2
                for dt in range(8):
                    w = []
                    if dt == 0:
                        w = [dg] + ([("se", st[("sil", it, tb - 2)])]
                                    if ("sil", it, tb - 2) in st else []) + \
                            ([("se", st[("sil", it - 1, tb + 2)])]
                             if ("sil", it - 1, tb + 2) in st else []) + \
                            ([("se2", self.se2_qx)] if (it, tb) == (0, 0) else [])
                    self.op("tensor", lambda e, dt=dt, tb=tb, bg=bg, wg=wg:
                            e.matmul(self.ps[bg][:, 0:512],
                                     wg[:, dt * 128:(dt + 1) * 128],
                                     qx[:, dt * 2048 + tb * 512:
                                        dt * 2048 + (tb + 1) * 512],
                                     start=(dt == 0), stop=(dt == 7)),
                            waits=w, sem="pe" if dt == 7 else None)
                for dt in range(8):
                    w = []
                    if dt == 0:
                        w = [du] + ([("ve", st[("mst", it, tb - 2)])]
                                    if ("mst", it, tb - 2) in st else []) + \
                            ([("ve", st[("mst", it - 1, tb + 2)])]
                             if ("mst", it - 1, tb + 2) in st else [])
                    self.op("tensor", lambda e, dt=dt, tb=tb, bu=bu, wu=wu:
                            e.matmul(self.ps[bu][:, 0:512],
                                     wu[:, dt * 128:(dt + 1) * 128],
                                     qx[:, dt * 2048 + tb * 512:
                                        dt * 2048 + (tb + 1) * 512],
                                     start=(dt == 0), stop=(dt == 7)),
                            waits=w, sem="pe" if dt == 7 else None)
                pe = self.now("pe")
                if tb == 3:
                    st[("mml", it)] = pe
                sgb = self.fv("tmp0", 0, 128, 0, 512) if tb % 2 == 0 else \
                    self.fv("tmp1", 0, 128, 0, 512)
                ssil = self.op("scalar", lambda e, bg=bg, it=it, o=sgb:
                               e.activation(o, self.ps[bg][:, 0:512], AF.Silu,
                                            bias=0.0,
                                            scale=sgc[:, it:it + 1]),
                               waits=[("pe", pe)] +
                               ([("ve", vsc[0])] if it == 0 and tb == 0 else []),
                               sem="se")
                st[("sil", it, tb)] = ssil
                msl = self.mband(it)[:, tb * 512:(tb + 1) * 512]
                self.op("vector", lambda e, bu=bu, it=it, o=msl, sgb=sgb:
                        e.scalar_tensor_tensor(o, self.ps[bu][:, 0:512],
                                               suc[:, it:it + 1], sgb,
                                               ALU.mult, ALU.mult),
                        waits=[("se", ssil)] +
                        ([("ve", vsc[1])] if it == 0 and tb == 0 else []),
                        sem="ve")
                tc = self.fv("cmb", 0, 128, 30, 1)
                self.op("vector", lambda e, o=msl: e.reduce_max(
                    tc, o, AX.X, apply_absolute_value=True), sem="ve")
                self.op("vector", lambda e: e.tensor_tensor(
                    amc, amc, tc, ALU.max),
                    waits=[("ve", vam)] if (it, tb) == (0, 0) else [], sem="ve")
                st[("mst", it, tb)] = self.now("ve")
        vamf = self.now("ve")
        dsr = self.cdma("sync", g["rows_hbm"][19, 0:128]
                       .rearrange("(c a) -> c a", a=1), amc,
                       waits=[("ve", vamf)], slow=True, slot="c11")
        arow = self.fv("xp", 0, 1, 132, 128)
        dld = self.cdma("sync", arow, g["rows_hbm"][19:20, 0:128],
                       waits=[dsr], slot="c11")
        gsc = self.fv("xp", 0, 1, 262, 1)
        vg = self.op("vector", lambda e: e.reduce_max(gsc, arow, AX.X),
                     waits=[dld], sem="ve")
        dag = self.cdma("sync", g["ag3_in"][:, :], gsc, waits=[("ve", vg)],
                       slot="c11")
        self.op("gpsimd", lambda e: e.collective_compute(
            "AllReduce", ALU.max, replica_groups=[list(range(NCORE))],
            ins=[g["ag3_in"][:, :]], outs=[g["ag3_out"][:, :]]),
            waits=[dag], sem="cc")
        self.cc_ag3 = self.now("cc")

    def p8b_down(self):
        g = self.g
        g2 = self.fv("cmb", 0, 128, 34, 1)
        dg2 = self.cdma("sync", g2, g["ag3_out"][0:1, :].partition_broadcast(128),
                       waits=[("cc", self.cc_ag3)], slot="c11")
        cdc = self.fv("cmb", 0, 128, 36, 1)
        dcd = self.cdma("sync", cdc, g["clip_dn"][0:1, :].partition_broadcast(128),
                       slot="c11")
        ssd = self.op("scalar", lambda e: e.activation(
            cdc, cdc, AF.Sigmoid, bias=0.0, scale=1.0), waits=[dcd], sem="se")
        s2f = self.fv("cmb", 0, 128, 38, 1)
        self.op("vector", lambda e: e.tensor_mul(s2f, cdc, g2),
                waits=[("se", ssd), dg2], sem="ve")
        s2i = self.fv("cmb", 0, 128, 40, 1)
        self.op("vector", lambda e: e.reciprocal(s2i, s2f), sem="ve")
        vs2 = self.op("vector", lambda e: e.tensor_scalar_mul(s2i, s2i, 127.0),
                      sem="ve")
        cd8 = self.fv("coef_dn", 0, 128, 0, 8)
        ddn = self.cdma("sync", cd8, g["rows_hbm"][6, 0:1024]
                       .rearrange("(dt p) -> p dt", p=128), slow=True,
                       slot="c11")
        self.op("vector", lambda e: e.tensor_mul(
            cd8, cd8, self.fv("modx", 0, 128, 24, 8)), waits=[ddn], sem="ve")
        self.op("vector", lambda e: e.tensor_scalar(
            cd8, cd8, s2f, None, ALU.mult), sem="ve")
        vcf = self.op("vector", lambda e: e.tensor_scalar_mul(
            cd8, cd8, 1.0 / 16129.0), sem="ve")
        dcr = self.cdma("sync", g["rows_hbm"][20, 0:1024]
                       .rearrange("(dt p) -> p dt", p=128), cd8,
                       waits=[("ve", vcf)], slow=True, slot="c11")
        dcf = self.cdma("sync", self.fv("coefrep", 0, 128, 0, 1024),
                       g["rows_hbm"][20:21, 0:1024].partition_broadcast(128),
                       waits=[dcr], slot="c11")
        def qdt(hb):
            if hb < 16:
                return self.bvv("R4", 0, 128, hb * 1024, 1024)
            return self.bvv("R6", 0, 128, (hb - 16) * 1024, 1024)
        for hb in range(IT):
            hv, itl = divmod(hb, 11)
            off = 2 * 128 * I + hv * 11 * 16384 + itl * 16384
            self.dma("sync", qdt(hb).rearrange("p (c h) -> p c h", h=128),
                     g["agq_out"][:, off:off + 16384]
                     .rearrange("c (p h) -> p c h", p=128),
                     slot="qdl")
        dqd = [("D_qdl", self.now("D_qdl"))]
        for hb in range(IT):
            band = self.mband(hb)
            for hf in range(2):
                tq = self.fv("tmp0", 0, 128, 0, 1024) if hf == 0 else \
                    self.fv("tmp1", 0, 128, 0, 1024)
                sl = band[:, hf * 1024:(hf + 1) * 1024]
                s1 = self.op("scalar", lambda e, i=sl, o=tq: e.activation(
                    o, i, AF.Identity, bias=self.mgc, scale=s2i),
                    waits=[("ve", vs2)] if hb == 0 and hf == 0 else [],
                    sem="se")
                s2 = self.op("scalar", lambda e, o=tq, q=sl: e.activation(
                    q, o, AF.Identity, bias=self.nmgc, scale=1.0), sem="se")
                self.op("vector", lambda e, q=sl: e.tensor_scalar(
                    q, q, 127.0, -128.0, ALU.min, ALU.max),
                    waits=[("se", s2)], sem="ve")
        vqmf = self.now("ve")
        st = {}
        for ch in range(NCH):
            dxr = self.dma("sync", self.fv("xrl", 0, 128, 0, 1024),
                           g["x2_hbm"][ch * 128:(ch + 1) * 128, :],
                           waits=[("ve", st[("x3v", ch - 1)])]
                           if ("x3v", ch - 1) in st else [], slot="xrl")
            for dN in range(2):
                for hb in range(IT):
                    w = []
                    if hb == 0:
                        w = [("ve", vqmf)] if ch == 0 and dN == 0 else []
                        if ch == 0:
                            w += dqd
                        if ("x3st", ch - 1, dN) in st:
                            w.append(st[("x3st", ch - 1, dN)])
                        if ("ev", ch - 1, dN) in st:
                            w.append(("ve", st[("ev", ch - 1, dN)]))
                    self.op("tensor", lambda e, hb=hb, ch=ch, dN=dN: e.matmul(
                        self.ps[dN][:, 0:512],
                        self.mband(hb)[:, ch * 128:(ch + 1) * 128],
                        qdt(hb)[:, dN * 512:(dN + 1) * 512],
                        start=(hb == 0), stop=(hb == IT - 1)),
                        waits=w, sem="pe" if hb == IT - 1 else None)
                pe = self.now("pe")
                xo = (self.fv("tmp0", 0, 128, 0, 512) if dN == 0
                      else self.fv("tmp1", 0, 128, 0, 512))
                self.op("vector", lambda e, dN=dN, xo=xo: e.tensor_tensor(
                    xo, self.ps[dN][:, 0:512],
                    self.fv("coefrep", 0, 128, dN * 512, 512), ALU.mult),
                    waits=[("pe", pe), dcf] + ([st[("x3st", ch - 1, dN)]]
                                               if ("x3st", ch - 1, dN) in st
                                               else []), sem="ve")
                self.op("vector", lambda e, dN=dN, xo=xo: e.tensor_add(
                    xo, xo, self.fv("xrl", 0, 128, dN * 512, 512)),
                    waits=[dxr], sem="ve")
                st[("ev", ch, dN)] = self.now("ve")
                st[("x3v", ch)] = self.now("ve")
                st[("x3st", ch, dN)] = self.dma(
                    "sync", self.out_x3[ch * 128:(ch + 1) * 128,
                                        dN * 512:(dN + 1) * 512], xo,
                    waits=[("ve", self.now("ve"))], slot="out%d" % dN)

    def emit_all(self):
        phases = [("p0", self.p0_consts, 0), ("p2a", self.p2a_mod_send, 0),
                  ("p1s", self.p1_ln1, 1), ("p2b", self.p2b_mod_recv, 1),
                  ("p2c", self.p2c_quant_weights, 0),
                  ("p1", self.p1_affine, 1), ("p4", self.p4_mixer_in, 1),
                  ("p5a", self.p5a_smalls, 0), ("p5b", self.p5b_z, 0),
                  ("p5c", self.p5c_loopA, 0), ("p5pre", self.p5e_pre, 0),
                  ("p5", self.p5e_loopB, 1),
                  ("p7", self.p7_ln2, 1), ("p8a", self.p8a_gateup, 1),
                  ("p8b", self.p8b_down, 1)]
        for name, fn, bar in phases:
            fn()
            if bar:
                self.barrier()
            if self.upto == name:
                self.final_wait()
                return
        if self.upto in ("all",):
            self.final_wait()
            return
        raise NotImplementedError("later phases pending")

    def final_wait(self):
        waits = [(sname, self.count[sname]) for sname in self.semnames
                 if sname.startswith("D_") and self.count[sname] > 0]
        waits += [("cc", self.now("cc"))]
        self.emit("sync", lambda e: e.nop(), waits, ())


def make_program(dbg=(), upto="all"):
    b = Build(dbg, upto)
    b.emit_all()
    b.materialize()
    b.stack.close()
    return b


def kernel(**inputs):
    maps = _stage(inputs)
    b = make_program()
    res = run_bass_kernel_spmd(b.nc, maps, core_ids=list(range(NCORE)))
    out = np.empty((B, L, D), np.float32)
    for k in range(NCORE):
        bb, q = divmod(k, 4)
        out[bb, q * T:(q + 1) * T] = np.asarray(res.results[k]["out_x3"])
    return out
